# version 02
# baseline (speedup 1.0000x reference)
import numpy as np
import ml_dtypes
from contextlib import ExitStack

import concourse.bass as bass
import concourse.mybir as mybir
from concourse.bass_utils import run_bass_kernel_spmd

F32 = mybir.dt.float32
BF16 = mybir.dt.bfloat16
U32 = mybir.dt.uint32
AF = mybir.ActivationFunctionType
ALU = mybir.AluOpType
AX = mybir.AxisListType
BF16NP = ml_dtypes.bfloat16

NCORES = 8
TPC = 1024
THALO = 1152
D = 2048
INNER = 4096
CHUNK = 64
DINP = 8512


def f32(x):
    return np.ascontiguousarray(np.asarray(x), dtype=np.float32)


def bf16(x):
    return np.ascontiguousarray(np.asarray(x), dtype=BF16NP)



def build_A():
    nc = bass.Bass("TRN2", target_bir_lowering=False)
    dram = nc.dram_tensor
    u = dram("u", [THALO, D], F32, kind="ExternalInput").ap()
    w_inT = dram("w_inT", [D, DINP], F32, kind="ExternalInput").ap()
    convw = dram("convw", [128, 136], F32, kind="ExternalInput").ap()
    convb = dram("convb", [128, 34], F32, kind="ExternalInput").ap()
    dtb = dram("dtb", [64, 1], F32, kind="ExternalInput").ap()
    alog = dram("alog", [64, 1], F32, kind="ExternalInput").ap()
    ident = dram("ident", [128, 128], BF16, kind="ExternalInput").ap()

    o_xnfm = dram("xn_fm", [D, TPC], BF16, kind="ExternalOutput").ap()
    o_xntm = dram("xn_tm", [TPC, D], BF16, kind="ExternalOutput").ap()
    o_siluz = dram("siluz", [INNER, TPC], BF16, kind="ExternalOutput").ap()
    o_xconv = dram("xconv", [INNER, TPC], BF16, kind="ExternalOutput").ap()
    o_B = dram("Bo", [128, TPC], BF16, kind="ExternalOutput").ap()
    o_C = dram("Co", [128, TPC], BF16, kind="ExternalOutput").ap()
    o_dt = dram("dto", [64, TPC], F32, kind="ExternalOutput").ap()
    o_Adt = dram("Adto", [64, TPC], F32, kind="ExternalOutput").ap()
    o_Acs = dram("Acso", [64, TPC], F32, kind="ExternalOutput").ap()

    NOC = 67

    with ExitStack() as ex:
        e = ex.enter_context
        cw = e(nc.sbuf_tensor("cw", [128, 136], F32))
        cb = e(nc.sbuf_tensor("cb", [128, 34], F32))
        db = e(nc.sbuf_tensor("db", [64, 1], F32))
        al = e(nc.sbuf_tensor("al", [64, 1], F32))
        negA = e(nc.sbuf_tensor("negA", [64, 1], F32))
        idt = e(nc.sbuf_tensor("idt", [128, 128], BF16))
        ut = e(nc.sbuf_tensor("ut", [128, 2 * D], F32))
        usq = e(nc.sbuf_tensor("usq", [128, D], BF16))
        stat_sm = e(nc.sbuf_tensor("stat_sm", [128, 16], F32))
        stat_ss = e(nc.sbuf_tensor("stat_ss", [128, 16], F32))
        m_col = e(nc.sbuf_tensor("m_col", [128, 16], F32))
        v_col = e(nc.sbuf_tensor("v_col", [128, 16], F32))
        r_col = e(nc.sbuf_tensor("r_col", [128, 16], F32))
        xntm = e(nc.sbuf_tensor("xntm", [128, 2 * D], BF16))
        xn = e(nc.sbuf_tensor("xn", [128, 16 * THALO], BF16))
        wsl = e(nc.sbuf_tensor("wsl", [128, 2 * 16 * 128], BF16))
        zsb = e(nc.sbuf_tensor("zsb", [128, 2 * TPC], BF16))
        xbc = e(nc.sbuf_tensor("xbc", [128, 2 * THALO], BF16))
        dg = e(nc.sbuf_tensor("dg", [128, 2 * 4 * 128], BF16))
        xcv = e(nc.sbuf_tensor("xcv", [128, 2 * TPC], BF16))
        tdt = e(nc.sbuf_tensor("tdt", [64, THALO], F32))
        dtf = e(nc.sbuf_tensor("dtf", [64, THALO], F32))
        adt = e(nc.sbuf_tensor("adt", [64, TPC], F32))
        acs = e(nc.sbuf_tensor("acs", [64, TPC], F32))
        ones64 = e(nc.sbuf_tensor("ones64", [64, 64], F32))
        cEPS = e(nc.sbuf_tensor("cEPS", [128, 1], F32))
        cONE = e(nc.sbuf_tensor("cONE", [128, 1], F32))

        psM0 = e(nc.psum_tensor("psM0", [128, THALO], F32))
        psM1 = e(nc.psum_tensor("psM1", [128, THALO], F32))
        psT_guard = nc.psum_tensor("psT", [128, 128], BF16)
        psT = psT_guard.__enter__()

        LD = e(nc.semaphore("LD"))
        WL = e(nc.semaphore("WL"))
        MM = e(nc.semaphore("MM"))
        SE = e(nc.semaphore("SE"))
        VE = e(nc.semaphore("VE"))
        SO = e(nc.semaphore("SO"))

        ld = wl = mm = se = ve = so = 0
        psum = [psM0, psM1]

        with nc.Block() as blk:
            @blk.sync
            def _(s):
                s.dma_start(out=cw[:], in_=convw[:, :]).then_inc(LD, 16)
                s.dma_start(out=cb[:], in_=convb[:, :]).then_inc(LD, 16)
                s.dma_start(out=db[:], in_=dtb[:, :]).then_inc(LD, 16)
                s.dma_start(out=al[:], in_=alog[:, :]).then_inc(LD, 16)
                s.dma_start(out=idt[:], in_=ident[:, :]).then_inc(LD, 16)
            ld += 80
            ld_p0 = ld

            @blk.scalar
            def _(s):
                s.wait_ge(LD, ld_p0)
                s.activation(negA[:], al[:], AF.Exp, bias=0.0, scale=1.0).then_inc(SE, 1)
            se += 1
            se_negA = se

            @blk.vector
            def _(v):
                v.wait_ge(SE, se_negA)
                v.tensor_scalar(negA[:], negA[:], scalar1=-1.0, scalar2=None,
                                op0=ALU.mult).then_inc(VE, 1)
                v.memset(ones64[:], 1.0).then_inc(VE, 1)
                v.memset(cEPS[:], 1e-5).then_inc(VE, 1)
                v.memset(cONE[:], 1.0).then_inc(VE, 1)
            ve += 4
            ve_p0 = ve

            se_sq = [0] * 9
            ve_tile = [0] * 9
            mm_t = [0] * 9
            so_tm = [0] * 9
            for tt in range(9):
                p = tt % 2

                @blk.sync
                def _(s, tt=tt, p=p):
                    if tt >= 2:
                        s.wait_ge(VE, ve_tile[tt - 2])
                    s.dma_start(out=ut[:, p * D:(p + 1) * D],
                                in_=u[tt * 128:(tt + 1) * 128, :]).then_inc(LD, 16)
                ld += 16
                ld_t = ld

                @blk.scalar
                def _(s, tt=tt, p=p, ld_t=ld_t):
                    s.wait_ge(LD, ld_t)
                    s.activation(usq[:], ut[:, p * D:(p + 1) * D], AF.Square,
                                 bias=0.0, scale=1.0,
                                 accum_out=stat_ss[:, tt:tt + 1]).then_inc(SE, 1)
                se += 1
                se_sq[tt] = se

                v0 = ve

                @blk.vector
                def _(v, tt=tt, p=p, ld_t=ld_t, v0=v0):
                    v.wait_ge(LD, ld_t)
                    uv = ut[:, p * D:(p + 1) * D]
                    v.reduce_sum(stat_sm[:, tt:tt + 1], uv, axis=AX.X).then_inc(VE, 1)
                    v.wait_ge(VE, v0 + 1)
                    v.tensor_scalar(m_col[:, tt:tt + 1], stat_sm[:, tt:tt + 1],
                                    scalar1=1.0 / D, scalar2=None,
                                    op0=ALU.mult).then_inc(VE, 1)
                    v.wait_ge(SE, se_sq[tt])
                    v.tensor_scalar(v_col[:, tt:tt + 1], stat_ss[:, tt:tt + 1],
                                    scalar1=1.0 / D, scalar2=None,
                                    op0=ALU.mult).then_inc(VE, 1)
                    v.wait_ge(VE, v0 + 3)
                    v.tensor_tensor(r_col[:, tt:tt + 1], m_col[:, tt:tt + 1],
                                    m_col[:, tt:tt + 1], op=ALU.mult).then_inc(VE, 1)
                    v.wait_ge(VE, v0 + 4)
                    v.tensor_tensor(v_col[:, tt:tt + 1], v_col[:, tt:tt + 1],
                                    r_col[:, tt:tt + 1], op=ALU.subtract).then_inc(VE, 1)
                ve += 5
                ve_stats = ve

                @blk.scalar
                def _(s, tt=tt, se0=se):
                    s.wait_ge(VE, ve_stats)
                    s.wait_ge(VE, ve_p0)
                    s.activation(r_col[:, tt:tt + 1], v_col[:, tt:tt + 1], AF.Ln,
                                 bias=cEPS[:, 0:1], scale=1.0).then_inc(SE, 1)
                    s.wait_ge(SE, se0 + 1)
                    s.activation(r_col[:, tt:tt + 1], r_col[:, tt:tt + 1], AF.Exp,
                                 bias=0.0, scale=-0.5).then_inc(SE, 1)
                se += 2
                se_rstd = se

                @blk.vector
                def _(v, tt=tt, p=p):
                    v.wait_ge(SE, se_rstd)
                    if tt >= 2:
                        v.wait_ge(SO, so_tm[tt - 2])
                        v.wait_ge(MM, mm_t[tt - 2])
                    v.tensor_scalar(xntm[:, p * D:(p + 1) * D], ut[:, p * D:(p + 1) * D],
                                    scalar1=m_col[:, tt:tt + 1], scalar2=r_col[:, tt:tt + 1],
                                    op0=ALU.subtract, op1=ALU.mult).then_inc(VE, 1)
                ve += 1
                ve_xntm = ve

                @blk.sync
                def _(s, tt=tt, p=p):
                    s.wait_ge(VE, ve_xntm)
                    if tt >= 1:
                        s.dma_start(out=o_xntm[(tt - 1) * 128:tt * 128, :],
                                    in_=xntm[:, p * D:(p + 1) * D]).then_inc(SO, 16)
                if tt >= 1:
                    so += 16
                so_tm[tt] = so

                v0 = ve

                @blk.tensor
                def _(t, tt=tt, p=p, v0=v0):
                    t.wait_ge(VE, ve_xntm)
                    for kc in range(16):
                        if kc > 0:
                            t.wait_ge(VE, v0 + kc)
                        t.transpose(psT[:], xntm[:, p * D + kc * 128:p * D + (kc + 1) * 128],
                                    idt[:]).then_inc(MM, 1)
                mm += 16
                mm_t[tt] = mm

                @blk.vector
                def _(v, tt=tt, mm0=mm - 16):
                    for kc in range(16):
                        v.wait_ge(MM, mm0 + kc + 1)
                        v.tensor_copy(xn[:, kc * THALO + tt * 128:kc * THALO + (tt + 1) * 128],
                                      psT[:]).then_inc(VE, 1)
                ve += 16
                ve_tile[tt] = ve

            ve_ln_done = ve

            @blk.sync
            def _(s):
                s.wait_ge(VE, ve_ln_done)
                for kc in range(16):
                    s.dma_start(out=o_xnfm[kc * 128:(kc + 1) * 128, :],
                                in_=xn[:, kc * THALO + 128:(kc + 1) * THALO]).then_inc(SO, 16)
            so += 256

            psT_guard.__exit__(None, None, None)
            psC = e(nc.psum_tensor("psC", [128, TPC], F32))
            mm_oc = [0] * NOC
            se_psum = [0] * NOC
            se_xbc = [0] * NOC
            cm_ch = [0] * 34
            se_cv = [0] * 34
            ve_dg = [0] * 34
            so_z = [0] * NOC
            so_cv = [0] * 34

            for oc in range(NOC):
                p = oc % 2
                ocw = 128 if oc < 66 else 64

                @blk.gpsimd
                def _(g, oc=oc, p=p, ocw=ocw):
                    if oc >= 2:
                        g.wait_ge(MM, mm_oc[oc - 2])
                    src = w_inT[:, oc * 128:oc * 128 + ocw]
                    src = src.rearrange("(kc p) m -> p kc m", p=128)
                    g.dma_start(out=wsl[:, p * 2048:p * 2048 + 16 * ocw],
                                in_=src).then_inc(WL, 16)
                wl += 16
                wl_oc = wl

                @blk.tensor
                def _(t, oc=oc, p=p, ocw=ocw, wl_oc=wl_oc):
                    t.wait_ge(WL, wl_oc)
                    t.wait_ge(VE, ve_ln_done)
                    if oc >= 2:
                        t.wait_ge(SE, se_psum[oc - 2])
                    for kc in range(16):
                        for (cs, cwd) in ((0, 512), (512, 512), (1024, 128)):
                            ins = t.matmul(psum[p][0:ocw, cs:cs + cwd],
                                           wsl[:, p * 2048 + kc * ocw:p * 2048 + (kc + 1) * ocw],
                                           xn[:, kc * THALO + cs:kc * THALO + cs + cwd],
                                           start=(kc == 0), stop=(kc == 15))
                            if kc == 15 and cs == 1024:
                                ins.then_inc(MM, 1)
                mm += 1
                mm_oc[oc] = mm

                if oc < 32:
                    @blk.scalar
                    def _(s, oc=oc, p=p):
                        s.wait_ge(MM, mm_oc[oc])
                        if oc >= 2:
                            s.wait_ge(SO, so_z[oc - 2])
                        s.activation(zsb[:, p * TPC:(p + 1) * TPC],
                                     psum[p][:, 128:THALO], AF.Silu,
                                     bias=0.0, scale=1.0).then_inc(SE, 1)
                    se += 1
                    se_psum[oc] = se

                    @blk.sync
                    def _(s, oc=oc, p=p, se_w=se):
                        s.wait_ge(SE, se_w)
                        s.dma_start(out=o_siluz[oc * 128:(oc + 1) * 128, :],
                                    in_=zsb[:, p * TPC:(p + 1) * TPC]).then_inc(SO, 16)
                    so += 16
                    so_z[oc] = so

                elif oc < 66:
                    ch = oc - 32

                    @blk.scalar
                    def _(s, oc=oc, p=p, ch=ch):
                        s.wait_ge(MM, mm_oc[oc])
                        if ch >= 2:
                            s.wait_ge(MM, cm_ch[ch - 2])
                        s.activation(xbc[:, p * THALO:(p + 1) * THALO],
                                     psum[p][:, 0:THALO], AF.Copy,
                                     bias=0.0, scale=1.0).then_inc(SE, 1)
                    se += 1
                    se_psum[oc] = se
                    se_xbc[oc] = se

                    @blk.vector
                    def _(v, ch=ch, p=p):
                        v.wait_ge(VE, ve_p0)
                        if ch >= 2:
                            v.wait_ge(MM, cm_ch[ch - 2])
                        for k in range(4):
                            v.tensor_scalar(dg[:, (p * 4 + k) * 128:(p * 4 + k + 1) * 128],
                                            idt[:], scalar1=cw[:, ch * 4 + k:ch * 4 + k + 1],
                                            scalar2=None, op0=ALU.mult).then_inc(VE, 1)
                    ve += 4
                    ve_dg[ch] = ve

                    @blk.tensor
                    def _(t, ch=ch, p=p):
                        t.wait_ge(SE, se_xbc[ch + 32])
                        t.wait_ge(VE, ve_dg[ch])
                        if ch >= 1:
                            t.wait_ge(SE, se_cv[ch - 1])
                        for k in range(4):
                            for (cs, cwd) in ((0, 512), (512, 512)):
                                ins = t.matmul(psC[:, cs:cs + cwd],
                                               dg[:, (p * 4 + k) * 128:(p * 4 + k + 1) * 128],
                                               xbc[:, p * THALO + 125 + k + cs:
                                                   p * THALO + 125 + k + cs + cwd],
                                               start=(k == 0), stop=(k == 3))
                                if k == 3 and cs == 512:
                                    ins.then_inc(MM, 1)
                    mm += 1
                    cm_ch[ch] = mm

                    @blk.scalar
                    def _(s, ch=ch, p=p):
                        s.wait_ge(MM, cm_ch[ch])
                        if ch >= 2:
                            s.wait_ge(SO, so_cv[ch - 2])
                        s.activation(xcv[:, p * TPC:(p + 1) * TPC], psC[:, 0:TPC],
                                     AF.Silu, bias=cb[:, ch:ch + 1],
                                     scale=1.0).then_inc(SE, 1)
                    se += 1
                    se_cv[ch] = se

                    @blk.sync
                    def _(s, ch=ch, p=p, se_w=se):
                        s.wait_ge(SE, se_w)
                        if ch < 32:
                            dst = o_xconv[ch * 128:(ch + 1) * 128, :]
                        elif ch == 32:
                            dst = o_B[:, :]
                        else:
                            dst = o_C[:, :]
                        s.dma_start(out=dst, in_=xcv[:, p * TPC:(p + 1) * TPC]).then_inc(SO, 16)
                    so += 16
                    so_cv[ch] = so

                else:
                    @blk.scalar
                    def _(s, oc=oc, p=p, se0=se):
                        s.wait_ge(MM, mm_oc[oc])
                        s.activation(tdt[:], psum[p][0:64, 0:THALO], AF.Exp,
                                     bias=db[:, 0:1], scale=1.0).then_inc(SE, 1)
                        s.wait_ge(SE, se0 + 1)
                        s.activation(dtf[:], tdt[:], AF.Ln, bias=cONE[0:64, 0:1],
                                     scale=1.0).then_inc(SE, 1)
                    se += 2
                    se_psum[oc] = se - 1
                    se_dt = se

                    v0 = ve

                    @blk.vector
                    def _(v, v0=v0):
                        v.wait_ge(SE, se_dt)
                        v.tensor_scalar(adt[:], dtf[:, 128:THALO],
                                        scalar1=negA[:, 0:1], scalar2=None,
                                        op0=ALU.mult).then_inc(VE, 1)
                        v.wait_ge(VE, v0 + 1)
                        for c in range(16):
                            v.tensor_tensor_scan(acs[:, c * 64:(c + 1) * 64],
                                                 ones64[:], adt[:, c * 64:(c + 1) * 64],
                                                 initial=0.0, op0=ALU.mult,
                                                 op1=ALU.add).then_inc(VE, 1)
                    ve += 17
                    ve_dt = ve

                    @blk.sync
                    def _(s):
                        s.wait_ge(SE, se_dt)
                        s.dma_start(out=o_dt[:, :], in_=dtf[:, 128:THALO]).then_inc(SO, 16)
                        s.wait_ge(VE, ve_dt)
                        s.dma_start(out=o_Adt[:, :], in_=adt[:, :]).then_inc(SO, 16)
                        s.dma_start(out=o_Acs[:, :], in_=acs[:, :]).then_inc(SO, 16)
                    so += 48

            @blk.sync
            def _(s):
                s.wait_ge(SO, so)
    return nc


def host_A(inputs):
    u = np.asarray(inputs["L0_attn_u"])[0]
    u_pad = np.pad(f32(u), ((128, 0), (0, 0)))
    w_inT = f32(np.asarray(inputs["L0_attn_W_in"]).T)
    cwi = np.asarray(inputs["L0_attn_conv_w"])
    convw = f32(cwi.reshape(34, 128, 4).transpose(1, 0, 2).reshape(128, 136))
    cbi = np.asarray(inputs["L0_attn_conv_b"])
    convb = f32(cbi.reshape(34, 128).T)
    dtb = f32(np.asarray(inputs["L0_attn_dt_bias"]).reshape(64, 1))
    alog = f32(np.asarray(inputs["L0_attn_A_log"]).reshape(64, 1))
    ident = bf16(np.eye(128, dtype=np.float32))
    maps = []
    for c in range(NCORES):
        maps.append({
            "u": f32(u_pad[c * TPC:c * TPC + THALO]),
            "w_inT": w_inT,
            "convw": convw,
            "convb": convb,
            "dtb": dtb,
            "alog": alog,
            "ident": ident,
        })
    return maps



def build_B1():
    import os
    NGRP = int(os.environ.get("B1_NGRP", "8"))
    SUB = int(os.environ.get("B1_SUB", "9"))
    FOPS = int(os.environ.get("B1_FOPS", "3"))
    DO_STATES = os.environ.get("B1_STATES", "1") == "1"
    DO_SCAN = os.environ.get("B1_SCAN", "1") == "1"
    nc = bass.Bass("TRN2", target_bir_lowering=False)
    dram = nc.dram_tensor
    i_xconv = dram("xconv", [INNER, TPC], BF16, kind="ExternalInput").ap()
    i_B = dram("Bo", [128, TPC], BF16, kind="ExternalInput").ap()
    i_dt = dram("dto", [64, TPC], F32, kind="ExternalInput").ap()
    i_Adt = dram("Adto", [64, TPC], F32, kind="ExternalInput").ap()
    i_Acs = dram("Acso", [64, TPC], F32, kind="ExternalInput").ap()
    i_sel = dram("selg", [16, TPC], F32, kind="ExternalInput").ap()
    i_eye64 = dram("eye64", [64, 64], F32, kind="ExternalInput").ap()
    i_eye128b = dram("eye128b", [128, 128], BF16, kind="ExternalInput").ap()

    o_cs = dram("cso", [128, 16 * 4096], F32, kind="ExternalOutput").ap()
    o_seg = dram("sego", [128, 4096], F32, kind="ExternalOutput").ap()
    o_lnl = dram("lnlo", [64, 1], F32, kind="ExternalOutput").ap()

    with ExitStack() as ex:
        e = ex.enter_context
        selg = e(nc.sbuf_tensor("selg_s", [16, TPC], F32))
        ey64 = e(nc.sbuf_tensor("ey64", [64, 64], F32))
        ey128 = e(nc.sbuf_tensor("ey128", [128, 128], BF16))
        dtfm = e(nc.sbuf_tensor("dtfm", [64, TPC], F32))
        adtfm = e(nc.sbuf_tensor("adtfm", [64, TPC], F32))
        acsfm = e(nc.sbuf_tensor("acsfm", [64, TPC], F32))
        aend = e(nc.sbuf_tensor("aend", [64, 16], F32))
        aendT = e(nc.sbuf_tensor("aendT", [16, 64], F32))
        lam = e(nc.sbuf_tensor("lam", [1, 16 * 64], F32))
        lnl = e(nc.sbuf_tensor("lnl", [64, 1], F32))
        ones1 = e(nc.sbuf_tensor("ones1", [1, 128], F32))
        Bfull = e(nc.sbuf_tensor("Bfull", [128, TPC], BF16))
        Btm = e(nc.sbuf_tensor("Btm", [128, 128], BF16))
        xcg = e(nc.sbuf_tensor("xcg", [128, 2 * 4096], BF16))
        xtm = e(nc.sbuf_tensor("xtm", [128, 4096], BF16))
        xdd = e(nc.sbuf_tensor("xdd", [128, 4096], BF16))
        acstm = e(nc.sbuf_tensor("acstm", [128, 64], F32))
        dttm = e(nc.sbuf_tensor("dttm", [128, 64], F32))
        Ftm = e(nc.sbuf_tensor("Ftm", [128, 64], F32))
        state = e(nc.sbuf_tensor("state", [128, 4096], F32))
        csb = e(nc.sbuf_tensor("csb", [128, 2 * 2048], F32))

        psS = e(nc.psum_tensor("psS", [128, 2048], F32))
        psTb = e(nc.psum_tensor("psTb", [128, 128], BF16))
        psTf = e(nc.psum_tensor("psTf", [128, 128], F32))
        psM = e(nc.psum_tensor("psM", [128, 512], F32))

        LD = e(nc.semaphore("LD"))
        MM = e(nc.semaphore("MM"))
        SE = e(nc.semaphore("SE"))
        VE = e(nc.semaphore("VE"))
        SO = e(nc.semaphore("SO"))
        ld = mm = se = ve = so = 0

        with nc.Block() as blk:
            @blk.sync
            def _(s):
                s.dma_start(out=selg[:], in_=i_sel[:, :]).then_inc(LD, 16)
                s.dma_start(out=ey64[:], in_=i_eye64[:, :]).then_inc(LD, 16)
                s.dma_start(out=ey128[:], in_=i_eye128b[:, :]).then_inc(LD, 16)
                s.dma_start(out=dtfm[:], in_=i_dt[:, :]).then_inc(LD, 16)
                s.dma_start(out=adtfm[:], in_=i_Adt[:, :]).then_inc(LD, 16)
                s.dma_start(out=acsfm[:], in_=i_Acs[:, :]).then_inc(LD, 16)
                s.dma_start(out=Bfull[:], in_=i_B[:, :]).then_inc(LD, 16)
            ld += 112
            ld_p0 = ld

            v0 = ve

            @blk.vector
            def _(v, v0=v0):
                v.wait_ge(LD, ld_p0)
                for c in range(16):
                    v.reduce_sum(aend[:, c:c + 1], adtfm[:, c * 64:(c + 1) * 64],
                                 axis=AX.X).then_inc(VE, 1)
                v.wait_ge(VE, v0 + 16)
                v.reduce_sum(lnl[:], aend[:], axis=AX.X).then_inc(VE, 1)
                v.memset(ones1[:], 1.0).then_inc(VE, 1)
                v.memset(state[:], 0.0).then_inc(VE, 1)
            ve += 19
            ve_pre = ve

            @blk.sync
            def _(s):
                s.wait_ge(VE, ve_pre)
                s.dma_start(out=o_lnl[:, :], in_=lnl[:]).then_inc(SO, 16)
            so += 16

            @blk.tensor
            def _(t):
                t.wait_ge(VE, ve_pre)
                t.transpose(psTf[0:16, 0:64], aend[:], ey64[:]).then_inc(MM, 1)
            mm += 1
            mm_aT = mm

            @blk.vector
            def _(v):
                v.wait_ge(MM, mm_aT)
                v.tensor_copy(aendT[:], psTf[0:16, 0:64]).then_inc(VE, 1)
            ve += 1
            ve_aT = ve

            for c in range(16):
                @blk.tensor
                def _(t, c=c):
                    t.wait_ge(VE, ve_aT)
                    if c > 0:
                        t.wait_ge(SE, se + c)
                    t.matmul(psM[0:1, 64:128], aend[:, c:c + 1], ey64[:],
                             start=True, stop=True).then_inc(MM, 1)
                mm += 1

                @blk.scalar
                def _(s, c=c, mm_w=mm):
                    s.wait_ge(MM, mm_w)
                    s.activation(lam[0:1, c * 64:(c + 1) * 64], psM[0:1, 64:128],
                                 AF.Exp, bias=0.0, scale=1.0).then_inc(SE, 1)
            se += 16
            se_lam = se

            ve_grp = [0] * 8
            so_cs = {}
            ve_scan = {}
            for g in range(NGRP):
                pg = g % 2

                @blk.sync
                def _(s, g=g, pg=pg):
                    if g >= 2:
                        s.wait_ge(VE, ve_grp[g - 2])
                    src = i_xconv[:, g * 128:(g + 1) * 128]
                    src = src.rearrange("(pc p) t -> p pc t", p=128)
                    s.dma_start(out=xcg[:, pg * 4096:(pg + 1) * 4096],
                                in_=src).then_inc(LD, 16)
                ld += 16
                ld_g = ld

                v0 = ve

                if SUB >= 2:
                    @blk.tensor
                    def _(t, g=g, pg=pg, ld_g=ld_g, v0=v0):
                        t.wait_ge(LD, ld_g)
                        t.wait_ge(SE, se_lam)
                        if g >= 1:
                            t.wait_ge(VE, ve_grp[g - 1])
                        for pc in range(32):
                            if pc > 0:
                                t.wait_ge(VE, v0 + pc)
                            t.transpose(psTb[:],
                                        xcg[:, pg * 4096 + pc * 128:pg * 4096 + (pc + 1) * 128],
                                        ey128[:]).then_inc(MM, 1)
                        if SUB >= 3:
                            t.wait_ge(VE, v0 + 32)
                            t.transpose(psTb[:], Bfull[:, g * 128:(g + 1) * 128],
                                        ey128[:]).then_inc(MM, 1)
                            t.wait_ge(VE, v0 + 33)
                            t.transpose(psTf[:, 0:64], acsfm[0:64, g * 128:(g + 1) * 128],
                                        ey64[:]).then_inc(MM, 1)
                            t.wait_ge(VE, v0 + 34)
                            t.transpose(psTf[:, 64:128], dtfm[0:64, g * 128:(g + 1) * 128],
                                        ey64[:]).then_inc(MM, 1)
                    mm0 = mm
                    mm += 32 if SUB == 2 else 35

                    @blk.vector
                    def _(v, g=g, mm0=mm0):
                        for pc in range(32):
                            v.wait_ge(MM, mm0 + pc + 1)
                            v.tensor_copy(xtm[:, pc * 128:(pc + 1) * 128], psTb[:]).then_inc(VE, 1)
                        if SUB >= 3:
                            v.wait_ge(MM, mm0 + 33)
                            v.tensor_copy(Btm[:], psTb[:]).then_inc(VE, 1)
                            v.wait_ge(MM, mm0 + 34)
                            v.tensor_copy(acstm[:, 0:64], psTf[:, 0:64]).then_inc(VE, 1)
                            v.wait_ge(MM, mm0 + 35)
                            v.tensor_copy(dttm[:, 0:64], psTf[:, 64:128]).then_inc(VE, 1)
                    ve += 32 if SUB == 2 else 35
                else:
                    mm0 = mm

                if SUB >= 4:
                    @blk.tensor
                    def _(t, g=g, ve_w=ve):
                        t.wait_ge(VE, ve_w)
                        t.matmul(psM[:, 0:64], selg[:, g * 128:(g + 1) * 128], aendT[:],
                                 start=True, stop=True).then_inc(MM, 1)
                    mm += 1
                mm_E = mm

                if SUB >= 5:
                    @blk.vector
                    def _(v, g=g, mm_E=mm_E):
                        v.wait_ge(MM, mm_E)
                        if FOPS >= 1:
                            v.tensor_tensor(Ftm[:], psM[:, 0:64], acstm[:, 0:64],
                                            op=ALU.subtract).then_inc(VE, 1)
                        else:
                            v.engine_nop().then_inc(VE, 1)
                    ve += 1
                    ve_F1 = ve

                    @blk.scalar
                    def _(s, g=g, ve_F1=ve_F1):
                        s.wait_ge(VE, ve_F1)
                        if FOPS >= 2:
                            s.activation(Ftm[:], Ftm[:], AF.Exp, bias=0.0,
                                         scale=1.0).then_inc(SE, 1)
                        else:
                            s.nop().then_inc(SE, 1)
                    se += 1
                    se_F = se

                    v0 = ve

                    @blk.vector
                    def _(v, g=g, pg=pg, v0=v0, se_F=se_F):
                        v.wait_ge(SE, se_F)
                        if FOPS >= 3:
                            v.tensor_tensor(Ftm[:], Ftm[:], dttm[:, 0:64],
                                            op=ALU.mult).then_inc(VE, 1)
                        else:
                            v.engine_nop().then_inc(VE, 1)
                        if SUB >= 6:
                            v.wait_ge(VE, v0 + 1)
                            xv = xtm[:, :].rearrange("p (h q) -> p h q", h=64)
                            fv = Ftm[:, 0:64].unsqueeze(2).broadcast_to((128, 64, 64))
                            v.tensor_tensor(xdd[:, :].rearrange("p (h q) -> p h q", h=64),
                                            xv, fv, op=ALU.mult).then_inc(VE, 1)
                        else:
                            v.engine_nop().then_inc(VE, 1)
                    ve += 2
                ve_xdd = ve
                ve_grp[g] = ve

                for ci in range(2 if DO_STATES else 0):
                    c = 2 * g + ci
                    for hh in range(2):
                        sl_idx = (2 * c + hh) % 2

                        @blk.tensor
                        def _(t, c=c, ci=ci, hh=hh, ve_xdd=ve_xdd):
                            t.wait_ge(VE, ve_xdd)
                            if not (c == 0 and hh == 0):
                                pc_, ph_ = (c, 0) if hh == 1 else (c - 1, 1)
                                t.wait_ge(VE, ve_scan[(pc_, ph_)])
                            ins = None
                            for h in range(32 * hh, 32 * hh + 32):
                                ins = t.matmul(psS[:, (h - 32 * hh) * 64:(h - 32 * hh + 1) * 64],
                                               Btm[ci * 64:(ci + 1) * 64, :],
                                               xdd[ci * 64:(ci + 1) * 64, h * 64:(h + 1) * 64],
                                               start=True, stop=True)
                            ins.then_inc(MM, 1)
                        mm += 1
                        mm_S = mm

                        @blk.tensor
                        def _(t, c=c, hh=hh, ve_xdd=ve_xdd):
                            t.wait_ge(VE, ve_xdd)
                            t.matmul(psM[:, 128:160], ones1[:, :],
                                     lam[0:1, c * 64 + hh * 32:c * 64 + hh * 32 + 32],
                                     start=True, stop=True).then_inc(MM, 1)
                        mm += 1
                        mm_L = mm

                        v0 = ve

                        @blk.vector
                        def _(v, c=c, hh=hh, v0=v0, mm_S=mm_S, mm_L=mm_L,
                              sl_idx=sl_idx):
                            v.wait_ge(MM, mm_S)
                            prev2 = (c * 2 + hh) - 2
                            if prev2 >= 0:
                                v.wait_ge(SO, so_cs[(prev2 // 2, prev2 % 2)])
                            v.tensor_copy(csb[:, sl_idx * 2048:(sl_idx + 1) * 2048],
                                          psS[:]).then_inc(VE, 1)
                            v.wait_ge(MM, mm_L)
                            v.wait_ge(VE, v0 + 1)
                            sl = state[:, hh * 2048:(hh + 1) * 2048]
                            sv = sl.rearrange("p (h q) -> p h q", h=32)
                            lv = psM[:, 128:160].unsqueeze(2).broadcast_to((128, 32, 64))
                            if DO_SCAN:
                                v.tensor_tensor(sv, sv, lv, op=ALU.mult).then_inc(VE, 1)
                                v.wait_ge(VE, v0 + 2)
                                v.tensor_tensor(sl, sl, psS[:], op=ALU.add).then_inc(VE, 1)
                            else:
                                v.engine_nop().then_inc(VE, 1)
                                v.engine_nop().then_inc(VE, 1)
                        ve += 3
                        ve_scan[(c, hh)] = ve

                        @blk.sync
                        def _(s, c=c, hh=hh, sl_idx=sl_idx, ve_w=ve - 2):
                            s.wait_ge(VE, ve_w)
                            s.dma_start(out=o_cs[:, c * 4096 + hh * 2048:
                                                 c * 4096 + (hh + 1) * 2048],
                                        in_=csb[:, sl_idx * 2048:(sl_idx + 1) * 2048]
                                        ).then_inc(SO, 16)
                        so += 16
                        so_cs[(c, hh)] = so

            @blk.sync
            def _(s):
                if (2 * NGRP - 1, 1) in ve_scan:
                    s.wait_ge(VE, ve_scan[(2 * NGRP - 1, 1)])
                else:
                    s.wait_ge(VE, ve)
                s.dma_start(out=o_seg[:, :], in_=state[:]).then_inc(SO, 16)
                s.wait_ge(SO, so + 16)
            so += 16
    return nc


def host_B1(a_outs, consts):
    maps = []
    for c in range(NCORES):
        r = a_outs[c]
        maps.append({
            "xconv": r["xconv"], "Bo": r["Bo"], "dto": r["dto"],
            "Adto": r["Adto"], "Acso": r["Acso"],
            "selg": consts["selg"], "eye64": consts["eye64"],
            "eye128b": consts["eye128b"],
        })
    return maps


def mk_consts():
    sel = np.zeros((16, TPC), np.float32)
    for g in range(8):
        for s_ in range(128):
            sel[2 * g + (1 if s_ >= 64 else 0), g * 128 + s_] = 1.0
    return {
        "selg": f32(sel),
        "eye64": f32(np.eye(64)),
        "eye128b": bf16(np.eye(128)),
    }



def build_B2():
    nc = bass.Bass("TRN2", target_bir_lowering=False)
    dram = nc.dram_tensor
    i_xconv = dram("xconv", [INNER, TPC], BF16, kind="ExternalInput").ap()
    i_siluz = dram("siluz", [INNER, TPC], BF16, kind="ExternalInput").ap()
    i_B = dram("Bo", [128, TPC], BF16, kind="ExternalInput").ap()
    i_C = dram("Co", [128, TPC], BF16, kind="ExternalInput").ap()
    i_dt = dram("dto", [64, TPC], F32, kind="ExternalInput").ap()
    i_Adt = dram("Adto", [64, TPC], F32, kind="ExternalInput").ap()
    i_Acs = dram("Acso", [64, TPC], F32, kind="ExternalInput").ap()
    i_cs = dram("cso", [128, 16 * 4096], F32, kind="ExternalInput").ap()
    i_segs = dram("segs", [8 * 128, 4096], F32, kind="ExternalInput").ap()
    i_lnl = dram("lnl_all", [8, 64], F32, kind="ExternalInput").ap()
    i_mmT = dram("mmT", [8, 8], F32, kind="ExternalInput").ap()
    i_wb = dram("wbias", [8, 1], F32, kind="ExternalInput").ap()
    i_rep = dram("rep64", [64, 4096], F32, kind="ExternalInput").ap()
    i_madd = dram("madd", [128, 128], F32, kind="ExternalInput").ap()
    i_drep = dram("drep", [128, 32], F32, kind="ExternalInput").ap()
    i_nw = dram("normw", [128, 32], F32, kind="ExternalInput").ap()
    i_eye64 = dram("eye64", [64, 64], F32, kind="ExternalInput").ap()
    i_eye128b = dram("eye128b", [128, 128], BF16, kind="ExternalInput").ap()
    i_woutT = dram("woutT", [INNER, D], F32, kind="ExternalInput").ap()

    o_s0 = dram("s0o", [D, TPC], BF16, kind="ExternalOutput").ap()

    with ExitStack() as ex:
        e = ex.enter_context
        ey64 = e(nc.sbuf_tensor("ey64", [64, 64], F32))
        ey128 = e(nc.sbuf_tensor("ey128", [128, 128], BF16))
        dtfm = e(nc.sbuf_tensor("dtfm", [64, TPC], F32))
        adtfm = e(nc.sbuf_tensor("adtfm", [64, TPC], F32))
        acsfm = e(nc.sbuf_tensor("acsfm", [64, TPC], F32))
        aend = e(nc.sbuf_tensor("aend", [64, 16], F32))
        lam = e(nc.sbuf_tensor("lam", [1, 16 * 64], F32))
        lnla = e(nc.sbuf_tensor("lnla", [8, 64], F32))
        mmTs = e(nc.sbuf_tensor("mmTs", [8, 8], F32))
        wbs = e(nc.sbuf_tensor("wbs", [8, 1], F32))
        madd = e(nc.sbuf_tensor("madd_s", [128, 128], F32))
        drep = e(nc.sbuf_tensor("drep_s", [128, 32], F32))
        nw = e(nc.sbuf_tensor("nw_s", [128, 32], F32))
        ones1 = e(nc.sbuf_tensor("ones1", [1, 128], F32))
        onesb = e(nc.sbuf_tensor("onesb", [128, 1], BF16))
        ww = e(nc.sbuf_tensor("ww", [8, 64], F32))
        wwT = e(nc.sbuf_tensor("wwT", [64, 8], F32))
        Bfull = e(nc.sbuf_tensor("Bfull", [128, TPC], BF16))
        Cfull = e(nc.sbuf_tensor("Cfull", [128, TPC], BF16))
        state = e(nc.sbuf_tensor("state", [128, 4096], F32))
        yg = e(nc.sbuf_tensor("yg", [128, 32 * TPC], BF16))
        ysq = e(nc.sbuf_tensor("ysq", [128, TPC], BF16))
        rr = e(nc.sbuf_tensor("rr", [1, TPC], F32))

        cEPS_B2 = e(nc.sbuf_tensor("cEPS_B2", [1, 1], F32))
        psM2 = e(nc.psum_tensor("psM2", [128, 512], F32))

        LD = e(nc.semaphore("LD"))
        WL = e(nc.semaphore("WL"))
        MM = e(nc.semaphore("MM"))
        SE = e(nc.semaphore("SE"))
        VE = e(nc.semaphore("VE"))
        SO = e(nc.semaphore("SO"))
        ld = wl = mm = se = ve = so = 0

        with nc.Block() as blk:
            @blk.sync
            def _(s):
                for (dst, src) in ((ey64, i_eye64), (ey128, i_eye128b),
                                   (dtfm, i_dt), (adtfm, i_Adt), (acsfm, i_Acs),
                                   (lnla, i_lnl), (mmTs, i_mmT), (wbs, i_wb),
                                   (madd, i_madd), (drep, i_drep),
                                   (nw, i_nw), (Bfull, i_B), (Cfull, i_C)):
                    s.dma_start(out=dst[:], in_=src[:, :]).then_inc(LD, 16)
            ld += 13 * 16
            ld_p0 = ld

            v0 = ve

            @blk.vector
            def _(v, v0=v0):
                v.wait_ge(LD, ld_p0)
                for c in range(16):
                    v.reduce_sum(aend[:, c:c + 1], adtfm[:, c * 64:(c + 1) * 64],
                                 axis=AX.X).then_inc(VE, 1)
                v.memset(ones1[:], 1.0).then_inc(VE, 1)
                v.memset(onesb[:], 1.0).then_inc(VE, 1)
                v.memset(cEPS_B2[:], 1e-5).then_inc(VE, 1)
            ve += 19
            ve_pre = ve

            for c in range(16):
                @blk.tensor
                def _(t, c=c):
                    t.wait_ge(VE, ve_pre)
                    if c > 0:
                        t.wait_ge(SE, se + c)
                    t.matmul(psM2[0:1, 256:320], aend[:, c:c + 1], ey64[:],
                             start=True, stop=True).then_inc(MM, 1)
                mm += 1

                @blk.scalar
                def _(s, c=c, mm_w=mm):
                    s.wait_ge(MM, mm_w)
                    s.activation(lam[0:1, c * 64:(c + 1) * 64], psM2[0:1, 256:320],
                                 AF.Exp, bias=0.0, scale=1.0).then_inc(SE, 1)
            se += 16
            se_lam = se

            psW_g = nc.psum_tensor("psW", [128, 2048], F32)
            psW = psW_g.__enter__()
            rep_g = nc.sbuf_tensor("rep_s", [64, 4096], F32)
            rep = rep_g.__enter__()
            segsl_g = nc.sbuf_tensor("segsl", [128, 2 * 2048], F32)
            segsl = segsl_g.__enter__()

            @blk.sync
            def _(s):
                s.dma_start(out=rep[:], in_=i_rep[:, :]).then_inc(LD, 16)
            ld += 16
            ld_rep = ld

            @blk.tensor
            def _(t):
                t.wait_ge(VE, ve_pre)
                t.wait_ge(SE, se_lam)
                t.matmul(psM2[0:8, 0:64], mmTs[:], lnla[:],
                         start=True, stop=True).then_inc(MM, 1)
            mm += 1
            mm_wl = mm

            @blk.scalar
            def _(s):
                s.wait_ge(MM, mm_wl)
                s.wait_ge(SE, se_lam)
                s.activation(ww[:], psM2[0:8, 0:64], AF.Exp, bias=wbs[:, 0:1],
                             scale=1.0).then_inc(SE, 1)
            se += 1
            se_ww = se

            @blk.tensor
            def _(t):
                t.wait_ge(SE, se_ww)
                t.transpose(psM2[0:64, 64:72], ww[:], ey64[0:8, 0:8]).then_inc(MM, 1)
            mm += 1
            mm_wT = mm

            @blk.vector
            def _(v):
                v.wait_ge(MM, mm_wT)
                v.tensor_copy(wwT[:], psM2[0:64, 64:72]).then_inc(VE, 1)
                v.memset(state[:], 0.0).then_inc(VE, 1)
            ve += 2
            ve_wT = ve

            ve_inc = ve

            for j in range(8):
                for hf in range(2):
                    idx = j * 2 + hf
                    pj = idx % 2

                    @blk.sync
                    def _(s, j=j, hf=hf, pj=pj, idx=idx):
                        if idx >= 2:
                            s.wait_ge(VE, ve_inc + 2 * (idx - 1))
                        s.dma_start(out=segsl[:, pj * 2048:(pj + 1) * 2048],
                                    in_=i_segs[j * 128:(j + 1) * 128,
                                               hf * 2048:(hf + 1) * 2048]).then_inc(LD, 16)
                    ld += 16
                    ld_ij = ld

                    @blk.tensor
                    def _(t, j=j, hf=hf, ld_ij=ld_ij, idx=idx):
                        t.wait_ge(VE, ve_wT)
                        t.wait_ge(LD, ld_rep)
                        if idx >= 1:
                            t.wait_ge(VE, ve_inc + 2 * idx)
                        lh = wwT[:, j:j + 1].broadcast_to((64, 128))
                        for k in range(4):
                            t.matmul(psW[:, k * 512:(k + 1) * 512], lh,
                                     rep[:, hf * 2048 + k * 512:hf * 2048 + (k + 1) * 512],
                                     start=True, stop=True).then_inc(MM, 1)
                    mm += 4
                    mm_wb = mm

                    v0 = ve

                    @blk.vector
                    def _(v, j=j, hf=hf, pj=pj, v0=v0, mm_wb=mm_wb, ld_ij=ld_ij):
                        v.wait_ge(MM, mm_wb)
                        v.wait_ge(LD, ld_ij)
                        v.tensor_tensor(segsl[:, pj * 2048:(pj + 1) * 2048],
                                        segsl[:, pj * 2048:(pj + 1) * 2048],
                                        psW[:], op=ALU.mult).then_inc(VE, 1)
                        v.wait_ge(VE, v0 + 1)
                        v.tensor_tensor(state[:, hf * 2048:(hf + 1) * 2048],
                                        state[:, hf * 2048:(hf + 1) * 2048],
                                        segsl[:, pj * 2048:(pj + 1) * 2048],
                                        op=ALU.add).then_inc(VE, 1)
                    ve += 2

            psW_g.__exit__(None, None, None)
            segsl_g.__exit__(None, None, None)
            rep_g.__exit__(None, None, None)
            _p2_guards = []

            def _p2alloc(name, shape, dt_):
                g_ = nc.sbuf_tensor(name, shape, dt_)
                _p2_guards.append(g_)
                return g_.__enter__()

            snap0 = _p2alloc("snap0", [128, 4096], BF16)
            snap1 = _p2alloc("snap1", [128, 4096], BF16)
            csl = _p2alloc("csl", [128, 4096], F32)
            xcg = _p2alloc("xcg", [128, 4096], BF16)
            szg = _p2alloc("szg", [128, 4096], BF16)
            xtm = _p2alloc("xtm", [128, 4096], BF16)
            xdt = _p2alloc("xdt", [128, 4096], BF16)
            acstm = _p2alloc("acstm", [128, 64], F32)
            dttm = _p2alloc("dttm", [128, 64], F32)
            acsflat = _p2alloc("acsflat", [1, 2048], F32)
            eacs = _p2alloc("eacs", [64, 128], F32)
            eflat = _p2alloc("eflat", [1, 2048], F32)
            dq = _p2alloc("dq", [128, 2048], F32)
            sq = _p2alloc("sq", [128, 2048], BF16)
            cdq = _p2alloc("cdq", [128, 2048], BF16)
            ve_incdone = ve

            psCB_g = nc.psum_tensor("psCB", [128, 128], F32)
            psCB = psCB_g.__enter__()
            psQ_g = nc.psum_tensor("psQ", [128, 2048], F32)
            psQ = psQ_g.__enter__()
            psTb_g = nc.psum_tensor("psTb", [128, 128], BF16)
            psTb = psTb_g.__enter__()
            psY = [psQ[:, 0:128], psQ[:, 1024:1152]]

            ve_grp = [0] * 8
            ve_gate = [0] * 8
            last_evac = {0: None, 1: None}
            for g in range(8):
                @blk.sync
                def _(s, g=g):
                    if g >= 1:
                        s.wait_ge(VE, ve_gate[g - 1])
                    src = i_xconv[:, g * 128:(g + 1) * 128]
                    s.dma_start(out=xcg[:],
                                in_=src.rearrange("(pc p) t -> p pc t", p=128)).then_inc(LD, 16)
                    src2 = i_siluz[:, g * 128:(g + 1) * 128]
                    s.dma_start(out=szg[:],
                                in_=src2.rearrange("(pc p) t -> p pc t", p=128)).then_inc(LD, 16)
                ld += 32
                ld_g = ld

                v0 = ve
                mm0 = mm

                @blk.tensor
                def _(t, g=g, ld_g=ld_g, v0=v0):
                    t.wait_ge(LD, ld_g)
                    t.wait_ge(SE, se_lam)
                    t.wait_ge(VE, ve_incdone)
                    if g >= 1:
                        t.wait_ge(VE, ve_grp[g - 1])
                    for pc in range(32):
                        if pc > 0:
                            t.wait_ge(VE, v0 + pc)
                        t.transpose(psTb[:], xcg[:, pc * 128:(pc + 1) * 128],
                                    ey128[:]).then_inc(MM, 1)
                    t.wait_ge(VE, v0 + 32)
                    t.transpose(psM2[:, 320:384], acsfm[0:64, g * 128:(g + 1) * 128],
                                ey64[:]).then_inc(MM, 1)
                    t.wait_ge(VE, v0 + 33)
                    t.transpose(psM2[:, 384:448], dtfm[0:64, g * 128:(g + 1) * 128],
                                ey64[:]).then_inc(MM, 1)
                mm += 34

                @blk.vector
                def _(v, g=g, mm0=mm0):
                    for pc in range(32):
                        v.wait_ge(MM, mm0 + pc + 1)
                        v.tensor_copy(xtm[:, pc * 128:(pc + 1) * 128], psTb[:]).then_inc(VE, 1)
                    v.wait_ge(MM, mm0 + 33)
                    v.tensor_copy(acstm[:, 0:64], psM2[:, 320:384]).then_inc(VE, 1)
                    v.wait_ge(MM, mm0 + 34)
                    v.tensor_copy(dttm[:, 0:64], psM2[:, 384:448]).then_inc(VE, 1)
                ve += 34
                ve_tr = ve

                @blk.vector
                def _(v, g=g, ve_tr=ve_tr):
                    v.wait_ge(VE, ve_tr)
                    xv = xtm[:, :].rearrange("p (h q) -> p h q", h=64)
                    fv = dttm[:, 0:64].unsqueeze(2).broadcast_to((128, 64, 64))
                    v.tensor_tensor(xdt[:, :].rearrange("p (h q) -> p h q", h=64),
                                    xv, fv, op=ALU.mult).then_inc(VE, 1)
                ve += 1
                ve_xdt = ve

                @blk.scalar
                def _(s, g=g, ve_tr=ve_tr):
                    s.wait_ge(VE, ve_tr)
                    s.activation(eacs[:], acsfm[0:64, g * 128:(g + 1) * 128],
                                 AF.Exp, bias=0.0, scale=1.0).then_inc(SE, 1)
                se += 1
                se_ea = se


                @blk.tensor
                def _(t, g=g, ve_xdt=ve_xdt):
                    t.wait_ge(VE, ve_xdt)
                    t.matmul(psCB[:], Bfull[:, g * 128:(g + 1) * 128],
                             Cfull[:, g * 128:(g + 1) * 128],
                             start=True, stop=True).then_inc(MM, 1)
                mm += 1
                mm_cb = mm

                v0 = ve

                @blk.vector
                def _(v, g=g, v0=v0, ld_g=ld_g):
                    v.wait_ge(LD, ld_g)
                    v.tensor_copy(snap0[:], state[:]).then_inc(VE, 1)
                ve += 1
                ve_sn0 = ve

                for ci in range(2):
                    c = 2 * g + ci

                    @blk.sync
                    def _(s, c=c, ci=ci, ve_w=ve):
                        if not (c == 0):
                            s.wait_ge(VE, ve_w)
                        s.dma_start(out=csl[:],
                                    in_=i_cs[:, c * 4096:(c + 1) * 4096]).then_inc(LD, 16)
                    ld += 16
                    ld_c = ld

                    @blk.tensor
                    def _(t, c=c, ve_w=ve):
                        t.wait_ge(VE, ve_w)
                        t.matmul(psM2[:, 64:128], ones1[:, :],
                                 lam[0:1, c * 64:(c + 1) * 64],
                                 start=True, stop=True).then_inc(MM, 1)
                    mm += 1
                    mm_lam = mm

                    v0 = ve

                    @blk.vector
                    def _(v, c=c, ci=ci, v0=v0, mm_lam=mm_lam, ld_c=ld_c):
                        v.wait_ge(MM, mm_lam)
                        v.wait_ge(LD, ld_c)
                        sv = state[:, :].rearrange("p (h q) -> p h q", h=64)
                        lv = psM2[:, 64:128].unsqueeze(2).broadcast_to((128, 64, 64))
                        v.tensor_tensor(sv, sv, lv, op=ALU.mult).then_inc(VE, 1)
                        v.wait_ge(VE, v0 + 1)
                        v.tensor_tensor(state[:], state[:], csl[:],
                                        op=ALU.add).then_inc(VE, 1)
                        if ci == 0:
                            v.wait_ge(VE, v0 + 2)
                            v.tensor_copy(snap1[:], state[:]).then_inc(VE, 1)
                    ve += 3 if ci == 0 else 2
                ve_scan = ve

                ve_q = [0] * 4
                for q in range(4):
                    @blk.gpsimd
                    def _(gp, g=g, q=q, se_ea=se_ea, ve_w=ve):
                        gp.wait_ge(SE, se_ea)
                        if q > 0:
                            gp.wait_ge(VE, ve_q[q - 1])
                        gp.dma_start(out=acsflat[0:1, :],
                                     in_=acsfm[16 * q:16 * (q + 1),
                                               g * 128:(g + 1) * 128]).then_inc(WL, 16)
                        gp.dma_start(out=eflat[0:1, :],
                                     in_=eacs[16 * q:16 * (q + 1), :]).then_inc(WL, 16)
                    wl += 32
                    wl_fl = wl

                    @blk.tensor
                    def _(t, g=g, q=q, wl_fl=wl_fl, mm_cb=mm_cb):
                        t.wait_ge(WL, wl_fl)
                        t.wait_ge(VE, ve_scan)
                        if q > 0:
                            t.wait_ge(VE, ve_q[q - 1])
                        for k in range(4):
                            t.matmul(psQ[:, k * 512:(k + 1) * 512], ones1[:, :],
                                     acsflat[0:1, k * 512:(k + 1) * 512],
                                     start=True, stop=True).then_inc(MM, 1)
                    mm += 4
                    mm_t1 = mm

                    v0 = ve

                    @blk.vector
                    def _(v, g=g, q=q, v0=v0, mm_t1=mm_t1):
                        v.wait_ge(MM, mm_t1)
                        t2 = acstm[:, q * 16:(q + 1) * 16].unsqueeze(2).broadcast_to((128, 16, 128))
                        v.tensor_tensor(dq[:, :].rearrange("p (h l) -> p h l", h=16),
                                        psQ[:, :].rearrange("p (h l) -> p h l", h=16),
                                        t2, op=ALU.subtract).then_inc(VE, 1)
                        v.wait_ge(VE, v0 + 1)
                        mv = madd[:, :].unsqueeze(1).broadcast_to((128, 16, 128))
                        v.tensor_tensor(dq[:, :].rearrange("p (h l) -> p h l", h=16),
                                        dq[:, :].rearrange("p (h l) -> p h l", h=16),
                                        mv, op=ALU.add).then_inc(VE, 1)
                    ve += 2
                    ve_d = ve

                    @blk.scalar
                    def _(s, g=g, q=q, ve_d=ve_d):
                        s.wait_ge(VE, ve_d)
                        s.activation(dq[:], dq[:], AF.Exp, bias=0.0,
                                     scale=1.0).then_inc(SE, 1)
                    se += 1
                    se_l = se

                    v0 = ve

                    @blk.vector
                    def _(v, g=g, q=q, v0=v0, se_l=se_l, mm_cb=mm_cb):
                        v.wait_ge(SE, se_l)
                        v.wait_ge(MM, mm_cb)
                        cb = psCB[:, :].unsqueeze(1).broadcast_to((128, 16, 128))
                        v.tensor_tensor(sq[:, :].rearrange("p (h l) -> p h l", h=16),
                                        dq[:, :].rearrange("p (h l) -> p h l", h=16),
                                        cb, op=ALU.mult).then_inc(VE, 1)
                    ve += 1
                    ve_s = ve

                    @blk.tensor
                    def _(t, g=g, q=q, ve_s=ve_s):
                        t.wait_ge(VE, ve_s)
                        for k in range(4):
                            t.matmul(psQ[:, k * 512:(k + 1) * 512], ones1[:, :],
                                     eflat[0:1, k * 512:(k + 1) * 512],
                                     start=True, stop=True).then_inc(MM, 1)
                    mm += 4
                    mm_ea = mm

                    v0 = ve

                    @blk.vector
                    def _(v, g=g, q=q, v0=v0, mm_ea=mm_ea):
                        v.wait_ge(MM, mm_ea)
                        cf = Cfull[:, g * 128:(g + 1) * 128].unsqueeze(1).broadcast_to((128, 16, 128))
                        v.tensor_tensor(cdq[:, :].rearrange("p (h l) -> p h l", h=16),
                                        psQ[:, :].rearrange("p (h l) -> p h l", h=16),
                                        cf, op=ALU.mult).then_inc(VE, 1)
                    ve += 1
                    ve_cd = ve

                    for i in range(8):
                        pc = q * 8 + i
                        h0 = 16 * q + 2 * i
                        pp = i % 2

                        le = last_evac[pp]
                        ycol = 0 if pp == 0 else 1024

                        @blk.tensor
                        def _(t, g=g, q=q, i=i, pc=pc, h0=h0, pp=pp,
                              ve_cd=ve_cd, le=le, ycol=ycol):
                            t.wait_ge(VE, ve_cd)
                            if le is not None:
                                t.wait_ge(VE, le)
                            for hh in range(2):
                                h = h0 + hh
                                base = hh * 64
                                t.matmul(psQ[base:base + 64, ycol:ycol + 128],
                                         xdt[:, h * 64:(h + 1) * 64],
                                         sq[:, (h - 16 * q) * 128:(h - 16 * q + 1) * 128],
                                         start=True, stop=False)
                                t.matmul(psQ[base:base + 64, ycol:ycol + 64],
                                         snap0[:, h * 64:(h + 1) * 64],
                                         cdq[:, (h - 16 * q) * 128:(h - 16 * q) * 128 + 64],
                                         start=False, stop=False)
                                ins = t.matmul(psQ[base:base + 64, ycol + 64:ycol + 128],
                                               snap1[:, h * 64:(h + 1) * 64],
                                               cdq[:, (h - 16 * q) * 128 + 64:(h - 16 * q + 1) * 128],
                                               start=False, stop=True)
                            ins.then_inc(MM, 1)
                        mm += 1
                        mm_y = mm

                        @blk.vector
                        def _(v, g=g, pc=pc, pp=pp, mm_y=mm_y, ycol=ycol):
                            v.wait_ge(MM, mm_y)
                            v.scalar_tensor_tensor(yg[:, pc * TPC + g * 128:pc * TPC + (g + 1) * 128],
                                                   xcg[:, pc * 128:(pc + 1) * 128],
                                                   drep[:, pc:pc + 1],
                                                   psQ[:, ycol:ycol + 128],
                                                   op0=ALU.mult, op1=ALU.add).then_inc(VE, 1)
                        ve += 1
                        last_evac[pp] = ve
                    ve_q[q] = ve

                ve_grp[g] = ve

                @blk.vector
                def _(v, g=g, ve_w=ve):
                    v.wait_ge(VE, ve_w)
                    ygv = yg[:, :].rearrange("p (pc t) -> p pc t", pc=32)[:, :, g * 128:(g + 1) * 128]
                    szv = szg[:, :].rearrange("p (pc t) -> p pc t", pc=32)
                    v.tensor_tensor(ygv, ygv, szv, op=ALU.mult).then_inc(VE, 1)
                ve += 1
                ve_gate[g] = ve

            for gg in (psTb_g, psQ_g, psCB_g):
                gg.__exit__(None, None, None)
            for gg in reversed(_p2_guards):
                gg.__exit__(None, None, None)

            wsl_g = nc.sbuf_tensor("wsl", [128, 2 * 32 * 128], BF16)
            wsl = wsl_g.__enter__()
            s0b_g = nc.sbuf_tensor("s0b", [128, 2 * TPC], BF16)
            s0b = s0b_g.__enter__()
            psR_g = nc.psum_tensor("psR", [1, TPC], F32)
            psR = psR_g.__enter__()
            psBc_g = nc.psum_tensor("psBc", [128, TPC], F32)
            psBc = psBc_g.__enter__()
            psO_g = nc.psum_tensor("psO", [128, TPC], F32)
            psO = psO_g.__enter__()

            ve_all = ve_gate[7]
            mm_sq = [0] * 32
            for pc in range(32):
                @blk.scalar
                def _(s, pc=pc):
                    s.wait_ge(VE, ve_all)
                    if pc >= 1:
                        s.wait_ge(MM, mm_sq[pc - 1])
                    s.activation(ysq[:], yg[:, pc * TPC:(pc + 1) * TPC], AF.Square,
                                 bias=0.0, scale=1.0).then_inc(SE, 1)
                se += 1

                @blk.tensor
                def _(t, pc=pc, se_w=se):
                    t.wait_ge(SE, se_w)
                    for k in range(2):
                        t.matmul(psR[0:1, k * 512:(k + 1) * 512], onesb[:],
                                 ysq[:, k * 512:(k + 1) * 512],
                                 start=(pc == 0), stop=(pc == 31)).then_inc(MM, 1)
                mm += 2
                mm_sq[pc] = mm
            mm_r = mm

            v0 = ve

            @blk.vector
            def _(v, v0=v0, mm_r=mm_r):
                v.wait_ge(MM, mm_r)
                v.tensor_scalar(rr[0:1, 0:TPC], psR[0:1, :], scalar1=1.0 / INNER,
                                scalar2=None, op0=ALU.mult).then_inc(VE, 1)
            ve += 1
            ve_rr = ve

            @blk.scalar
            def _(s, ve_rr=ve_rr, se0=se):
                s.wait_ge(VE, ve_rr)
                s.activation(rr[0:1, 0:TPC], rr[0:1, 0:TPC], AF.Ln,
                             bias=cEPS_B2[0:1, 0:1], scale=1.0).then_inc(SE, 1)
                s.wait_ge(SE, se0 + 1)
                s.activation(rr[0:1, 0:TPC], rr[0:1, 0:TPC], AF.Exp,
                             bias=0.0, scale=-0.5).then_inc(SE, 1)
            se += 2
            se_rstd = se

            @blk.tensor
            def _(t, se_rstd=se_rstd):
                t.wait_ge(SE, se_rstd)
                for k in range(2):
                    t.matmul(psBc[:, k * 512:(k + 1) * 512], ones1[:, :],
                             rr[0:1, k * 512:(k + 1) * 512],
                             start=True, stop=True).then_inc(MM, 1)
            mm += 2
            mm_bc = mm

            for pc in range(32):
                @blk.vector
                def _(v, pc=pc, mm_bc=mm_bc):
                    v.wait_ge(MM, mm_bc)
                    v.scalar_tensor_tensor(yg[:, pc * TPC:(pc + 1) * TPC],
                                           yg[:, pc * TPC:(pc + 1) * TPC],
                                           nw[:, pc:pc + 1], psBc[:],
                                           op0=ALU.mult, op1=ALU.mult).then_inc(VE, 1)
            ve += 32
            ve_yn = ve

            so_s0 = [0] * 16
            mm_oc = [0] * 16
            se_oc = [0] * 16
            for oc in range(16):
                p = oc % 2

                @blk.gpsimd
                def _(gp, oc=oc, p=p):
                    if oc >= 2:
                        gp.wait_ge(MM, mm_oc[oc - 2])
                    src = i_woutT[:, oc * 128:(oc + 1) * 128]
                    gp.dma_start(out=wsl[:, p * 4096:(p + 1) * 4096],
                                 in_=src.rearrange("(kc q) m -> q kc m", q=128)).then_inc(WL, 16)
                wl += 16
                wl_oc = wl

                @blk.tensor
                def _(t, oc=oc, p=p, wl_oc=wl_oc):
                    t.wait_ge(WL, wl_oc)
                    t.wait_ge(VE, ve_yn)
                    if oc >= 1:
                        t.wait_ge(SE, se_oc[oc - 1])
                    for kc in range(32):
                        for ks in range(2):
                            ins = t.matmul(psO[:, ks * 512:(ks + 1) * 512],
                                           wsl[:, p * 4096 + kc * 128:p * 4096 + (kc + 1) * 128],
                                           yg[:, kc * TPC + ks * 512:kc * TPC + (ks + 1) * 512],
                                           start=(kc == 0), stop=(kc == 31))
                    ins.then_inc(MM, 1)
                mm += 1
                mm_oc[oc] = mm

                @blk.scalar
                def _(s, oc=oc, p=p, mm_w=mm):
                    s.wait_ge(MM, mm_w)
                    if oc >= 2:
                        s.wait_ge(SO, so_s0[oc - 2])
                    s.activation(s0b[:, p * TPC:(p + 1) * TPC], psO[:], AF.Copy,
                                 bias=0.0, scale=1.0).then_inc(SE, 1)
                se += 1
                se_oc[oc] = se

                @blk.sync
                def _(s, oc=oc, p=p, se_w=se):
                    s.wait_ge(SE, se_w)
                    s.dma_start(out=o_s0[oc * 128:(oc + 1) * 128, :],
                                in_=s0b[:, p * TPC:(p + 1) * TPC]).then_inc(SO, 16)
                so += 16
                so_s0[oc] = so

            @blk.sync
            def _(s):
                s.wait_ge(SO, so)

            psO_g.__exit__(None, None, None)
            psBc_g.__exit__(None, None, None)
            psR_g.__exit__(None, None, None)
            s0b_g.__exit__(None, None, None)
            wsl_g.__exit__(None, None, None)
    return nc


def mk_consts_b2(inputs):
    rep64 = np.zeros((64, 4096), np.float32)
    for h in range(64):
        rep64[h, h * 64:(h + 1) * 64] = 1.0
    ss = np.arange(128)[:, None]
    ll = np.arange(128)[None, :]
    madd = np.where((ss // 64 == ll // 64) & (ll >= ss), 0.0, -1e30).astype(np.float32)
    Dp = np.asarray(inputs["L0_attn_D_param"], np.float32)
    drep = np.zeros((128, 32), np.float32)
    for pc in range(32):
        drep[0:64, pc] = Dp[2 * pc]
        drep[64:128, pc] = Dp[2 * pc + 1]
    nwv = np.asarray(inputs["L0_attn_norm_w"], np.float32)
    normw = nwv.reshape(32, 128).T.copy()
    woutT = f32(np.asarray(inputs["L0_attn_W_out"]).T)
    return {"rep64": f32(rep64), "madd": f32(madd), "drep": drep,
            "normw": f32(normw), "woutT": woutT}


def host_B2(inputs, a_outs, b1_outs, consts, consts2):
    segs = np.concatenate([f32(b1_outs[c]["sego"]) for c in range(NCORES)], axis=0)
    lnl_all = np.concatenate([f32(b1_outs[c]["lnlo"]).reshape(1, 64)
                              for c in range(NCORES)], axis=0)
    maps = []
    for c in range(NCORES):
        r = a_outs[c]
        mm_ = np.zeros((8, 8), np.float32)
        wb = np.full((8, 1), -1e30, np.float32)
        for j in range(c):
            wb[j, 0] = 0.0
            for m in range(j + 1, c):
                mm_[j, m] = 1.0
        maps.append({
            "xconv": r["xconv"], "siluz": r["siluz"], "Bo": r["Bo"], "Co": r["Co"],
            "dto": r["dto"], "Adto": r["Adto"], "Acso": r["Acso"],
            "cso": f32(b1_outs[c]["cso"]), "segs": segs, "lnl_all": lnl_all,
            "mmT": f32(mm_.T), "wbias": wb,
            "rep64": consts2["rep64"], "madd": consts2["madd"],
            "drep": consts2["drep"], "normw": consts2["normw"],
            "eye64": consts["eye64"], "eye128b": consts["eye128b"],
            "woutT": consts2["woutT"],
        })
    return maps



CAP = 384


def build_C():
    import os
    PH = int(os.environ.get("C_PH", "9"))
    nc = bass.Bass("TRN2", target_bir_lowering=False)
    dram = nc.dram_tensor
    i_xnfm = dram("xn_fm", [D, TPC], BF16, kind="ExternalInput").ap()
    i_xntm = dram("xn_tm", [TPC, D], BF16, kind="ExternalInput").ap()
    i_rwT = dram("rwT", [D, 8], F32, kind="ExternalInput").ap()
    i_w1 = dram("w1", [8, D, 512], F32, kind="ExternalInput").ap()
    i_w2 = dram("w2", [8, 512, D], F32, kind="ExternalInput").ap()
    i_w3 = dram("w3", [8, D, 512], F32, kind="ExternalInput").ap()
    i_ufm = dram("u_fm", [D, TPC], F32, kind="ExternalInput").ap()
    i_s0 = dram("s0o", [D, TPC], BF16, kind="ExternalInput").ap()
    i_triu = dram("triu", [128, 128], F32, kind="ExternalInput").ap()
    i_eidx = dram("eidx", [128, 8], F32, kind="ExternalInput").ap()
    i_eye128b = dram("eye128b", [128, 128], BF16, kind="ExternalInput").ap()

    o_x1 = dram("x1o", [D, TPC], F32, kind="ExternalOutput").ap()
    o_dbg = dram("dbgslot", [128, 16], U32, kind="ExternalOutput").ap()

    gin = dram("gin", [8 * CAP, D], BF16)
    gout = dram("gout", [8 * CAP, D], BF16)

    with ExitStack() as ex:
        e = ex.enter_context
        xn = e(nc.sbuf_tensor("xn", [128, 16 * TPC], BF16))
        rwT = e(nc.sbuf_tensor("rwT_s", [128, 16 * 8], BF16))
        onesf = e(nc.sbuf_tensor("onesf", [128, 1], F32))
        triu = e(nc.sbuf_tensor("triu_s", [128, 128], F32))
        eidx = e(nc.sbuf_tensor("eidx_s", [128, 8], F32))
        ey128 = e(nc.sbuf_tensor("ey128", [128, 128], BF16))
        G = e(nc.sbuf_tensor("G", [128, 8 * 8], F32))
        E1 = e(nc.sbuf_tensor("E1", [128, 8], F32))
        E2 = e(nc.sbuf_tensor("E2", [128, 8], F32))
        P1 = e(nc.sbuf_tensor("P1", [128, 8], F32))
        P2 = e(nc.sbuf_tensor("P2", [128, 8], F32))
        scr = e(nc.sbuf_tensor("scr", [128, 32], F32))
        eall = e(nc.sbuf_tensor("eall", [128, 16], F32))
        inde = e(nc.sbuf_tensor("inde", [128, 16], F32))
        slote = e(nc.sbuf_tensor("slote", [128, 16], F32))
        slacc = e(nc.sbuf_tensor("slacc", [128, 16], F32))
        slotu = e(nc.sbuf_tensor("slotu", [128, 16], U32))
        colsum = e(nc.sbuf_tensor("colsum", [1, 48], F32))
        onesb = e(nc.sbuf_tensor("onesb", [128, 1], BF16))
        ones1 = e(nc.sbuf_tensor("ones1", [1, 128], F32))
        xtt = e(nc.sbuf_tensor("xtt", [128, 2 * D], BF16))
        xe = e(nc.sbuf_tensor("xe", [128, 16 * CAP], BF16))
        wup = e(nc.sbuf_tensor("wup", [128, 2 * 16 * 512], BF16))
        w2s = e(nc.sbuf_tensor("w2s", [128, 4 * D], BF16))
        h1s = e(nc.sbuf_tensor("h1s", [128, 4 * CAP], BF16))
        hs = e(nc.sbuf_tensor("hs", [128, 4 * CAP], BF16))
        dsb = e(nc.sbuf_tensor("dsb", [128, 512], BF16))
        g1b = e(nc.sbuf_tensor("g1b", [128, D], BF16))
        g2b = e(nc.sbuf_tensor("g2b", [128, D], BF16))
        cmb = e(nc.sbuf_tensor("cmb", [128, D], BF16))
        gfm = e(nc.sbuf_tensor("gfm", [128, 16 * TPC], BF16))
        usl = e(nc.sbuf_tensor("usl", [128, 2 * TPC], F32))
        s0s = e(nc.sbuf_tensor("s0s", [128, 2 * TPC], BF16))
        x1s = e(nc.sbuf_tensor("x1s", [128, 2 * TPC], F32))

        psG = e(nc.psum_tensor("psG", [128, 1024], F32))
        psU0 = e(nc.psum_tensor("psU0", [128, 384], F32))
        psU1 = e(nc.psum_tensor("psU1", [128, 384], F32))
        psD = e(nc.psum_tensor("psD", [128, 512], F32))
        psT = e(nc.psum_tensor("psT", [128, 128], BF16))

        LD = e(nc.semaphore("LD"))
        WL = e(nc.semaphore("WL"))
        MM = e(nc.semaphore("MM"))
        SE = e(nc.semaphore("SE"))
        VE = e(nc.semaphore("VE"))
        SO = e(nc.semaphore("SO"))
        GP = e(nc.semaphore("GP"))
        ld = wl = mm = se = ve = so = gp = 0

        with nc.Block() as blk:
            @blk.sync
            def _(s):
                s.dma_start(out=xn[:], in_=i_xnfm.rearrange("(kc p) t -> p kc t", p=128)).then_inc(LD, 16)
                s.dma_start(out=triu[:], in_=i_triu[:, :]).then_inc(LD, 16)
                s.dma_start(out=eidx[:], in_=i_eidx[:, :]).then_inc(LD, 16)
                s.dma_start(out=ey128[:], in_=i_eye128b[:, :]).then_inc(LD, 16)
            ld += 64

            @blk.gpsimd
            def _(g):
                g.dma_start(out=rwT[:], in_=i_rwT.rearrange("(kc p) e -> p kc e", p=128)).then_inc(LD, 16)
            ld += 16
            ld_p0 = ld

            @blk.vector
            def _(v):
                v.wait_ge(LD, ld_p0)
                v.memset(onesb[:], 1.0).then_inc(VE, 1)
                v.memset(ones1[:], 1.0).then_inc(VE, 1)
                v.memset(onesf[:], 1.0).then_inc(VE, 1)
            ve += 3
            ve_p0 = ve

            ve_rt_tile = [0] * 8
            for tt in range(8 if PH >= 1 else 0):
                @blk.tensor
                def _(t, tt=tt):
                    t.wait_ge(LD, ld_p0)
                    if tt >= 2:
                        t.wait_ge(VE, ve_rt_tile[tt - 2])
                    for kc in range(16):
                        ins = t.matmul(psG[:, (tt % 2) * 512:(tt % 2) * 512 + 8],
                                       xn[:, kc * TPC + tt * 128:kc * TPC + (tt + 1) * 128],
                                       rwT[:, kc * 8:(kc + 1) * 8],
                                       start=(kc == 0), stop=(kc == 15))
                    ins.then_inc(MM, 1)
                mm += 1
                mm_g = mm

                v0 = ve

                @blk.vector
                def _(v, tt=tt, v0=v0, mm_g=mm_g):
                    p8 = (tt % 2) * 512
                    gg = G[:, tt * 8:(tt + 1) * 8]
                    v.wait_ge(MM, mm_g)
                    v.tensor_copy(gg, psG[:, p8:p8 + 8]).then_inc(VE, 1)
                    v.wait_ge(VE, v0 + 1)
                    m1 = scr[:, 0:1]
                    v.reduce_max(m1, gg, axis=AX.X).then_inc(VE, 1)
                    v.wait_ge(VE, v0 + 2)
                    i1 = scr[:, 2:10]
                    v.tensor_scalar(i1, gg, scalar1=m1, scalar2=None,
                                    op0=ALU.is_ge).then_inc(VE, 1)
                    v.wait_ge(VE, v0 + 3)
                    t1 = scr[:, 10:18]
                    v.tensor_tensor(t1, i1, eidx[:], op=ALU.mult).then_inc(VE, 1)
                    v.wait_ge(VE, v0 + 4)
                    v.reduce_max(E1[:, tt:tt + 1], t1, axis=AX.X).then_inc(VE, 1)
                    g2 = scr[:, 18:26]
                    v.wait_ge(VE, v0 + 5)
                    v.scalar_tensor_tensor(g2, i1, -1e30, gg,
                                           op0=ALU.mult, op1=ALU.add).then_inc(VE, 1)
                    v.wait_ge(VE, v0 + 6)
                    m2 = scr[:, 1:2]
                    v.reduce_max(m2, g2, axis=AX.X).then_inc(VE, 1)
                    v.wait_ge(VE, v0 + 7)
                    i2 = scr[:, 2:10]
                    v.tensor_scalar(i2, g2, scalar1=m2, scalar2=None,
                                    op0=ALU.is_ge).then_inc(VE, 1)
                    v.wait_ge(VE, v0 + 8)
                    v.tensor_tensor(t1, i2, eidx[:], op=ALU.mult).then_inc(VE, 1)
                    v.wait_ge(VE, v0 + 9)
                    v.reduce_max(E2[:, tt:tt + 1], t1, axis=AX.X).then_inc(VE, 1)
                    v.wait_ge(VE, v0 + 10)
                    v.tensor_tensor(scr[:, 0:1], m1, m2, op=ALU.subtract).then_inc(VE, 1)
                ve += 11

                @blk.scalar
                def _(s, tt=tt, ve_w=ve):
                    s.wait_ge(VE, ve_w)
                    s.activation(P1[:, tt:tt + 1], scr[:, 0:1], AF.Sigmoid,
                                 bias=0.0, scale=1.0).then_inc(SE, 1)
                se += 1

                @blk.vector
                def _(v, tt=tt, se_w=se):
                    v.wait_ge(SE, se_w)
                    v.tensor_scalar(P2[:, tt:tt + 1], P1[:, tt:tt + 1],
                                    scalar1=-1.0, scalar2=1.0, op0=ALU.mult,
                                    op1=ALU.add).then_inc(VE, 1)
                ve += 1
                ve_rt_tile[tt] = ve
            ve_rt = ve

            SUB = int(os.environ.get("C_SUB", "9"))

            @blk.vector
            def _(v):
                v.wait_ge(VE, ve_rt)
                v.tensor_copy(eall[:, 0:8], E1[:]).then_inc(VE, 1)
                v.tensor_copy(eall[:, 8:16], E2[:]).then_inc(VE, 1)
            ve += 2
            ve_ea = ve

            ve_sl = [ve] * 8
            for ee in range(8 if (PH >= 1 and SUB >= 2) else 0):
                @blk.vector
                def _(v, ee=ee):
                    v.wait_ge(VE, ve_ea)
                    if ee >= 1:
                        v.wait_ge(VE, ve_sl[ee - 1])
                    v.tensor_scalar(inde[:], eall[:], scalar1=float(ee), scalar2=None,
                                    op0=ALU.is_equal).then_inc(VE, 1)
                ve += 1
                ve_ind = ve

                @blk.tensor
                def _(t, ee=ee, ve_ind=ve_ind):
                    t.wait_ge(VE, ve_ind)
                    t.matmul(psG[:, 16:32], triu[:], inde[:],
                             start=True, stop=True)
                    t.matmul(psG[0:1, 32:48], onesf[:], inde[:],
                             start=True, stop=True).then_inc(MM, 1)
                mm += 1
                mm_tri = mm

                v0 = ve

                @blk.vector
                def _(v, ee=ee, v0=v0, mm_tri=mm_tri):
                    v.wait_ge(MM, mm_tri)
                    v.tensor_copy(colsum[0:1, 0:16], psG[0:1, 32:48]).then_inc(VE, 1)
                    v.memset(colsum[0:1, 32:48], 1.0).then_inc(VE, 1)
                    v.wait_ge(VE, v0 + 2)
                    v.tensor_tensor_scan(colsum[0:1, 16:32], colsum[0:1, 32:48],
                                         colsum[0:1, 0:16], initial=0.0,
                                         op0=ALU.mult, op1=ALU.add).then_inc(VE, 1)
                    v.wait_ge(VE, v0 + 3)
                    v.tensor_tensor(colsum[0:1, 16:32], colsum[0:1, 16:32],
                                    colsum[0:1, 0:16], op=ALU.subtract).then_inc(VE, 1)
                ve += 4
                ve_cs = ve

                @blk.tensor
                def _(t, ee=ee, ve_cs=ve_cs):
                    t.wait_ge(VE, ve_cs)
                    t.matmul(psG[:, 48:64], ones1[:, :], colsum[0:1, 16:32],
                             start=True, stop=True).then_inc(MM, 1)
                mm += 1
                mm_cb = mm

                v0 = ve

                @blk.vector
                def _(v, ee=ee, v0=v0, mm_cb=mm_cb):
                    v.wait_ge(MM, mm_cb)
                    v.tensor_copy(scr[:, 0:16], psG[:, 16:32]).then_inc(VE, 1)
                    v.wait_ge(VE, v0 + 1)
                    v.tensor_tensor(slote[:], scr[:, 0:16], psG[:, 48:64],
                                    op=ALU.add).then_inc(VE, 1)
                    v.wait_ge(VE, v0 + 2)
                    v.tensor_scalar(scr[:, 0:16], slote[:], scalar1=float(CAP) - 0.5,
                                    scalar2=None, op0=ALU.is_ge).then_inc(VE, 1)
                    v.wait_ge(VE, v0 + 3)
                    v.scalar_tensor_tensor(slote[:], scr[:, 0:16], 1e9, slote[:],
                                           op0=ALU.mult, op1=ALU.add).then_inc(VE, 1)
                    v.wait_ge(VE, v0 + 4)
                    v.tensor_scalar(slote[:], slote[:], scalar1=float(CAP * ee),
                                    scalar2=None, op0=ALU.add).then_inc(VE, 1)
                    v.wait_ge(VE, v0 + 5)
                    v.tensor_tensor(inde[:], inde[:], slote[:], op=ALU.mult).then_inc(VE, 1)
                    v.wait_ge(VE, v0 + 6)
                    if ee == 0:
                        v.tensor_copy(slacc[:], inde[:]).then_inc(VE, 1)
                    else:
                        v.tensor_tensor(slacc[:], slacc[:], inde[:],
                                        op=ALU.add).then_inc(VE, 1)
                ve += 7
                ve_sl[ee] = ve
            ve_slots = ve

            @blk.vector
            def _(v):
                v.wait_ge(VE, ve_slots)
                if SUB >= 3:
                    v.tensor_copy(slotu[:], slacc[:]).then_inc(VE, 1)
                else:
                    v.engine_nop().then_inc(VE, 1)
            ve += 1
            ve_slotu = ve

            @blk.sync
            def _(s):
                s.wait_ge(VE, ve_slotu)
                s.dma_start(out=o_dbg[:, :], in_=slotu[:]).then_inc(SO, 16)
            so += 16
            so_slotflush = so

            ld_tts = [0] * 8
            for tt in range(8 if PH >= 2 else 0):
                @blk.sync
                def _(s, tt=tt):
                    if tt >= 2:
                        s.wait_ge(GP, 32 * (tt - 1))
                    s.dma_start(out=xtt[:, (tt % 2) * D:(tt % 2 + 1) * D],
                                in_=i_xntm[tt * 128:(tt + 1) * 128, :]).then_inc(LD, 16)
                ld += 16
                ld_tts[tt] = ld

                for ch in range(2):
                    @blk.gpsimd
                    def _(g, tt=tt, ch=ch, gp0=gp):
                        g.wait_ge(LD, ld_tts[tt])
                        g.wait_ge(SO, so_slotflush)
                        g.wait_ge(GP, gp0)
                        col = ch * 8 + tt
                        g.indirect_dma_start(
                            out=gin.ap()[:, :],
                            in_=xtt[:, (tt % 2) * D:(tt % 2 + 1) * D],
                            out_offset=bass.IndirectOffsetOnAxis(ap=slotu[:, col:col + 1], axis=0),
                            in_offset=None,
                            bounds_check=8 * CAP - 1, oob_is_err=False,
                        ).then_inc(GP, 16)
                    gp += 16
            gp_scatter = gp

            mm_dn_last = 0
            se_dn_last = 0
            for ee in range(8 if PH >= 3 else 0):
                @blk.sync
                def _(s, ee=ee, mm_w=mm_dn_last):
                    s.wait_ge(GP, gp_scatter)
                    if ee >= 1:
                        s.wait_ge(MM, mm_w)
                    for kc in range(16):
                        s.dma_start(out=xe[:, kc * CAP:(kc + 1) * CAP],
                                    in_=gin.ap()[ee * CAP:(ee + 1) * CAP,
                                                 kc * 128:(kc + 1) * 128],
                                    transpose=True).then_inc(LD, 16)
                ld += 256
                ld_xe = ld

                @blk.gpsimd
                def _(g, ee=ee, mm_w=mm_dn_last):
                    if ee >= 1:
                        g.wait_ge(MM, mm_w)
                    g.dma_start(out=wup[:, 0:8192],
                                in_=i_w1[ee].rearrange("(kc p) f -> p kc f", p=128)).then_inc(WL, 16)
                    g.dma_start(out=wup[:, 8192:16384],
                                in_=i_w3[ee].rearrange("(kc p) f -> p kc f", p=128)).then_inc(WL, 16)
                    g.dma_start(out=w2s[:, :],
                                in_=i_w2[ee].rearrange("(fc p) d -> p fc d", p=128)).then_inc(WL, 16)
                wl += 48
                wl_e = wl

                mm_h1 = [0] * 4
                mm_h3 = [0] * 4
                se_h1 = [0] * 4
                ve_hh = [0] * 4
                for fc in range(4):
                    @blk.tensor
                    def _(t, ee=ee, fc=fc, ld_xe=ld_xe, wl_e=wl_e):
                        t.wait_ge(LD, ld_xe)
                        t.wait_ge(WL, wl_e)
                        if fc >= 1:
                            t.wait_ge(SE, se_h1[fc - 1])
                            t.wait_ge(VE, ve_hh[fc - 1])
                        for kc in range(16):
                            ins = t.matmul(psU0[:, 0:CAP],
                                           wup[:, kc * 512 + fc * 128:kc * 512 + (fc + 1) * 128],
                                           xe[:, kc * CAP:(kc + 1) * CAP],
                                           start=(kc == 0), stop=(kc == 15))
                        ins.then_inc(MM, 1)
                        for kc in range(16):
                            ins = t.matmul(psU1[:, 0:CAP],
                                           wup[:, 8192 + kc * 512 + fc * 128:8192 + kc * 512 + (fc + 1) * 128],
                                           xe[:, kc * CAP:(kc + 1) * CAP],
                                           start=(kc == 0), stop=(kc == 15))
                        ins.then_inc(MM, 1)
                    mm += 2
                    mm_h1[fc] = mm - 1
                    mm_h3[fc] = mm

                    @blk.scalar
                    def _(s, ee=ee, fc=fc, mm_w=mm_h1[fc]):
                        s.wait_ge(MM, mm_w)
                        s.activation(h1s[:, fc * CAP:(fc + 1) * CAP], psU0[:, 0:CAP],
                                     AF.Silu, bias=0.0, scale=1.0).then_inc(SE, 1)
                    se += 1
                    se_h1[fc] = se

                    @blk.vector
                    def _(v, ee=ee, fc=fc, se_w=se, mm_w=mm_h3[fc]):
                        v.wait_ge(SE, se_w)
                        v.wait_ge(MM, mm_w)
                        v.tensor_tensor(hs[:, fc * CAP:(fc + 1) * CAP],
                                        h1s[:, fc * CAP:(fc + 1) * CAP],
                                        psU1[:, 0:CAP], op=ALU.mult).then_inc(VE, 1)
                    ve += 1
                    ve_hh[fc] = ve
                ve_h = ve

                for sc in range(3):
                    for dg in range(4):
                        @blk.tensor
                        def _(t, ee=ee, sc=sc, dg=dg, ve_h=ve_h, se_w=se_dn_last):
                            t.wait_ge(VE, ve_h)
                            if se_w:
                                t.wait_ge(SE, se_w)
                            for fc in range(4):
                                ins = t.matmul(psD[:, 0:512],
                                               hs[:, fc * CAP + sc * 128:fc * CAP + sc * 128 + 128],
                                               w2s[:, fc * D + dg * 512:fc * D + (dg + 1) * 512],
                                               start=(fc == 0), stop=(fc == 3))
                            ins.then_inc(MM, 1)
                        mm += 1
                        mm_dn = mm

                        @blk.scalar
                        def _(s, ee=ee, sc=sc, dg=dg, mm_w=mm_dn, so_w=so - 16):
                            s.wait_ge(MM, mm_w)
                            if so_w >= 0:
                                s.wait_ge(SO, so_w + 16)
                            s.activation(dsb[:, 0:512], psD[:, 0:512], AF.Copy,
                                         bias=0.0, scale=1.0).then_inc(SE, 1)
                        se += 1
                        se_dn_last = se

                        @blk.sync
                        def _(s, ee=ee, sc=sc, dg=dg, se_w=se):
                            s.wait_ge(SE, se_w)
                            s.dma_start(out=gout.ap()[ee * CAP + sc * 128:ee * CAP + (sc + 1) * 128,
                                                      dg * 512:(dg + 1) * 512],
                                        in_=dsb[:, 0:512]).then_inc(SO, 16)
                        so += 16
                mm_dn_last = mm
            so_experts = so

            mm_tr_last = 0
            for tt in range(8 if PH >= 4 else 0):
                v0 = ve

                @blk.vector
                def _(v, tt=tt, v0=v0, mm_w=mm_tr_last):
                    v.wait_ge(VE, v0)
                    if mm_w:
                        v.wait_ge(MM, mm_w)
                    v.memset(g1b[:], 0.0).then_inc(VE, 1)
                    v.memset(g2b[:], 0.0).then_inc(VE, 1)
                ve += 2
                ve_z = ve

                @blk.gpsimd
                def _(g, tt=tt, ve_z=ve_z, gp0=gp):
                    g.wait_ge(SO, so_experts)
                    g.wait_ge(SO, so_slotflush)
                    g.wait_ge(VE, ve_z)
                    g.wait_ge(GP, gp0)
                    g.indirect_dma_start(
                        out=g1b[:], in_=gout.ap()[:, :],
                        in_offset=bass.IndirectOffsetOnAxis(ap=slotu[:, tt:tt + 1], axis=0),
                        out_offset=None,
                        bounds_check=8 * CAP - 1, oob_is_err=False,
                    ).then_inc(GP, 16)
                    g.wait_ge(GP, gp0 + 16)
                    g.indirect_dma_start(
                        out=g2b[:], in_=gout.ap()[:, :],
                        in_offset=bass.IndirectOffsetOnAxis(ap=slotu[:, 8 + tt:8 + tt + 1], axis=0),
                        out_offset=None,
                        bounds_check=8 * CAP - 1, oob_is_err=False,
                    ).then_inc(GP, 16)
                gp += 32
                gp_gb = gp

                v0 = ve

                @blk.vector
                def _(v, tt=tt, v0=v0, gp_gb=gp_gb):
                    v.wait_ge(GP, gp_gb)
                    v.tensor_scalar(cmb[:], g1b[:], scalar1=P1[:, tt:tt + 1],
                                    scalar2=None, op0=ALU.mult).then_inc(VE, 1)
                    v.wait_ge(VE, v0 + 1)
                    v.scalar_tensor_tensor(cmb[:], g2b[:], P2[:, tt:tt + 1], cmb[:],
                                           op0=ALU.mult, op1=ALU.add).then_inc(VE, 1)
                ve += 2
                ve_cmb = ve

                mm0 = mm

                @blk.tensor
                def _(t, tt=tt, ve_cmb=ve_cmb, mm0=mm0, se_w=se):
                    t.wait_ge(VE, ve_cmb)
                    for kc in range(16):
                        if kc > 0:
                            t.wait_ge(SE, se_w + kc)
                        elif tt > 0:
                            t.wait_ge(SE, se_w)
                        t.transpose(psT[:], cmb[:, kc * 128:(kc + 1) * 128],
                                    ey128[:]).then_inc(MM, 1)
                mm += 16
                mm_tr_last = mm

                for kc in range(16):
                    @blk.scalar
                    def _(s, tt=tt, kc=kc, mm_w=mm0 + kc + 1):
                        s.wait_ge(MM, mm_w)
                        s.activation(gfm[:, kc * TPC + tt * 128:kc * TPC + (tt + 1) * 128],
                                     psT[:], AF.Silu, bias=0.0, scale=1.0).then_inc(SE, 1)
                    se += 1
            se_gfm = se

            so_x1 = [0] * 16
            ve_x1 = [0] * 16
            for kc in range(16 if PH >= 5 else 0):
                p = kc % 2

                @blk.sync
                def _(s, kc=kc, p=p):
                    if kc >= 2:
                        s.wait_ge(VE, ve_x1[kc - 2])
                    s.dma_start(out=usl[:, p * TPC:(p + 1) * TPC],
                                in_=i_ufm[kc * 128:(kc + 1) * 128, :]).then_inc(LD, 16)
                    s.dma_start(out=s0s[:, p * TPC:(p + 1) * TPC],
                                in_=i_s0[kc * 128:(kc + 1) * 128, :]).then_inc(LD, 16)
                ld += 32
                ld_kc = ld

                v0 = ve

                @blk.vector
                def _(v, kc=kc, p=p, ld_kc=ld_kc, v0=v0):
                    v.wait_ge(LD, ld_kc)
                    v.wait_ge(SE, se_gfm)
                    if kc >= 2:
                        v.wait_ge(SO, so_x1[kc - 2])
                    v.tensor_tensor(x1s[:, p * TPC:(p + 1) * TPC],
                                    s0s[:, p * TPC:(p + 1) * TPC],
                                    gfm[:, kc * TPC:(kc + 1) * TPC],
                                    op=ALU.mult).then_inc(VE, 1)
                    v.wait_ge(VE, v0 + 1)
                    v.tensor_tensor(x1s[:, p * TPC:(p + 1) * TPC],
                                    x1s[:, p * TPC:(p + 1) * TPC],
                                    usl[:, p * TPC:(p + 1) * TPC],
                                    op=ALU.add).then_inc(VE, 1)
                ve += 2
                ve_x1[kc] = ve

                @blk.sync
                def _(s, kc=kc, p=p, ve_w=ve):
                    s.wait_ge(VE, ve_w)
                    s.dma_start(out=o_x1[kc * 128:(kc + 1) * 128, :],
                                in_=x1s[:, p * TPC:(p + 1) * TPC]).then_inc(SO, 16)
                so += 16
                so_x1[kc] = so

            @blk.sync
            def _(s):
                s.wait_ge(SO, so)
    return nc


def host_C(inputs, a_outs, b2_outs, consts, u_fm_full):
    rwT = f32(np.asarray(inputs["L0_ffn_router_w"]).T)
    w1 = f32(inputs["L0_ffn_w1"])
    w2 = f32(inputs["L0_ffn_w2"])
    w3 = f32(inputs["L0_ffn_w3"])
    triu = f32(np.triu(np.ones((128, 128), np.float32), 1))
    eidx = f32(np.tile(np.arange(8, dtype=np.float32), (128, 1)))
    maps = []
    for c in range(NCORES):
        maps.append({
            "xn_fm": a_outs[c]["xn_fm"], "xn_tm": a_outs[c]["xn_tm"],
            "rwT": rwT, "w1": w1, "w2": w2, "w3": w3,
            "u_fm": f32(u_fm_full[:, c * TPC:(c + 1) * TPC]),
            "s0o": b2_outs[c]["s0o"],
            "triu": triu, "eidx": eidx, "eye128b": consts["eye128b"],
        })
    return maps



def build_D():
    import os
    PH = int(os.environ.get("D_PH", "9"))
    nc = bass.Bass("TRN2", target_bir_lowering=False)
    dram = nc.dram_tensor
    i_x1 = dram("x1o", [D, TPC], F32, kind="ExternalInput").ap()
    i_wqkvT = dram("wqkvT", [D, 3 * D], F32, kind="ExternalInput").ap()
    i_wgateT = dram("wgateT", [D, D], F32, kind="ExternalInput").ap()
    i_wqkv2T = dram("wqkv2T", [D, 3 * D], F32, kind="ExternalInput").ap()
    i_wg2T = dram("wg2T", [D, 16], F32, kind="ExternalInput").ap()
    i_slope = dram("slope", [16, 1], F32, kind="ExternalInput").ap()
    i_dmask = dram("dmask", [128, 512], F32, kind="ExternalInput").ap()
    i_armask = dram("armask", [1, TPC], F32, kind="ExternalInput").ap()
    i_kmask = dram("kmask", [128, 2], F32, kind="ExternalInput").ap()
    i_eye128b = dram("eye128b", [128, 128], BF16, kind="ExternalInput").ap()
    i_eye16 = dram("eye16", [16, 16], F32, kind="ExternalInput").ap()

    o_attn = dram("attno", [D, TPC], BF16, kind="ExternalOutput").ap()
    o_qd = dram("qdo", [D, TPC], BF16, kind="ExternalOutput").ap()
    o_kvseg = dram("kvsego", [128, 16 * 128], F32, kind="ExternalOutput").ap()
    o_gate = dram("gateo", [D, TPC], BF16, kind="ExternalOutput").ap()
    o_q2 = dram("q2o", [D, TPC], BF16, kind="ExternalOutput").ap()
    o_ksh = dram("ksho", [128, TPC], BF16, kind="ExternalOutput").ap()
    o_vsh = dram("vsho", [TPC, 128], BF16, kind="ExternalOutput").ap()

    with ExitStack() as ex:
        e = ex.enter_context
        ey128 = e(nc.sbuf_tensor("ey128", [128, 128], BF16))
        ey16 = e(nc.sbuf_tensor("ey16", [16, 16], F32))
        slp = e(nc.sbuf_tensor("slp", [16, 1], F32))
        dmask = e(nc.sbuf_tensor("dmask_s", [128, 512], F32))
        armask = e(nc.sbuf_tensor("armask_s", [1, TPC], F32))
        kmask = e(nc.sbuf_tensor("kmask_s", [128, 2], F32))
        snegs = e(nc.sbuf_tensor("snegs", [128, 16], F32))
        bdf = e(nc.sbuf_tensor("bdf", [128, 16], F32))
        x1f = e(nc.sbuf_tensor("x1f", [128, 16 * TPC], F32))
        xsq = e(nc.sbuf_tensor("xsq", [128, TPC], F32))
        rowA = e(nc.sbuf_tensor("rowA", [1, TPC], F32))
        rowB = e(nc.sbuf_tensor("rowB", [1, TPC], F32))
        abc = e(nc.sbuf_tensor("abc", [128, 2 * TPC], F32))
        xn1 = e(nc.sbuf_tensor("xn1", [128, 16 * TPC], BF16))
        ones1 = e(nc.sbuf_tensor("ones1", [1, 128], F32))
        onesf = e(nc.sbuf_tensor("onesf", [128, 1], F32))
        wsl = e(nc.sbuf_tensor("wsl", [128, 2 * 16 * 128], BF16))
        qh = e(nc.sbuf_tensor("qh", [128, TPC], BF16))
        kh = e(nc.sbuf_tensor("kh", [128, TPC], BF16))
        vh = e(nc.sbuf_tensor("vh", [128, TPC], BF16))
        ktm = e(nc.sbuf_tensor("ktm", [128, TPC], BF16))
        vtm = e(nc.sbuf_tensor("vtm", [128, TPC], BF16))
        ksc = e(nc.sbuf_tensor("ksc", [128, TPC], BF16))
        qd = e(nc.sbuf_tensor("qd", [128, TPC], BF16))
        dth = e(nc.sbuf_tensor("dth", [128, 512], BF16))
        kdc = e(nc.sbuf_tensor("kdc", [128, 2], F32))
        qdrow = e(nc.sbuf_tensor("qdrow", [1, TPC], F32))
        pb = e(nc.sbuf_tensor("pb", [128, 512], BF16))
        kvst = e(nc.sbuf_tensor("kvst", [128, 128], F32))
        kvbf = e(nc.sbuf_tensor("kvbf", [128, 128], BF16))
        aout = e(nc.sbuf_tensor("aout", [128, TPC], BF16))
        G2 = e(nc.sbuf_tensor("G2", [128, 8 * 16], F32))
        IND = e(nc.sbuf_tensor("IND", [128, 8 * 16], BF16))
        T1V = e(nc.sbuf_tensor("T1V", [128, 8], F32))
        kacc = e(nc.sbuf_tensor("kacc", [128, 8 * 128], F32))
        vacc = e(nc.sbuf_tensor("vacc", [128, 8 * 128], F32))
        shtm = e(nc.sbuf_tensor("shtm", [128, 8 * 128], BF16))
        gsl = e(nc.sbuf_tensor("gsl", [128, 2 * TPC], BF16))

        psM = e(nc.psum_tensor("psM", [128, 512], F32))
        psP0 = e(nc.psum_tensor("psP0", [128, TPC], F32))
        psP1 = e(nc.psum_tensor("psP1", [128, TPC], F32))
        psQK = e(nc.psum_tensor("psQK", [128, 1024], F32))
        psT = e(nc.psum_tensor("psT", [128, 128], BF16))
        psIN = psM[:, 128:384]

        LD = e(nc.semaphore("LD"))
        WL = e(nc.semaphore("WL"))
        MM = e(nc.semaphore("MM"))
        SE = e(nc.semaphore("SE"))
        VE = e(nc.semaphore("VE"))
        SO = e(nc.semaphore("SO"))
        ld = wl = mm = se = ve = so = 0
        psP = [psP0, psP1]

        with nc.Block() as blk:
            @blk.sync
            def _(s):
                for (dst, src) in ((ey128, i_eye128b), (ey16, i_eye16),
                                   (slp, i_slope), (dmask, i_dmask),
                                   (armask, i_armask), (kmask, i_kmask)):
                    s.dma_start(out=dst[:], in_=src[:, :]).then_inc(LD, 16)
                s.dma_start(out=x1f[:], in_=i_x1.rearrange("(kc p) t -> p kc t", p=128)).then_inc(LD, 16)
            ld += 7 * 16
            ld_p0 = ld

            @blk.vector
            def _(v):
                v.wait_ge(LD, ld_p0)
                v.memset(ones1[:], 1.0).then_inc(VE, 1)
                v.memset(onesf[:], 1.0).then_inc(VE, 1)
            ve += 2
            ve_p0 = ve

            SLP = int(os.environ.get("D_SLP", "9"))

            @blk.tensor
            def _(t):
                t.wait_ge(VE, ve_p0)
                if SLP >= 1:
                    t.transpose(psM[0:1, 0:16], slp[:], ey16[:]).then_inc(MM, 1)
                else:
                    t.nop().then_inc(MM, 1)
            mm += 1
            mm_st = mm

            @blk.vector
            def _(v):
                v.wait_ge(MM, mm_st)
                if SLP >= 2:
                    v.tensor_copy(rowA[0:1, 0:16], psM[0:1, 0:16]).then_inc(VE, 1)
                else:
                    v.engine_nop().then_inc(VE, 1)
            ve += 1
            ve_srow = ve

            @blk.tensor
            def _(t):
                t.wait_ge(VE, ve_srow)
                if SLP >= 3:
                    t.matmul(psM[:, 16:32], ones1[:, :], rowA[0:1, 0:16],
                             start=True, stop=True).then_inc(MM, 1)
                else:
                    t.nop().then_inc(MM, 1)
            mm += 1
            mm_sb = mm

            @blk.vector
            def _(v):
                v.wait_ge(MM, mm_sb)
                v.tensor_scalar(snegs[:], psM[:, 16:32], scalar1=-1.0, scalar2=None,
                                op0=ALU.mult).then_inc(VE, 1)
            ve += 1
            ve_sn = ve

            @blk.scalar
            def _(s):
                s.wait_ge(VE, ve_sn)
                s.activation(bdf[:], snegs[:], AF.Exp, bias=0.0, scale=256.0).then_inc(SE, 1)
            se += 1
            se_bdf = se

            LNG = int(os.environ.get("D_LN", "9"))
            mm_ln = mm
            se_sq = [0] * 16
            for kc in range(16 if LNG >= 1 else 0):
                @blk.scalar
                def _(s, kc=kc, mm_w=mm_ln):
                    s.wait_ge(LD, ld_p0)
                    if kc >= 1:
                        s.wait_ge(MM, mm_ln + kc)
                    s.activation(xsq[:], x1f[:, kc * TPC:(kc + 1) * TPC], AF.Square,
                                 bias=0.0, scale=1.0).then_inc(SE, 1)
                se += 1
                se_sq[kc] = se

                @blk.tensor
                def _(t, kc=kc, se_w=se):
                    t.wait_ge(SE, se_w)
                    t.wait_ge(SE, se_bdf)
                    ins = None
                    for ks in range(2):
                        t.matmul(psP0[0:1, ks * 512:(ks + 1) * 512],
                                 onesf[:], xsq[:, ks * 512:(ks + 1) * 512],
                                 start=(kc == 0), stop=(kc == 15))
                        ins = t.matmul(psP1[0:1, ks * 512:(ks + 1) * 512],
                                       onesf[:], x1f[:, kc * TPC + ks * 512:kc * TPC + ks * 512 + 512],
                                       start=(kc == 0), stop=(kc == 15))
                    ins.then_inc(MM, 1)
                mm += 1
            mm_stats = mm

            v0 = ve

            @blk.vector
            def _(v, v0=v0):
                v.wait_ge(MM, mm_stats)
                if LNG < 2:
                    v.engine_nop().then_inc(VE, 1)
                    v.engine_nop().then_inc(VE, 1)
                    v.engine_nop().then_inc(VE, 1)
                    v.engine_nop().then_inc(VE, 1)
                    v.engine_nop().then_inc(VE, 1)
                    return
                v.tensor_scalar(rowA[0:1, :], psP1[0:1, 0:TPC], scalar1=1.0 / D,
                                scalar2=None, op0=ALU.mult).then_inc(VE, 1)
                v.wait_ge(VE, v0 + 1)
                v.tensor_scalar(rowB[0:1, :], psP0[0:1, 0:TPC], scalar1=1.0 / D,
                                scalar2=None, op0=ALU.mult).then_inc(VE, 1)
                v.wait_ge(VE, v0 + 2)
                v.tensor_tensor(xsq[0:1, 0:TPC], rowA[0:1, :], rowA[0:1, :],
                                op=ALU.mult).then_inc(VE, 1)
                v.wait_ge(VE, v0 + 3)
                v.tensor_tensor(rowB[0:1, :], rowB[0:1, :], xsq[0:1, 0:TPC],
                                op=ALU.subtract).then_inc(VE, 1)
                v.memset(xsq[0:1, 0:1], 1e-5).then_inc(VE, 1)
            ve += 5
            ve_var = ve

            @blk.scalar
            def _(s, se0=se):
                s.wait_ge(VE, ve_var)
                if LNG < 3:
                    s.nop().then_inc(SE, 1)
                    s.nop().then_inc(SE, 1)
                    return
                s.activation(rowB[0:1, :], rowB[0:1, :], AF.Ln,
                             bias=xsq[0:1, 0:1], scale=1.0).then_inc(SE, 1)
                s.wait_ge(SE, se0 + 1)
                s.activation(rowB[0:1, :], rowB[0:1, :], AF.Exp,
                             bias=0.0, scale=-0.5).then_inc(SE, 1)
            se += 2
            se_rstd = se

            v0 = ve

            @blk.vector
            def _(v, v0=v0):
                v.wait_ge(SE, se_rstd)
                v.tensor_tensor(rowA[0:1, :], rowA[0:1, :], rowB[0:1, :],
                                op=ALU.mult).then_inc(VE, 1)
                v.wait_ge(VE, v0 + 1)
                v.tensor_scalar(rowA[0:1, :], rowA[0:1, :], scalar1=-1.0,
                                scalar2=None, op0=ALU.mult).then_inc(VE, 1)
            ve += 2
            ve_ab = ve

            @blk.tensor
            def _(t):
                t.wait_ge(VE, ve_ab)
                ins = None
                for ks in range(2):
                    t.matmul(psP0[:, ks * 512:(ks + 1) * 512], ones1[:, :],
                             rowB[0:1, ks * 512:(ks + 1) * 512],
                             start=True, stop=True)
                    ins = t.matmul(psP1[:, ks * 512:(ks + 1) * 512], ones1[:, :],
                                   rowA[0:1, ks * 512:(ks + 1) * 512],
                                   start=True, stop=True)
                ins.then_inc(MM, 1)
            mm += 1
            mm_ab = mm

            v0 = ve

            @blk.vector
            def _(v, v0=v0):
                v.wait_ge(MM, mm_ab)
                v.tensor_copy(abc[:, 0:TPC], psP0[:, 0:TPC]).then_inc(VE, 1)
                v.tensor_copy(abc[:, TPC:2 * TPC], psP1[:, 0:TPC]).then_inc(VE, 1)
            ve += 2
            ve_abc = ve

            ve_xn = [0] * 16
            for kc in range(16):
                v0 = ve

                @blk.vector
                def _(v, kc=kc, v0=v0):
                    v.wait_ge(VE, ve_abc)
                    v.tensor_tensor(xn1[:, kc * TPC:(kc + 1) * TPC],
                                    x1f[:, kc * TPC:(kc + 1) * TPC],
                                    abc[:, 0:TPC], op=ALU.mult).then_inc(VE, 1)
                    v.wait_ge(VE, v0 + 1)
                    v.tensor_tensor(xn1[:, kc * TPC:(kc + 1) * TPC],
                                    xn1[:, kc * TPC:(kc + 1) * TPC],
                                    abc[:, TPC:2 * TPC], op=ALU.add).then_inc(VE, 1)
                ve += 2
                ve_xn[kc] = ve
            ve_xn1 = ve

            pstate = {"n": 0, "se0": 0, "se1": 0, "mmw": {0: 0, 1: 0}}

            def proj_oc(wT, oc):
                nonlocal ld, wl, mm
                pp = pstate["n"] % 2
                pstate["n"] += 1
                se_free = pstate["se0"] if pp == 0 else pstate["se1"]
                mm_prev = pstate["mmw"][pp]

                @blk.gpsimd
                def _(g, oc=oc, pp=pp, wT=wT, mm_prev=mm_prev):
                    if mm_prev:
                        g.wait_ge(MM, mm_prev)
                    src = wT[:, oc * 128:(oc + 1) * 128]
                    g.dma_start(out=wsl[:, pp * 2048:(pp + 1) * 2048],
                                in_=src.rearrange("(kc p) m -> p kc m", p=128)).then_inc(WL, 16)
                wl += 16
                wl_w = wl

                @blk.tensor
                def _(t, oc=oc, pp=pp, wl_w=wl_w, se_free=se_free):
                    t.wait_ge(WL, wl_w)
                    t.wait_ge(VE, ve_xn1)
                    if se_free:
                        t.wait_ge(SE, se_free)
                    for kc in range(16):
                        for ks in range(2):
                            ins = t.matmul(psP[pp][:, ks * 512:(ks + 1) * 512],
                                           wsl[:, pp * 2048 + kc * 128:pp * 2048 + (kc + 1) * 128],
                                           xn1[:, kc * TPC + ks * 512:kc * TPC + ks * 512 + 512],
                                           start=(kc == 0), stop=(kc == 15))
                    ins.then_inc(MM, 1)
                mm += 1
                pstate["mmw"][pp] = mm
                return pp, mm

            def proj_done(pp):
                if pp == 0:
                    pstate["se0"] = se
                else:
                    pstate["se1"] = se

            se_dth = 0
            ve_blk_last = 0
            se_ev = {}
            mm_kv = {}
            so_ah = [0] * 16
            prev_head_done = 0
            for h in range(16 if PH >= 2 else 0):
                for (oc, dst, act) in ((h, qh, AF.Silu), (16 + h, kh, AF.Silu),
                                       (32 + h, vh, AF.Silu)):
                    pp, mm_p = proj_oc(i_wqkvT, oc)

                    @blk.scalar
                    def _(s, oc=oc, dst=dst, act=act, pp=pp, mm_p=mm_p, h=h):
                        s.wait_ge(MM, mm_p)
                        if h >= 1:
                            s.wait_ge(SO, so_ah[h - 1])
                        s.activation(dst[:], psP[pp][:, 0:TPC], act,
                                     bias=0.0, scale=1.0).then_inc(SE, 1)
                    se += 1
                    proj_done(pp)
                se_qkv = se

                v0 = ve
                mm0 = mm

                @blk.tensor
                def _(t, h=h, v0=v0, se_qkv=se_qkv):
                    t.wait_ge(SE, se_qkv)
                    t.wait_ge(VE, v0)
                    for i8 in range(8):
                        if i8 > 0:
                            t.wait_ge(VE, v0 + i8)
                        t.transpose(psT[:], kh[:, i8 * 128:(i8 + 1) * 128],
                                    ey128[:]).then_inc(MM, 1)
                    for i8 in range(8):
                        t.wait_ge(VE, v0 + 8 + i8)
                        t.transpose(psT[:], vh[:, i8 * 128:(i8 + 1) * 128],
                                    ey128[:]).then_inc(MM, 1)
                mm += 16

                @blk.vector
                def _(v, h=h, mm0=mm0):
                    for i8 in range(8):
                        v.wait_ge(MM, mm0 + i8 + 1)
                        v.tensor_copy(ktm[:, i8 * 128:(i8 + 1) * 128], psT[:]).then_inc(VE, 1)
                    for i8 in range(8):
                        v.wait_ge(MM, mm0 + 8 + i8 + 1)
                        v.tensor_copy(vtm[:, i8 * 128:(i8 + 1) * 128], psT[:]).then_inc(VE, 1)
                ve += 16
                ve_tm = ve

                @blk.scalar
                def _(s, h=h, ve_tm=ve_tm):
                    s.wait_ge(VE, ve_tm)
                    s.activation(dth[:], dmask[:], AF.Exp, bias=0.0,
                                 scale=snegs[:, h:h + 1]).then_inc(SE, 1)
                    s.activation(kdc[:], kmask[:], AF.Exp, bias=0.0,
                                 scale=snegs[:, h:h + 1]).then_inc(SE, 1)
                    s.activation(qdrow[0:1, :], armask[0:1, :], AF.Exp, bias=0.0,
                                 scale=snegs[0:1, h:h + 1]).then_inc(SE, 1)
                se += 3
                se_dk = se

                @blk.tensor
                def _(t, h=h, se_dk=se_dk):
                    t.wait_ge(SE, se_dk)
                    for ks in range(2):
                        t.matmul(psQK[:, 0:512], ones1[:, :],
                                 qdrow[0:1, ks * 512:(ks + 1) * 512],
                                 start=True, stop=True).then_inc(MM, 1)
                        t.wait_ge(VE, ve + 1 + ks)
                mm += 2
                mm_qd0 = mm

                @blk.vector
                def _(v, h=h, mm_qd0=mm_qd0):
                    for ks in range(2):
                        v.wait_ge(MM, mm_qd0 - 1 + ks)
                        v.tensor_tensor(qd[:, ks * 512:(ks + 1) * 512],
                                        qh[:, ks * 512:(ks + 1) * 512],
                                        psQK[:, 0:512], op=ALU.mult).then_inc(VE, 1)
                ve += 2
                ve_qd = ve

                @blk.sync
                def _(s, h=h, ve_qd=ve_qd):
                    s.wait_ge(VE, ve_qd)
                    s.dma_start(out=o_qd[h * 128:(h + 1) * 128, :], in_=qd[:]).then_inc(SO, 16)
                so += 16

                v0 = ve

                @blk.vector
                def _(v, h=h, v0=v0, se_dk=se_dk):
                    v.wait_ge(SE, se_dk)
                    for i8 in range(8):
                        v.tensor_scalar(ksc[:, i8 * 128:(i8 + 1) * 128],
                                        ktm[:, i8 * 128:(i8 + 1) * 128],
                                        scalar1=kdc[:, i8 % 2:i8 % 2 + 1],
                                        scalar2=None, op0=ALU.mult).then_inc(VE, 1)
                    v.memset(kvst[:], 0.0).then_inc(VE, 1)
                    v.memset(kvbf[:], 0.0).then_inc(VE, 1)
                ve += 10
                ve_ks = ve

                ve_bl = ve
                for bi in range(4):
                    mm0 = mm

                    @blk.tensor
                    def _(t, h=h, bi=bi, ve_ks=ve_ks, mm0=mm0, ve_bl=ve_bl):
                        t.wait_ge(VE, ve_bl)
                        for hf in range(2):
                            t.matmul(psQK[:, hf * 512:hf * 512 + 256],
                                     kh[:, bi * 256 + hf * 128:bi * 256 + (hf + 1) * 128],
                                     qh[:, bi * 256:(bi + 1) * 256],
                                     start=True, stop=True).then_inc(MM, 1)
                    mm += 2

                    v0 = ve

                    @blk.vector
                    def _(v, h=h, bi=bi, v0=v0, mm0=mm0):
                        for hf in range(2):
                            v.wait_ge(MM, mm0 + hf + 1)
                            v.tensor_tensor(pb[:, hf * 256:(hf + 1) * 256],
                                            psQK[:, hf * 512:hf * 512 + 256],
                                            dth[:, hf * 256:(hf + 1) * 256],
                                            op=ALU.mult).then_inc(VE, 1)
                    ve += 2
                    ve_pb = ve
                    mm0b = mm

                    @blk.tensor
                    def _(t, h=h, bi=bi, ve_pb=ve_pb, mm0b=mm0b):
                        t.wait_ge(VE, ve_pb)
                        t.matmul(psIN,
                                 vtm[:, bi * 256:bi * 256 + 128].rearrange("a b -> a b") if False
                                 else vtm[:, bi * 256 * 0 + 0:128] if False
                                 else vtm[:, (bi * 2) * 128:(bi * 2 + 1) * 128],
                                 pb[:, 0:256], start=True, stop=False)
                        t.matmul(psIN,
                                 vtm[:, (bi * 2 + 1) * 128:(bi * 2 + 2) * 128],
                                 pb[:, 256:512], start=False, stop=False)
                        t.matmul(psIN, kvbf[:],
                                 qd[:, bi * 256:(bi + 1) * 256],
                                 start=False, stop=True).then_inc(MM, 1)
                        t.matmul(psQK[:, 0:128],
                                 ksc[:, (bi * 2) * 128:(bi * 2 + 1) * 128],
                                 vtm[:, (bi * 2) * 128:(bi * 2 + 1) * 128],
                                 start=True, stop=False)
                        t.matmul(psQK[:, 0:128],
                                 ksc[:, (bi * 2 + 1) * 128:(bi * 2 + 2) * 128],
                                 vtm[:, (bi * 2 + 1) * 128:(bi * 2 + 2) * 128],
                                 start=False, stop=True).then_inc(MM, 1)
                    mm += 2
                    mm_in = mm

                    v0 = ve

                    @blk.vector
                    def _(v, h=h, bi=bi, v0=v0, mm_in=mm_in):
                        v.wait_ge(MM, mm_in - 1)
                        v.tensor_copy(aout[:, bi * 256:(bi + 1) * 256],
                                      psIN).then_inc(VE, 1)
                        v.wait_ge(MM, mm_in)
                        v.wait_ge(VE, v0 + 1)
                        v.scalar_tensor_tensor(kvst[:], kvst[:], bdf[:, h:h + 1],
                                               psQK[:, 0:128], op0=ALU.mult,
                                               op1=ALU.add).then_inc(VE, 1)
                        v.wait_ge(VE, v0 + 2)
                        v.tensor_copy(kvbf[:], kvst[:]).then_inc(VE, 1)
                    ve += 3
                    ve_bl = ve
                ve_blocks = ve

                @blk.sync
                def _(s, h=h, ve_blocks=ve_blocks):
                    s.wait_ge(VE, ve_blocks)
                    s.dma_start(out=o_attn[h * 128:(h + 1) * 128, :], in_=aout[:]).then_inc(SO, 16)
                    s.dma_start(out=o_kvseg[:, h * 128:(h + 1) * 128], in_=kvst[:]).then_inc(SO, 16)
                so += 32
                so_ah[h] = so
                prev_head_done = mm
            so_light = so

            @blk.gpsimd
            def _(g):
                g.dma_start(out=shtm[:, 0:256],
                            in_=i_wg2T.rearrange("(kc p) m -> p kc m", p=128)).then_inc(WL, 16)
            wl += 16
            wl_g2 = wl

            ve_light_end = ve
            ve_g2t = [0] * 8
            for tt in range(8 if PH >= 3 else 0):
                pg2 = [psM, psQK][tt % 2]

                @blk.tensor
                def _(t, tt=tt, wl_g2=wl_g2, pg2=pg2):
                    t.wait_ge(WL, wl_g2)
                    t.wait_ge(VE, ve_light_end)
                    if tt >= 2:
                        t.wait_ge(VE, ve_g2t[tt - 2])
                    for kc in range(16):
                        ins = t.matmul(pg2[:, 0:16],
                                       xn1[:, kc * TPC + tt * 128:kc * TPC + (tt + 1) * 128],
                                       shtm[:, kc * 16:(kc + 1) * 16],
                                       start=(kc == 0), stop=(kc == 15))
                    ins.then_inc(MM, 1)
                mm += 1
                mm_g2 = mm

                v0 = ve

                @blk.vector
                def _(v, tt=tt, v0=v0, mm_g2=mm_g2, pg2=pg2):
                    v.wait_ge(MM, mm_g2)
                    gg = G2[:, tt * 16:(tt + 1) * 16]
                    v.tensor_copy(gg, pg2[:, 0:16]).then_inc(VE, 1)
                    v.wait_ge(VE, v0 + 1)
                    v.reduce_max(T1V[:, tt:tt + 1], gg, axis=AX.X).then_inc(VE, 1)
                    v.wait_ge(VE, v0 + 2)
                    v.tensor_scalar(IND[:, tt * 16:(tt + 1) * 16], gg,
                                    scalar1=T1V[:, tt:tt + 1], scalar2=None,
                                    op0=ALU.is_ge).then_inc(VE, 1)
                ve += 3
                ve_g2t[tt] = ve
            ve_g2 = ve

            v0 = ve

            @blk.vector
            def _(v, v0=v0):
                v.wait_ge(VE, ve_g2)
                v.tensor_scalar(T1V[:, 0:8], T1V[:, 0:8], scalar1=-1.0,
                                scalar2=None, op0=ALU.mult).then_inc(VE, 1)
            ve += 1
            ve_negm = ve

            se_ex = [0] * 8
            for tt in range(8):
                @blk.scalar
                def _(s, tt=tt):
                    s.wait_ge(VE, ve_negm)
                    s.activation(G2[:, tt * 16:(tt + 1) * 16],
                                 G2[:, tt * 16:(tt + 1) * 16], AF.Exp,
                                 bias=T1V[:, tt:tt + 1], scale=1.0).then_inc(SE, 1)
                se += 1
                se_ex[tt] = se

            v0 = ve

            @blk.vector
            def _(v, v0=v0):
                v.wait_ge(SE, se_ex[7])
                for tt in range(8):
                    v.reduce_sum(kacc[:, tt:tt + 1], G2[:, tt * 16:(tt + 1) * 16],
                                 axis=AX.X).then_inc(VE, 1)
                v.wait_ge(VE, v0 + 8)
                v.reciprocal(T1V[:, 0:8], kacc[:, 0:8]).then_inc(VE, 1)
            ve += 9
            ve_top1 = ve

            v0 = ve

            @blk.vector
            def _(v, v0=v0):
                v.wait_ge(VE, ve_top1)
                v.memset(kacc[:], 0.0).then_inc(VE, 1)
                v.memset(vacc[:], 0.0).then_inc(VE, 1)
            ve += 2
            ve_acc0 = ve

            ve_sel = ve_acc0
            so_q2_prev = 0
            for h in range(16 if PH >= 4 else 0):
                for (oc, isq) in ((h, True), (16 + h, False), (32 + h, False)):
                    pp, mm_p = proj_oc(i_wqkv2T, oc)
                    if isq:
                        @blk.scalar
                        def _(s, oc=oc, pp=pp, mm_p=mm_p, h=h, so_w=so_q2_prev):
                            s.wait_ge(MM, mm_p)
                            if so_w:
                                s.wait_ge(SO, so_w)
                            s.activation(qh[:], psP[pp][:, 0:TPC], AF.Copy,
                                         bias=0.0, scale=1.0).then_inc(SE, 1)
                        se += 1
                        se_q2 = se
                        proj_done(pp)

                        @blk.sync
                        def _(s, h=h, se_q2=se_q2):
                            s.wait_ge(SE, se_q2)
                            s.dma_start(out=o_q2[h * 128:(h + 1) * 128, :],
                                        in_=qh[:]).then_inc(SO, 16)
                        so += 16
                        so_q2_prev = so
                    else:
                        isk = (oc < 32)
                        dst = kh if isk else vh

                        @blk.scalar
                        def _(s, oc=oc, pp=pp, mm_p=mm_p, dst=dst):
                            s.wait_ge(MM, mm_p)
                            s.activation(dst[:], psP[pp][:, 0:TPC], AF.Copy,
                                         bias=0.0, scale=1.0).then_inc(SE, 1)
                        se += 1
                        se_kv = se
                        proj_done(pp)

                        v0 = ve
                        mm0 = mm

                        @blk.tensor
                        def _(t, dst=dst, v0=v0, se_kv=se_kv):
                            t.wait_ge(SE, se_kv)
                            t.wait_ge(VE, v0)
                            for i8 in range(8):
                                if i8 > 0:
                                    t.wait_ge(VE, v0 + i8)
                                t.transpose(psT[:], dst[:, i8 * 128:(i8 + 1) * 128],
                                            ey128[:]).then_inc(MM, 1)
                        mm += 8

                        @blk.vector
                        def _(v, h=h, isk=isk, v0=v0, mm0=mm0):
                            acc = kacc if isk else vacc
                            for i8 in range(8):
                                v.wait_ge(MM, mm0 + i8 + 1)
                                v.tensor_copy(ktm[:, i8 * 128:(i8 + 1) * 128],
                                              psT[:]).then_inc(VE, 1)
                            v.wait_ge(VE, v0 + 8)
                            for i8 in range(8):
                                v.wait_ge(VE, v0 + 8 + i8)
                                v.scalar_tensor_tensor(acc[:, i8 * 128:(i8 + 1) * 128],
                                                       ktm[:, i8 * 128:(i8 + 1) * 128],
                                                       IND[:, i8 * 16 + h:i8 * 16 + h + 1],
                                                       acc[:, i8 * 128:(i8 + 1) * 128],
                                                       op0=ALU.mult, op1=ALU.add).then_inc(VE, 1)
                        ve += 16
                        ve_sel = ve
            ve_selend = ve

            v0 = ve

            @blk.vector
            def _(v, v0=v0):
                v.wait_ge(VE, ve_selend)
                for i8 in range(8):
                    v.tensor_scalar(shtm[:, i8 * 128:(i8 + 1) * 128],
                                    vacc[:, i8 * 128:(i8 + 1) * 128],
                                    scalar1=T1V[:, i8:i8 + 1], scalar2=None,
                                    op0=ALU.mult).then_inc(VE, 1)
            ve += 8
            ve_vsh = ve

            @blk.sync
            def _(s):
                s.wait_ge(VE, ve_vsh)
                for i8 in range(8):
                    s.dma_start(out=o_vsh[i8 * 128:(i8 + 1) * 128, :],
                                in_=shtm[:, i8 * 128:(i8 + 1) * 128]).then_inc(SO, 16)
            so += 128

            v0 = ve

            @blk.vector
            def _(v, v0=v0):
                v.wait_ge(VE, ve_vsh)
                for i8 in range(8):
                    v.tensor_copy(ktm[:, i8 * 128:(i8 + 1) * 128],
                                  kacc[:, i8 * 128:(i8 + 1) * 128]).then_inc(VE, 1)
            ve += 8
            ve_kbf = ve
            mm0 = mm

            @blk.tensor
            def _(t, v0=ve, mm0=mm0):
                t.wait_ge(VE, ve_kbf)
                for i8 in range(8):
                    if i8 > 0:
                        t.wait_ge(VE, v0 + i8)
                    t.transpose(psT[:], ktm[:, i8 * 128:(i8 + 1) * 128],
                                ey128[:]).then_inc(MM, 1)
            mm += 8

            @blk.vector
            def _(v, mm0=mm0):
                for i8 in range(8):
                    v.wait_ge(MM, mm0 + i8 + 1)
                    v.tensor_copy(vtm[:, i8 * 128:(i8 + 1) * 128], psT[:]).then_inc(VE, 1)
            ve += 8
            ve_ksh = ve

            @blk.sync
            def _(s):
                s.wait_ge(VE, ve_ksh)
                s.dma_start(out=o_ksh[:, :], in_=vtm[:, 0:TPC]).then_inc(SO, 16)
            so += 16

            so_g = [0] * 16
            for oc in range(16 if PH >= 5 else 0):
                pp, mm_p = proj_oc(i_wgateT, oc)

                @blk.scalar
                def _(s, oc=oc, pp=pp, mm_p=mm_p, so_w=so_g[oc - 2] if oc >= 2 else 0):
                    s.wait_ge(MM, mm_p)
                    if so_w:
                        s.wait_ge(SO, so_w)
                    s.activation(gsl[:, (oc % 2) * TPC:(oc % 2 + 1) * TPC],
                                 psP[pp][:, 0:TPC], AF.Sigmoid,
                                 bias=0.0, scale=1.0).then_inc(SE, 1)
                se += 1
                proj_done(pp)

                @blk.sync
                def _(s, oc=oc, se_w=se):
                    s.wait_ge(SE, se_w)
                    s.dma_start(out=o_gate[oc * 128:(oc + 1) * 128, :],
                                in_=gsl[:, (oc % 2) * TPC:(oc % 2 + 1) * TPC]).then_inc(SO, 16)
                so += 16
                so_g[oc] = so

            @blk.sync
            def _(s):
                s.wait_ge(SO, so)
    return nc



def build_E():
    nc = bass.Bass("TRN2", target_bir_lowering=False)
    dram = nc.dram_tensor
    i_attn = dram("attno", [D, TPC], BF16, kind="ExternalInput").ap()
    i_qd = dram("qdo", [D, TPC], BF16, kind="ExternalInput").ap()
    i_kvs = dram("kvstack", [8 * 128, 16 * 128], F32, kind="ExternalInput").ap()
    i_slope = dram("slope", [16, 1], F32, kind="ExternalInput").ap()
    i_delta = dram("delta", [1, 8], F32, kind="ExternalInput").ap()
    i_nw = dram("normw1", [128, 16], F32, kind="ExternalInput").ap()
    i_gate = dram("gateo", [D, TPC], BF16, kind="ExternalInput").ap()
    i_woT = dram("wout1T", [D, D], F32, kind="ExternalInput").ap()
    i_eye16 = dram("eye16", [16, 16], F32, kind="ExternalInput").ap()

    o_s1 = dram("s1o", [D, TPC], BF16, kind="ExternalOutput").ap()

    with ExitStack() as ex:
        e = ex.enter_context
        ey16 = e(nc.sbuf_tensor("ey16_s", [16, 16], F32))
        slp = e(nc.sbuf_tensor("slp", [16, 1], F32))
        dlt = e(nc.sbuf_tensor("dlt", [1, 8], F32))
        nw = e(nc.sbuf_tensor("nw_s", [128, 16], F32))
        ones1 = e(nc.sbuf_tensor("ones1", [1, 128], F32))
        onesb = e(nc.sbuf_tensor("onesb", [128, 1], BF16))
        cEPS = e(nc.sbuf_tensor("cEPS", [1, 1], F32))
        Wjh = e(nc.sbuf_tensor("Wjh", [8, 16], F32))
        wflat = e(nc.sbuf_tensor("wflat", [1, 128], F32))
        wfill = e(nc.sbuf_tensor("wfill", [128, 128], F32))
        bdfill = e(nc.sbuf_tensor("bdfill", [128, 64], F32))
        segl = e(nc.sbuf_tensor("segl", [128, 2 * 2048], F32))
        kvin = e(nc.sbuf_tensor("kvin", [128, 16 * 128], F32))
        kvsc = e(nc.sbuf_tensor("kvsc", [128, 128], BF16))
        qdh = e(nc.sbuf_tensor("qdh", [128, TPC], BF16))
        aout = e(nc.sbuf_tensor("aout", [128, 16 * TPC], BF16))
        ysq = e(nc.sbuf_tensor("ysq", [128, TPC], BF16))
        rr = e(nc.sbuf_tensor("rr", [1, TPC], F32))
        gsl = e(nc.sbuf_tensor("gsl", [128, 2 * TPC], BF16))
        wsl = e(nc.sbuf_tensor("wsl", [128, 2 * 16 * 128], BF16))
        s1b = e(nc.sbuf_tensor("s1b", [128, 2 * TPC], BF16))

        psA = e(nc.psum_tensor("psA", [128, 512], F32))
        psC1 = e(nc.psum_tensor("psC1", [128, 512], F32))
        psR = e(nc.psum_tensor("psR", [1, TPC], F32))
        psB = e(nc.psum_tensor("psB", [128, TPC], F32))
        psO = e(nc.psum_tensor("psO", [128, TPC], F32))

        LD = e(nc.semaphore("LD"))
        WL = e(nc.semaphore("WL"))
        MM = e(nc.semaphore("MM"))
        SE = e(nc.semaphore("SE"))
        VE = e(nc.semaphore("VE"))
        SO = e(nc.semaphore("SO"))
        ld = wl = mm = se = ve = so = 0

        with nc.Block() as blk:
            @blk.sync
            def _(s):
                for (dst, src) in ((ey16, i_eye16), (slp, i_slope), (dlt, i_delta),
                                   (nw, i_nw)):
                    s.dma_start(out=dst[:], in_=src[:, :]).then_inc(LD, 16)
                s.dma_start(out=aout[:], in_=i_attn.rearrange("(h p) t -> p h t", p=128)).then_inc(LD, 16)
            ld += 80
            ld_p0 = ld

            @blk.vector
            def _(v):
                v.wait_ge(LD, ld_p0)
                v.memset(ones1[:], 1.0).then_inc(VE, 1)
                v.memset(onesb[:], 1.0).then_inc(VE, 1)
                v.memset(cEPS[:], 1e-6).then_inc(VE, 1)
            ve += 3
            ve_p0 = ve

            @blk.tensor
            def _(t):
                t.wait_ge(VE, ve_p0)
                t.transpose(psA[0:1, 0:16], slp[:], ey16[:]).then_inc(MM, 1)
            mm += 1
            mm_sr = mm

            @blk.vector
            def _(v):
                v.wait_ge(MM, mm_sr)
                v.tensor_copy(rr[0:1, 0:16], psA[0:1, 0:16]).then_inc(VE, 1)
            ve += 1
            ve_srow = ve

            @blk.tensor
            def _(t):
                t.wait_ge(VE, ve_srow)
                t.matmul(psA[0:8, 16:32], dlt[:], rr[0:1, 0:16],
                         start=True, stop=True).then_inc(MM, 1)
            mm += 1
            mm_w = mm

            @blk.scalar
            def _(s):
                s.wait_ge(MM, mm_w)
                s.activation(Wjh[:], psA[0:8, 16:32], AF.Exp, bias=0.0,
                             scale=-1024.0).then_inc(SE, 1)
            se += 1
            se_w = se

            @blk.gpsimd
            def _(g):
                g.wait_ge(SE, se_w)
                g.dma_start(out=wflat[0:1, :], in_=Wjh[:, :]).then_inc(WL, 16)
            wl += 16
            wl_wf = wl

            @blk.tensor
            def _(t):
                t.wait_ge(WL, wl_wf)
                t.matmul(psA[:, 32:160], ones1[:, :], wflat[0:1, :],
                         start=True, stop=True).then_inc(MM, 1)
            mm += 1
            mm_wf = mm

            @blk.vector
            def _(v):
                v.wait_ge(MM, mm_wf)
                v.tensor_copy(wfill[:], psA[:, 32:160]).then_inc(VE, 1)
                v.memset(kvin[:], 0.0).then_inc(VE, 1)
            ve += 2
            ve_wf = ve

            ve_kv = [0] * 8
            for j in range(8):
                pj = j % 2

                @blk.sync
                def _(s, j=j, pj=pj):
                    if j >= 2:
                        s.wait_ge(VE, ve_kv[j - 2])
                    s.dma_start(out=segl[:, pj * 2048:(pj + 1) * 2048],
                                in_=i_kvs[j * 128:(j + 1) * 128, :]).then_inc(LD, 16)
                ld += 16
                ld_j = ld

                v0 = ve

                @blk.vector
                def _(v, j=j, pj=pj, v0=v0, ld_j=ld_j):
                    v.wait_ge(LD, ld_j)
                    v.wait_ge(VE, ve_wf)
                    sv = segl[:, pj * 2048:(pj + 1) * 2048].rearrange(
                        "p (h q) -> p h q", h=16)
                    wv = wfill[:, j * 16:(j + 1) * 16].unsqueeze(2).broadcast_to((128, 16, 128))
                    v.tensor_tensor(sv, sv, wv, op=ALU.mult).then_inc(VE, 1)
                    v.wait_ge(VE, v0 + 1)
                    v.tensor_tensor(kvin[:], kvin[:],
                                    segl[:, pj * 2048:(pj + 1) * 2048],
                                    op=ALU.add).then_inc(VE, 1)
                ve += 2
                ve_kv[j] = ve
            ve_kvin = ve

            @blk.gpsimd
            def _(g, ve_w=ve_kvin):
                g.wait_ge(VE, ve_w)
                g.iota(rr[0:1, 16:20], pattern=[[1, 4]], base=0,
                       channel_multiplier=0,
                       allow_small_or_imprecise_dtypes=True).then_inc(WL, 16)
            wl += 16
            wl_io = wl

            @blk.tensor
            def _(t):
                t.wait_ge(WL, wl_io)
                t.matmul(psA[0:4, 256:272], rr[0:1, 16:20], rr[0:1, 0:16],
                         start=True, stop=True).then_inc(MM, 1)
            mm += 1
            mm_bd1 = mm

            @blk.scalar
            def _(s):
                s.wait_ge(MM, mm_bd1)
                s.activation(ysq[0:4, 0:16], psA[0:4, 256:272], AF.Exp,
                             bias=0.0, scale=-256.0).then_inc(SE, 1)
            se += 1
            se_bd = se

            @blk.gpsimd
            def _(g):
                g.wait_ge(SE, se_bd)
                g.dma_start(out=rr[0:1, 128:192], in_=ysq[0:4, 0:16]).then_inc(WL, 16)
            wl += 16
            wl_bd = wl

            @blk.tensor
            def _(t):
                t.wait_ge(WL, wl_bd)
                t.matmul(psA[:, 256:320], ones1[:, :], rr[0:1, 128:192],
                         start=True, stop=True).then_inc(MM, 1)
            mm += 1
            mm_bd2 = mm

            @blk.vector
            def _(v):
                v.wait_ge(MM, mm_bd2)
                v.tensor_copy(bdfill[:], psA[:, 256:320]).then_inc(VE, 1)
            ve += 1
            ve_bd = ve

            ld_qd = [0] * 16
            ve_corr = [0] * 16
            for h in range(16):
                @blk.sync
                def _(s, h=h):
                    if h >= 1:
                        s.wait_ge(VE, ve_corr[h - 1])
                    s.dma_start(out=qdh[:],
                                in_=i_qd[h * 128:(h + 1) * 128, :]).then_inc(LD, 16)
                ld += 16
                ld_qd[h] = ld

                for bi in range(4):
                    v0 = ve

                    @blk.vector
                    def _(v, h=h, bi=bi, v0=v0):
                        v.wait_ge(VE, ve_bd)
                        v.wait_ge(VE, ve_kvin)
                        if not (h == 0 and bi == 0):
                            v.wait_ge(MM, mm)
                        v.tensor_scalar(kvsc[:], kvin[:, h * 128:(h + 1) * 128],
                                        scalar1=bdfill[:, bi * 16 + h:bi * 16 + h + 1],
                                        scalar2=None, op0=ALU.mult).then_inc(VE, 1)
                    ve += 1
                    ve_sc = ve

                    @blk.tensor
                    def _(t, h=h, bi=bi, ve_sc=ve_sc, ld_h=ld_qd[h]):
                        t.wait_ge(VE, ve_sc)
                        t.wait_ge(LD, ld_h)
                        t.matmul(psC1[:, 0:256], kvsc[:],
                                 qdh[:, bi * 256:(bi + 1) * 256],
                                 start=True, stop=True).then_inc(MM, 1)
                    mm += 1
                    mm_c = mm

                    @blk.vector
                    def _(v, h=h, bi=bi, mm_c=mm_c):
                        v.wait_ge(MM, mm_c)
                        sl_ = aout[:, h * TPC + bi * 256:h * TPC + (bi + 1) * 256]
                        v.tensor_tensor(sl_, sl_, psC1[:, 0:256],
                                        op=ALU.add).then_inc(VE, 1)
                    ve += 1
                ve_corr[h] = ve
            ve_corrall = ve

            mm_sq = [0] * 16
            for pc in range(16):
                @blk.scalar
                def _(s, pc=pc):
                    s.wait_ge(VE, ve_corrall)
                    if pc >= 1:
                        s.wait_ge(MM, mm_sq[pc - 1])
                    s.activation(ysq[:], aout[:, pc * TPC:(pc + 1) * TPC], AF.Square,
                                 bias=0.0, scale=1.0).then_inc(SE, 1)
                se += 1

                @blk.tensor
                def _(t, pc=pc, se_w=se):
                    t.wait_ge(SE, se_w)
                    for k in range(2):
                        ins = t.matmul(psR[0:1, k * 512:(k + 1) * 512], onesb[:],
                                       ysq[:, k * 512:(k + 1) * 512],
                                       start=(pc == 0), stop=(pc == 15))
                    ins.then_inc(MM, 1)
                mm += 1
                mm_sq[pc] = mm
            mm_r = mm

            v0 = ve

            @blk.vector
            def _(v, v0=v0):
                v.wait_ge(MM, mm_r)
                v.tensor_scalar(rr[0:1, 0:TPC], psR[0:1, :], scalar1=1.0 / D,
                                scalar2=None, op0=ALU.mult).then_inc(VE, 1)
            ve += 1
            ve_rr = ve

            @blk.scalar
            def _(s, se0=se):
                s.wait_ge(VE, ve_rr)
                s.activation(rr[0:1, 0:TPC], rr[0:1, 0:TPC], AF.Ln,
                             bias=cEPS[0:1, 0:1], scale=1.0).then_inc(SE, 1)
                s.wait_ge(SE, se0 + 1)
                s.activation(rr[0:1, 0:TPC], rr[0:1, 0:TPC], AF.Exp,
                             bias=0.0, scale=-0.5).then_inc(SE, 1)
            se += 2
            se_rstd = se

            @blk.tensor
            def _(t):
                t.wait_ge(SE, se_rstd)
                ins = None
                for k in range(2):
                    ins = t.matmul(psB[:, k * 512:(k + 1) * 512], ones1[:, :],
                                   rr[0:1, k * 512:(k + 1) * 512],
                                   start=True, stop=True)
                ins.then_inc(MM, 1)
            mm += 1
            mm_bc = mm

            ld_g = [0] * 16
            ve_yn = [0] * 16
            for pc in range(16):
                @blk.sync
                def _(s, pc=pc):
                    if pc >= 2:
                        s.wait_ge(VE, ve_yn[pc - 2])
                    s.dma_start(out=gsl[:, (pc % 2) * TPC:(pc % 2 + 1) * TPC],
                                in_=i_gate[pc * 128:(pc + 1) * 128, :]).then_inc(LD, 16)
                ld += 16
                ld_g[pc] = ld

                v0 = ve

                @blk.vector
                def _(v, pc=pc, v0=v0, ld_w=ld_g[pc]):
                    v.wait_ge(MM, mm_bc)
                    v.wait_ge(LD, ld_w)
                    sl_ = aout[:, pc * TPC:(pc + 1) * TPC]
                    v.scalar_tensor_tensor(sl_, sl_, nw[:, pc:pc + 1], psB[:],
                                           op0=ALU.mult, op1=ALU.mult).then_inc(VE, 1)
                    v.wait_ge(VE, v0 + 1)
                    v.tensor_tensor(sl_, sl_, gsl[:, (pc % 2) * TPC:(pc % 2 + 1) * TPC],
                                    op=ALU.mult).then_inc(VE, 1)
                ve += 2
                ve_yn[pc] = ve
            ve_ynall = ve

            so_s1 = [0] * 16
            se_oc = [0] * 16
            for oc in range(16):
                p = oc % 2

                @blk.gpsimd
                def _(g, oc=oc, p=p):
                    if oc >= 2:
                        g.wait_ge(MM, mm)
                    src = i_woT[:, oc * 128:(oc + 1) * 128]
                    g.dma_start(out=wsl[:, p * 2048:(p + 1) * 2048],
                                in_=src.rearrange("(kc q) m -> q kc m", q=128)).then_inc(WL, 16)
                wl += 16
                wl_oc = wl

                @blk.tensor
                def _(t, oc=oc, p=p, wl_oc=wl_oc):
                    t.wait_ge(WL, wl_oc)
                    t.wait_ge(VE, ve_ynall)
                    if oc >= 1:
                        t.wait_ge(SE, se_oc[oc - 1])
                    ins = None
                    for kc in range(16):
                        for ks in range(2):
                            ins = t.matmul(psO[:, ks * 512:(ks + 1) * 512],
                                           wsl[:, p * 2048 + kc * 128:p * 2048 + (kc + 1) * 128],
                                           aout[:, kc * TPC + ks * 512:kc * TPC + ks * 512 + 512],
                                           start=(kc == 0), stop=(kc == 15))
                    ins.then_inc(MM, 1)
                mm += 1
                mm_oc = mm

                @blk.scalar
                def _(s, oc=oc, p=p, mm_w=mm_oc, so_w=so_s1[oc - 2] if oc >= 2 else 0):
                    s.wait_ge(MM, mm_w)
                    if so_w:
                        s.wait_ge(SO, so_w)
                    s.activation(s1b[:, p * TPC:(p + 1) * TPC], psO[:], AF.Copy,
                                 bias=0.0, scale=1.0).then_inc(SE, 1)
                se += 1
                se_oc[oc] = se

                @blk.sync
                def _(s, oc=oc, p=p, se_w=se):
                    s.wait_ge(SE, se_w)
                    s.dma_start(out=o_s1[oc * 128:(oc + 1) * 128, :],
                                in_=s1b[:, p * TPC:(p + 1) * TPC]).then_inc(SO, 16)
                so += 16
                so_s1[oc] = so

            @blk.sync
            def _(s):
                s.wait_ge(SO, so)
    return nc



SEQ = 8192


def build_F():
    nc = bass.Bass("TRN2", target_bir_lowering=False)
    dram = nc.dram_tensor
    i_q = dram("q2h", [256, SEQ], BF16, kind="ExternalInput").ap()
    i_k = dram("kshf", [128, SEQ], BF16, kind="ExternalInput").ap()
    i_v = dram("vshf", [SEQ, 128], BF16, kind="ExternalInput").ap()
    i_msk = dram("cmask", [128, 4 * 512], BF16, kind="ExternalInput").ap()
    o_y = dram("y2o", [256, SEQ], BF16, kind="ExternalOutput").ap()

    with ExitStack() as ex:
        e = ex.enter_context
        qf = e(nc.sbuf_tensor("qf", [128, 2 * SEQ], BF16))
        kf = e(nc.sbuf_tensor("kf", [128, SEQ], BF16))
        vt = e(nc.sbuf_tensor("vt", [128, SEQ], BF16))
        msk = e(nc.sbuf_tensor("msk", [128, 2048], BF16))
        qsq = e(nc.sbuf_tensor("qsq", [128, SEQ], BF16))
        row = e(nc.sbuf_tensor("row", [1, SEQ], F32))
        srow = e(nc.sbuf_tensor("srow", [1, 512], F32))
        sc1 = e(nc.sbuf_tensor("sc1", [1, 8], F32))
        bias = e(nc.sbuf_tensor("biasc", [128, 1], F32))
        onesb = e(nc.sbuf_tensor("onesb", [128, 1], BF16))
        ones1 = e(nc.sbuf_tensor("ones1", [1, 128], F32))
        pb = e(nc.sbuf_tensor("pb", [128, 2 * 512], BF16))
        ysb = e(nc.sbuf_tensor("ysb", [128, 512], F32))
        y2 = e(nc.sbuf_tensor("y2", [128, 2 * SEQ], BF16))

        psS0 = e(nc.psum_tensor("psS0", [128, 512], F32))
        psS1 = e(nc.psum_tensor("psS1", [128, 512], F32))
        psO = e(nc.psum_tensor("psO", [128, 512], F32))
        psSm = e(nc.psum_tensor("psSm", [1, 512], F32))
        psB = e(nc.psum_tensor("psB", [128, 512], F32))

        LD = e(nc.semaphore("LD"))
        MM = e(nc.semaphore("MM"))
        SE = e(nc.semaphore("SE"))
        VE = e(nc.semaphore("VE"))
        SO = e(nc.semaphore("SO"))
        ld = mm = se = ve = so = 0
        psS = [psS0, psS1]

        with nc.Block() as blk:
            @blk.sync
            def _(s):
                s.dma_start(out=qf[:], in_=i_q.rearrange("(h p) t -> p h t", p=128)).then_inc(LD, 16)
                s.dma_start(out=kf[:], in_=i_k[:, :]).then_inc(LD, 16)
                s.dma_start(out=vt[:], in_=i_v.rearrange("(j p) e -> p j e", p=128)).then_inc(LD, 16)
                s.dma_start(out=msk[:], in_=i_msk[:, :]).then_inc(LD, 16)
            ld += 64
            ld_p0 = ld

            @blk.vector
            def _(v):
                v.wait_ge(LD, ld_p0)
                v.memset(onesb[:], 1.0).then_inc(VE, 1)
                v.memset(ones1[:], 1.0).then_inc(VE, 1)
            ve += 2
            ve_p0 = ve

            se_b = se
            for (src, roff) in ((0, 0), (1, 1), (2, 2)):
                @blk.scalar
                def _(s, src=src):
                    s.wait_ge(VE, ve_p0)
                    if src < 2:
                        s.activation(qsq[:], qf[:, src * SEQ:(src + 1) * SEQ],
                                     AF.Square, bias=0.0, scale=1.0).then_inc(SE, 1)
                    else:
                        s.activation(qsq[:], kf[:], AF.Square, bias=0.0,
                                     scale=1.0).then_inc(SE, 1)
                se += 1
                se_sq = se

                mm0 = mm

                @blk.tensor
                def _(t, src=src, se_sq=se_sq, mm0=mm0, v0=ve):
                    t.wait_ge(SE, se_sq)
                    t.wait_ge(VE, v0)
                    for j in range(16):
                        if j > 0:
                            t.wait_ge(VE, v0 + j)
                        t.matmul(psSm[0:1, :], onesb[:],
                                 qsq[:, j * 512:(j + 1) * 512],
                                 start=True, stop=True).then_inc(MM, 1)
                mm += 16

                @blk.vector
                def _(v, src=src, mm0=mm0):
                    for j in range(16):
                        v.wait_ge(MM, mm0 + j + 1)
                        v.tensor_copy(row[0:1, j * 512:(j + 1) * 512],
                                      psSm[0:1, :]).then_inc(VE, 1)
                    v.wait_ge(VE, ve + 16)
                    v.reduce_max(sc1[0:1, src:src + 1], row[0:1, :],
                                 axis=AX.X).then_inc(VE, 1)
                ve += 17
            ve_bnd = ve

            v0 = ve

            @blk.vector
            def _(v, v0=v0):
                v.wait_ge(VE, ve_bnd)
                v.tensor_tensor(sc1[0:1, 0:1], sc1[0:1, 0:1], sc1[0:1, 1:2],
                                op=ALU.max).then_inc(VE, 1)
                v.wait_ge(VE, v0 + 1)
                v.tensor_tensor(sc1[0:1, 3:4], sc1[0:1, 0:1], sc1[0:1, 2:3],
                                op=ALU.mult).then_inc(VE, 1)
            ve += 2
            ve_b2 = ve

            @blk.scalar
            def _(s, se0=se):
                s.wait_ge(VE, ve_b2)
                s.activation(sc1[0:1, 4:5], sc1[0:1, 3:4], AF.Ln, bias=0.0,
                             scale=1.0).then_inc(SE, 1)
                s.wait_ge(SE, se0 + 1)
                s.activation(sc1[0:1, 5:6], sc1[0:1, 4:5], AF.Exp, bias=0.0,
                             scale=0.5).then_inc(SE, 1)
            se += 2
            se_b = se

            v0 = ve

            @blk.vector
            def _(v, v0=v0):
                v.wait_ge(SE, se_b)
                v.tensor_scalar(sc1[0:1, 6:7], sc1[0:1, 5:6],
                                scalar1=-1.0 / (128.0 ** 0.5), scalar2=None,
                                op0=ALU.mult).then_inc(VE, 1)
            ve += 1
            ve_bv = ve

            @blk.tensor
            def _(t):
                t.wait_ge(VE, ve_bv)
                t.matmul(psB[:, 0:1], ones1[:, :], sc1[0:1, 6:7],
                         start=True, stop=True).then_inc(MM, 1)
            mm += 1
            mm_bb = mm

            @blk.vector
            def _(v):
                v.wait_ge(MM, mm_bb)
                v.tensor_copy(bias[:], psB[:, 0:1]).then_inc(VE, 1)
            ve += 1
            ve_bias = ve

            SCALE = 1.0 / (128.0 ** 0.5)
            se_exp = {}
            ve_msk = {}
            ve_ep = 0
            for h in range(2):
                for g in range(16):
                    nj = 4 * g + 4
                    mm_qk = {}
                    mm_pv_last = 0
                    for j in range(nj):
                        pp = j % 2

                        @blk.tensor
                        def _(t, h=h, g=g, j=j, pp=pp):
                            t.wait_ge(VE, ve_bias)
                            key = (h, g, j - 2)
                            if key in se_exp:
                                t.wait_ge(SE, se_exp[key])
                            elif j < 2 and (h, g) != (0, 0):
                                t.wait_ge(SE, se)
                            t.matmul(psS[pp][:], kf[:, j * 128:(j + 1) * 128],
                                     qf[:, h * SEQ + g * 512:h * SEQ + (g + 1) * 512],
                                     start=True, stop=True).then_inc(MM, 1)
                        mm += 1
                        mm_qk[j] = mm

                        @blk.scalar
                        def _(s, h=h, g=g, j=j, pp=pp, mm_w=mm, ve_w=ve_ep):
                            s.wait_ge(MM, mm_w)
                            if j >= 2:
                                pass
                            s.activation(pb[:, pp * 512:(pp + 1) * 512],
                                         psS[pp][:], AF.Exp, bias=bias[:, 0:1],
                                         scale=SCALE).then_inc(SE, 1)
                        se += 1
                        se_exp[(h, g, j)] = se

                        if j >= 4 * g:
                            r = j - 4 * g

                            @blk.vector
                            def _(v, r=r, pp=pp, se_w=se):
                                v.wait_ge(SE, se_w)
                                v.tensor_tensor(pb[:, pp * 512:(pp + 1) * 512],
                                                pb[:, pp * 512:(pp + 1) * 512],
                                                msk[:, r * 512:(r + 1) * 512],
                                                op=ALU.mult).then_inc(VE, 1)
                            ve += 1
                            ve_msk[(h, g, j)] = ve

                        @blk.tensor
                        def _(t, h=h, g=g, j=j, pp=pp, nj=nj, se_w=se):
                            if (h, g, j) in ve_msk:
                                t.wait_ge(VE, ve_msk[(h, g, j)])
                            else:
                                t.wait_ge(SE, se_w)
                            t.matmul(psO[:], vt[:, j * 128:(j + 1) * 128],
                                     pb[:, pp * 512:(pp + 1) * 512],
                                     start=(j == 0), stop=(j == nj - 1))
                            ins = t.matmul(psSm[0:1, :], onesb[:],
                                           pb[:, pp * 512:(pp + 1) * 512],
                                           start=(j == 0), stop=(j == nj - 1))
                            ins.then_inc(MM, 1)
                        mm += 1
                        mm_pv_last = mm

                    v0 = ve

                    @blk.vector
                    def _(v, h=h, g=g, v0=v0, mm_w=mm_pv_last):
                        v.wait_ge(MM, mm_w)
                        v.tensor_copy(srow[0:1, :], psSm[0:1, :]).then_inc(VE, 1)
                        v.wait_ge(VE, v0 + 1)
                        v.reciprocal(srow[0:1, :], srow[0:1, :]).then_inc(VE, 1)
                        v.wait_ge(VE, v0 + 2)
                        v.tensor_copy(ysb[:], psO[:]).then_inc(VE, 1)
                    ve += 3
                    ve_ep1 = ve

                    @blk.tensor
                    def _(t, ve_ep1=ve_ep1):
                        t.wait_ge(VE, ve_ep1)
                        t.matmul(psB[:, :], ones1[:, :], srow[0:1, :],
                                 start=True, stop=True).then_inc(MM, 1)
                    mm += 1
                    mm_b2 = mm

                    @blk.vector
                    def _(v, h=h, g=g, mm_b2=mm_b2):
                        v.wait_ge(MM, mm_b2)
                        v.tensor_tensor(y2[:, h * SEQ + g * 512:h * SEQ + (g + 1) * 512],
                                        ysb[:], psB[:, :], op=ALU.mult).then_inc(VE, 1)
                    ve += 1
                    ve_ep = ve

            @blk.sync
            def _(s, ve_w=ve):
                s.wait_ge(VE, ve_w)
                s.dma_start(out=o_y[0:128, :], in_=y2[:, 0:SEQ]).then_inc(SO, 16)
                s.dma_start(out=o_y[128:256, :], in_=y2[:, SEQ:2 * SEQ]).then_inc(SO, 16)
                s.wait_ge(SO, 32)
            so += 32
    return nc



def build_G():
    nc = bass.Bass("TRN2", target_bir_lowering=False)
    dram = nc.dram_tensor
    i_x1 = dram("x1o", [D, TPC], F32, kind="ExternalInput").ap()
    i_s1 = dram("s1o", [D, TPC], BF16, kind="ExternalInput").ap()
    i_y2 = dram("y2sl", [D, TPC], BF16, kind="ExternalInput").ap()
    i_eye128f = dram("eye128f", [128, 128], F32, kind="ExternalInput").ap()
    o_out = dram("outo", [TPC, D], F32, kind="ExternalOutput").ap()

    with ExitStack() as ex:
        e = ex.enter_context
        ey = e(nc.sbuf_tensor("ey_g", [128, 128], F32))
        x1s = e(nc.sbuf_tensor("x1s", [128, 2 * TPC], F32))
        s1s = e(nc.sbuf_tensor("s1s", [128, 2 * TPC], BF16))
        y2s = e(nc.sbuf_tensor("y2s", [128, 2 * TPC], BF16))
        g1s = e(nc.sbuf_tensor("g1s", [128, TPC], BF16))
        of = e(nc.sbuf_tensor("of", [128, 16 * TPC], F32))
        ot = e(nc.sbuf_tensor("ot", [128, 2 * D], F32))

        psT = e(nc.psum_tensor("psT_g", [128, 128], F32))

        LD = e(nc.semaphore("LD"))
        MM = e(nc.semaphore("MM"))
        SE = e(nc.semaphore("SE"))
        VE = e(nc.semaphore("VE"))
        SO = e(nc.semaphore("SO"))
        ld = mm = se = ve = so = 0

        with nc.Block() as blk:
            @blk.sync
            def _(s):
                s.dma_start(out=ey[:], in_=i_eye128f[:, :]).then_inc(LD, 16)
            ld += 16
            ld_ey = ld

            ve_kc = [0] * 16
            for kc in range(16):
                p = kc % 2

                @blk.sync
                def _(s, kc=kc, p=p):
                    if kc >= 2:
                        s.wait_ge(VE, ve_kc[kc - 2])
                    s.dma_start(out=x1s[:, p * TPC:(p + 1) * TPC],
                                in_=i_x1[kc * 128:(kc + 1) * 128, :]).then_inc(LD, 16)
                    s.dma_start(out=s1s[:, p * TPC:(p + 1) * TPC],
                                in_=i_s1[kc * 128:(kc + 1) * 128, :]).then_inc(LD, 16)
                    s.dma_start(out=y2s[:, p * TPC:(p + 1) * TPC],
                                in_=i_y2[kc * 128:(kc + 1) * 128, :]).then_inc(LD, 16)
                ld += 48
                ld_kc = ld

                @blk.scalar
                def _(s, kc=kc, p=p, ld_kc=ld_kc, ve_w=ve):
                    s.wait_ge(LD, ld_kc)
                    if kc >= 1:
                        s.wait_ge(VE, ve_w)
                    s.activation(g1s[:], y2s[:, p * TPC:(p + 1) * TPC], AF.Silu,
                                 bias=0.0, scale=1.0).then_inc(SE, 1)
                se += 1
                se_g = se

                v0 = ve

                @blk.vector
                def _(v, kc=kc, p=p, v0=v0, se_g=se_g):
                    v.wait_ge(SE, se_g)
                    v.tensor_tensor(of[:, kc * TPC:(kc + 1) * TPC],
                                    s1s[:, p * TPC:(p + 1) * TPC], g1s[:],
                                    op=ALU.mult).then_inc(VE, 1)
                    v.wait_ge(VE, v0 + 1)
                    v.tensor_tensor(of[:, kc * TPC:(kc + 1) * TPC],
                                    of[:, kc * TPC:(kc + 1) * TPC],
                                    x1s[:, p * TPC:(p + 1) * TPC],
                                    op=ALU.add).then_inc(VE, 1)
                ve += 2
                ve_kc[kc] = ve
            ve_of = ve

            so_t = {}
            for tt in range(8):
                mm0 = mm
                v0 = ve

                @blk.tensor
                def _(t, tt=tt, mm0=mm0, v0=v0):
                    t.wait_ge(VE, ve_of)
                    if tt > 0:
                        t.wait_ge(VE, v0)
                    for kc in range(16):
                        if kc > 0:
                            t.wait_ge(VE, v0 + kc)
                        t.transpose(psT[:], of[:, kc * TPC + tt * 128:kc * TPC + (tt + 1) * 128],
                                    ey[:]).then_inc(MM, 1)
                mm += 16

                @blk.vector
                def _(v, tt=tt, mm0=mm0):
                    if tt >= 2:
                        v.wait_ge(SO, so_t[tt - 2])
                    for kc in range(16):
                        v.wait_ge(MM, mm0 + kc + 1)
                        v.tensor_copy(ot[:, (tt % 2) * D + kc * 128:(tt % 2) * D + (kc + 1) * 128],
                                      psT[:]).then_inc(VE, 1)
                ve += 16

                @blk.sync
                def _(s, tt=tt, ve_w=ve):
                    s.wait_ge(VE, ve_w)
                    s.dma_start(out=o_out[tt * 128:(tt + 1) * 128, :],
                                in_=ot[:, (tt % 2) * D:(tt % 2 + 1) * D]).then_inc(SO, 16)
                so += 16
                so_t[tt] = so

            @blk.sync
            def _(s):
                s.wait_ge(SO, so)
    return nc



def host_D(inputs, c_outs):
    slope = np.asarray(inputs["L1_attn_slope_rate"], np.float32)
    dmask = np.full((128, 512), 1e30, np.float32)
    pp = np.arange(128)
    mm_ = np.arange(256)
    for half in range(2):
        kk = half * 128 + pp
        val = mm_[None, :] - kk[:, None]
        dmask[:, half * 256:(half + 1) * 256] = np.where(val >= 0, val, 1e30)
    armask = ((np.arange(TPC) % 256) + 1).astype(np.float32).reshape(1, TPC)
    kmask = np.zeros((128, 2), np.float32)
    for par in range(2):
        kmask[:, par] = 255 - (par * 128 + np.arange(128))
    wqkvT = f32(np.asarray(inputs["L1_attn_w_qkv"]).T)
    wgateT = f32(np.asarray(inputs["L1_attn_w_gate"]).T)
    wqkv2T = f32(np.asarray(inputs["L1_ffn_W_qkv"]).T)
    wg2T = f32(np.asarray(inputs["L1_ffn_W_gate"]).T)
    maps = []
    for c in range(NCORES):
        maps.append({
            "x1o": c_outs[c]["x1o"],
            "wqkvT": wqkvT, "wgateT": wgateT, "wqkv2T": wqkv2T, "wg2T": wg2T,
            "slope": f32(slope.reshape(16, 1)),
            "dmask": f32(dmask), "armask": f32(armask), "kmask": f32(kmask),
            "eye128b": bf16(np.eye(128)), "eye16": f32(np.eye(16)),
        })
    return maps


def host_E(inputs, d_outs):
    slope = np.asarray(inputs["L1_attn_slope_rate"], np.float32)
    kvstack = np.concatenate([f32(d_outs[c]["kvsego"]) for c in range(NCORES)], axis=0)
    nw1 = f32(np.asarray(inputs["L1_attn_norm_w"]).reshape(16, 128).T)
    woT = f32(np.asarray(inputs["L1_attn_w_out"]).T)
    maps = []
    for c in range(NCORES):
        delta = np.full((1, 8), 1e20, np.float32)
        for j in range(c):
            delta[0, j] = (c - 1 - j)
        maps.append({
            "attno": d_outs[c]["attno"], "qdo": d_outs[c]["qdo"],
            "kvstack": kvstack, "slope": f32(slope.reshape(16, 1)),
            "delta": f32(delta), "normw1": nw1,
            "gateo": d_outs[c]["gateo"], "wout1T": woT,
            "eye16": f32(np.eye(16)),
        })
    return maps


def host_F(d_outs):
    qfull = np.concatenate([np.asarray(d_outs[c]["q2o"]) for c in range(NCORES)],
                           axis=1)
    kshf = np.concatenate([np.asarray(d_outs[c]["ksho"]) for c in range(NCORES)],
                          axis=1)
    vshf = np.concatenate([np.asarray(d_outs[c]["vsho"]) for c in range(NCORES)],
                          axis=0)
    cmask = np.zeros((128, 4 * 512), np.float32)
    kk = np.arange(128)[:, None]
    mm_ = np.arange(512)[None, :]
    for r in range(4):
        cmask[:, r * 512:(r + 1) * 512] = (mm_ >= r * 128 + kk)
    maps = []
    for c in range(NCORES):
        maps.append({
            "q2h": bf16(np.ascontiguousarray(qfull[c * 256:(c + 1) * 256])),
            "kshf": bf16(kshf), "vshf": bf16(vshf), "cmask": bf16(cmask),
        })
    return maps


def host_G(c_outs, e_outs, f_res):
    y2full = np.concatenate([np.asarray(f_res[c]["y2o"]) for c in range(NCORES)],
                            axis=0)
    maps = []
    for c in range(NCORES):
        maps.append({
            "x1o": c_outs[c]["x1o"], "s1o": e_outs[c]["s1o"],
            "y2sl": bf16(np.ascontiguousarray(y2full[:, c * TPC:(c + 1) * TPC])),
            "eye128f": f32(np.eye(128)),
        })
    return maps



def kernel(**inputs):
    consts = mk_consts()
    consts2 = mk_consts_b2(inputs)
    u_fm_full = f32(np.asarray(inputs["L0_attn_u"])[0].T)

    cores = list(range(NCORES))
    a_res = run_bass_kernel_spmd(build_A(), host_A(inputs), core_ids=cores).results
    b1_res = run_bass_kernel_spmd(build_B1(), host_B1(a_res, consts),
                                  core_ids=cores).results
    b2_res = run_bass_kernel_spmd(build_B2(),
                                  host_B2(inputs, a_res, b1_res, consts, consts2),
                                  core_ids=cores).results
    c_res = run_bass_kernel_spmd(build_C(),
                                 host_C(inputs, a_res, b2_res, consts, u_fm_full),
                                 core_ids=cores).results
    d_res = run_bass_kernel_spmd(build_D(), host_D(inputs, c_res),
                                 core_ids=cores).results
    e_res = run_bass_kernel_spmd(build_E(), host_E(inputs, d_res),
                                 core_ids=cores).results
    f_res = run_bass_kernel_spmd(build_F(), host_F(d_res), core_ids=cores).results
    g_res = run_bass_kernel_spmd(build_G(), host_G(c_res, e_res, f_res),
                                 core_ids=cores).results
    out = np.concatenate([np.asarray(g_res[c]["outo"], np.float32)
                          for c in range(NCORES)], axis=0)
    return out.reshape(1, 8192, 2048)


# version 04
# speedup vs baseline: 1.2603x; 1.2603x over previous
import numpy as np
import ml_dtypes
from contextlib import ExitStack

import concourse.bass as bass
import concourse.mybir as mybir
from concourse.bass_utils import run_bass_kernel_spmd

F32 = mybir.dt.float32
BF16 = mybir.dt.bfloat16
U32 = mybir.dt.uint32
AF = mybir.ActivationFunctionType
ALU = mybir.AluOpType
AX = mybir.AxisListType
BF16NP = ml_dtypes.bfloat16

NCORES = 8
TPC = 1024
THALO = 1152
D = 2048
INNER = 4096
CHUNK = 64
DINP = 8512


def preload_set6(s):
    s.add_instruction(mybir.InstLoadActFuncSet(
        name=s.bass.get_next_instruction_name(),
        act_func_set_id=6, ins=[], outs=[]))


def f32(x):
    return np.ascontiguousarray(np.asarray(x), dtype=np.float32)


def bf16(x):
    return np.ascontiguousarray(np.asarray(x), dtype=BF16NP)



def build_A():
    nc = bass.Bass("TRN2", target_bir_lowering=False)
    dram = nc.dram_tensor
    u = dram("u", [THALO, D], F32, kind="ExternalInput").ap()
    w_inT = dram("w_inT", [D, DINP], F32, kind="ExternalInput").ap()
    convw = dram("convw", [128, 136], F32, kind="ExternalInput").ap()
    convb = dram("convb", [128, 34], F32, kind="ExternalInput").ap()
    dtb = dram("dtb", [64, 1], F32, kind="ExternalInput").ap()
    alog = dram("alog", [64, 1], F32, kind="ExternalInput").ap()
    ident = dram("ident", [128, 128], BF16, kind="ExternalInput").ap()

    o_xnfm = dram("xn_fm", [D, TPC], BF16, kind="ExternalOutput").ap()
    o_xntm = dram("xn_tm", [TPC, D], BF16, kind="ExternalOutput").ap()
    o_siluz = dram("siluz", [INNER, TPC], BF16, kind="ExternalOutput").ap()
    o_xconv = dram("xconv", [INNER, TPC], BF16, kind="ExternalOutput").ap()
    o_B = dram("Bo", [128, TPC], BF16, kind="ExternalOutput").ap()
    o_C = dram("Co", [128, TPC], BF16, kind="ExternalOutput").ap()
    o_dt = dram("dto", [64, TPC], F32, kind="ExternalOutput").ap()
    o_Adt = dram("Adto", [64, TPC], F32, kind="ExternalOutput").ap()
    o_Acs = dram("Acso", [64, TPC], F32, kind="ExternalOutput").ap()

    NOC = 67

    with ExitStack() as ex:
        e = ex.enter_context
        cw = e(nc.sbuf_tensor("cw", [128, 136], F32))
        cb = e(nc.sbuf_tensor("cb", [128, 34], F32))
        db = e(nc.sbuf_tensor("db", [64, 1], F32))
        al = e(nc.sbuf_tensor("al", [64, 1], F32))
        negA = e(nc.sbuf_tensor("negA", [64, 1], F32))
        idt = e(nc.sbuf_tensor("idt", [128, 128], BF16))
        ut = e(nc.sbuf_tensor("ut", [128, 2 * D], F32))
        usq = e(nc.sbuf_tensor("usq", [128, D], BF16))
        stat_sm = e(nc.sbuf_tensor("stat_sm", [128, 16], F32))
        stat_ss = e(nc.sbuf_tensor("stat_ss", [128, 16], F32))
        m_col = e(nc.sbuf_tensor("m_col", [128, 16], F32))
        v_col = e(nc.sbuf_tensor("v_col", [128, 16], F32))
        r_col = e(nc.sbuf_tensor("r_col", [128, 16], F32))
        xntm = e(nc.sbuf_tensor("xntm", [128, 2 * D], BF16))
        xn = e(nc.sbuf_tensor("xn", [128, 16 * THALO], BF16))
        wsl = e(nc.sbuf_tensor("wsl", [128, 2 * 16 * 128], BF16))
        zsb = e(nc.sbuf_tensor("zsb", [128, 2 * TPC], BF16))
        xbc = e(nc.sbuf_tensor("xbc", [128, 2 * THALO], BF16))
        dg = e(nc.sbuf_tensor("dg", [128, 2 * 4 * 128], BF16))
        xcv = e(nc.sbuf_tensor("xcv", [128, 2 * TPC], BF16))
        tdt = e(nc.sbuf_tensor("tdt", [64, THALO], F32))
        dtf = e(nc.sbuf_tensor("dtf", [64, THALO], F32))
        adt = e(nc.sbuf_tensor("adt", [64, TPC], F32))
        acs = e(nc.sbuf_tensor("acs", [64, TPC], F32))
        ones64 = e(nc.sbuf_tensor("ones64", [64, 64], F32))
        cEPS = e(nc.sbuf_tensor("cEPS", [128, 1], F32))
        cONE = e(nc.sbuf_tensor("cONE", [128, 1], F32))

        psM0 = e(nc.psum_tensor("psM0", [128, THALO], F32))
        psM1 = e(nc.psum_tensor("psM1", [128, THALO], F32))
        psT_guard = nc.psum_tensor("psT", [128, 128], BF16)
        psT = psT_guard.__enter__()

        LD = e(nc.semaphore("LD"))
        WL = e(nc.semaphore("WL"))
        MM = e(nc.semaphore("MM"))
        SE = e(nc.semaphore("SE"))
        VE = e(nc.semaphore("VE"))
        SO = e(nc.semaphore("SO"))

        ld = wl = mm = se = ve = so = 0
        psum = [psM0, psM1]

        with nc.Block() as blk:
            @blk.sync
            def _(s):
                s.dma_start(out=cw[:], in_=convw[:, :]).then_inc(LD, 16)
                s.dma_start(out=cb[:], in_=convb[:, :]).then_inc(LD, 16)
                s.dma_start(out=db[:], in_=dtb[:, :]).then_inc(LD, 16)
                s.dma_start(out=al[:], in_=alog[:, :]).then_inc(LD, 16)
                s.dma_start(out=idt[:], in_=ident[:, :]).then_inc(LD, 16)
            ld += 80
            ld_p0 = ld

            @blk.scalar
            def _(s):
                preload_set6(s)
                s.wait_ge(LD, ld_p0)
                s.activation(negA[:], al[:], AF.Exp, bias=0.0, scale=1.0).then_inc(SE, 1)
            se += 1
            se_negA = se

            @blk.vector
            def _(v):
                v.wait_ge(SE, se_negA)
                v.tensor_scalar(negA[:], negA[:], scalar1=-1.0, scalar2=None,
                                op0=ALU.mult).then_inc(VE, 1)
                v.memset(ones64[:], 1.0).then_inc(VE, 1)
                v.memset(cEPS[:], 1e-5).then_inc(VE, 1)
                v.memset(cONE[:], 1.0).then_inc(VE, 1)
            ve += 4
            ve_p0 = ve

            se_sq = [0] * 9
            ve_tile = [0] * 9
            mm_t = [0] * 9
            so_tm = [0] * 9
            for tt in range(9):
                p = tt % 2

                @blk.sync
                def _(s, tt=tt, p=p):
                    if tt >= 2:
                        s.wait_ge(VE, ve_tile[tt - 2])
                    s.dma_start(out=ut[:, p * D:(p + 1) * D],
                                in_=u[tt * 128:(tt + 1) * 128, :]).then_inc(LD, 16)
                ld += 16
                ld_t = ld

                @blk.scalar
                def _(s, tt=tt, p=p, ld_t=ld_t):
                    s.wait_ge(LD, ld_t)
                    s.activation(usq[:], ut[:, p * D:(p + 1) * D], AF.Square,
                                 bias=0.0, scale=1.0,
                                 accum_out=stat_ss[:, tt:tt + 1]).then_inc(SE, 1)
                se += 1
                se_sq[tt] = se

                v0 = ve

                @blk.vector
                def _(v, tt=tt, p=p, ld_t=ld_t, v0=v0):
                    v.wait_ge(LD, ld_t)
                    uv = ut[:, p * D:(p + 1) * D]
                    v.reduce_sum(stat_sm[:, tt:tt + 1], uv, axis=AX.X).then_inc(VE, 1)
                    v.wait_ge(VE, v0 + 1)
                    v.tensor_scalar(m_col[:, tt:tt + 1], stat_sm[:, tt:tt + 1],
                                    scalar1=1.0 / D, scalar2=None,
                                    op0=ALU.mult).then_inc(VE, 1)
                    v.wait_ge(SE, se_sq[tt])
                    v.tensor_scalar(v_col[:, tt:tt + 1], stat_ss[:, tt:tt + 1],
                                    scalar1=1.0 / D, scalar2=None,
                                    op0=ALU.mult).then_inc(VE, 1)
                    v.wait_ge(VE, v0 + 3)
                    v.tensor_tensor(r_col[:, tt:tt + 1], m_col[:, tt:tt + 1],
                                    m_col[:, tt:tt + 1], op=ALU.mult).then_inc(VE, 1)
                    v.wait_ge(VE, v0 + 4)
                    v.tensor_tensor(v_col[:, tt:tt + 1], v_col[:, tt:tt + 1],
                                    r_col[:, tt:tt + 1], op=ALU.subtract).then_inc(VE, 1)
                ve += 5
                ve_stats = ve

                @blk.scalar
                def _(s, tt=tt, se0=se):
                    s.wait_ge(VE, ve_stats)
                    s.wait_ge(VE, ve_p0)
                    s.activation(r_col[:, tt:tt + 1], v_col[:, tt:tt + 1], AF.Ln,
                                 bias=cEPS[:, 0:1], scale=1.0).then_inc(SE, 1)
                    s.wait_ge(SE, se0 + 1)
                    s.activation(r_col[:, tt:tt + 1], r_col[:, tt:tt + 1], AF.Exp,
                                 bias=0.0, scale=-0.5).then_inc(SE, 1)
                se += 2
                se_rstd = se

                @blk.vector
                def _(v, tt=tt, p=p):
                    v.wait_ge(SE, se_rstd)
                    if tt >= 2:
                        v.wait_ge(SO, so_tm[tt - 2])
                        v.wait_ge(MM, mm_t[tt - 2])
                    v.tensor_scalar(xntm[:, p * D:(p + 1) * D], ut[:, p * D:(p + 1) * D],
                                    scalar1=m_col[:, tt:tt + 1], scalar2=r_col[:, tt:tt + 1],
                                    op0=ALU.subtract, op1=ALU.mult).then_inc(VE, 1)
                ve += 1
                ve_xntm = ve

                @blk.sync
                def _(s, tt=tt, p=p):
                    s.wait_ge(VE, ve_xntm)
                    if tt >= 1:
                        s.dma_start(out=o_xntm[(tt - 1) * 128:tt * 128, :],
                                    in_=xntm[:, p * D:(p + 1) * D]).then_inc(SO, 16)
                if tt >= 1:
                    so += 16
                so_tm[tt] = so

                v0 = ve

                @blk.tensor
                def _(t, tt=tt, p=p, v0=v0):
                    t.wait_ge(VE, ve_xntm)
                    for kc in range(16):
                        if kc > 0:
                            t.wait_ge(VE, v0 + kc)
                        t.transpose(psT[:], xntm[:, p * D + kc * 128:p * D + (kc + 1) * 128],
                                    idt[:]).then_inc(MM, 1)
                mm += 16
                mm_t[tt] = mm

                @blk.vector
                def _(v, tt=tt, mm0=mm - 16):
                    for kc in range(16):
                        v.wait_ge(MM, mm0 + kc + 1)
                        v.tensor_copy(xn[:, kc * THALO + tt * 128:kc * THALO + (tt + 1) * 128],
                                      psT[:]).then_inc(VE, 1)
                ve += 16
                ve_tile[tt] = ve

            ve_ln_done = ve

            @blk.sync
            def _(s):
                s.wait_ge(VE, ve_ln_done)
                for kc in range(16):
                    s.dma_start(out=o_xnfm[kc * 128:(kc + 1) * 128, :],
                                in_=xn[:, kc * THALO + 128:(kc + 1) * THALO]).then_inc(SO, 16)
            so += 256

            psT_guard.__exit__(None, None, None)
            psC = e(nc.psum_tensor("psC", [128, TPC], F32))

            @blk.scalar
            def _(s):
                s.preload_act_table(AF.Silu)
            mm_oc = [0] * NOC
            se_psum = [0] * NOC
            se_xbc = [0] * NOC
            cm_ch = [0] * 34
            se_cv = [0] * 34
            ve_dg = [0] * 34
            so_z = [0] * NOC
            so_cv = [0] * 34

            for oc in range(NOC):
                p = oc % 2
                ocw = 128 if oc < 66 else 64

                @blk.gpsimd
                def _(g, oc=oc, p=p, ocw=ocw):
                    if oc >= 2:
                        g.wait_ge(MM, mm_oc[oc - 2])
                    src = w_inT[:, oc * 128:oc * 128 + ocw]
                    src = src.rearrange("(kc p) m -> p kc m", p=128)
                    g.dma_start(out=wsl[:, p * 2048:p * 2048 + 16 * ocw],
                                in_=src).then_inc(WL, 16)
                wl += 16
                wl_oc = wl

                @blk.tensor
                def _(t, oc=oc, p=p, ocw=ocw, wl_oc=wl_oc):
                    t.wait_ge(WL, wl_oc)
                    t.wait_ge(VE, ve_ln_done)
                    if oc >= 2:
                        t.wait_ge(SE, se_psum[oc - 2])
                    for kc in range(16):
                        for (cs, cwd) in ((0, 512), (512, 512), (1024, 128)):
                            ins = t.matmul(psum[p][0:ocw, cs:cs + cwd],
                                           wsl[:, p * 2048 + kc * ocw:p * 2048 + (kc + 1) * ocw],
                                           xn[:, kc * THALO + cs:kc * THALO + cs + cwd],
                                           start=(kc == 0), stop=(kc == 15))
                            if kc == 15 and cs == 1024:
                                ins.then_inc(MM, 1)
                mm += 1
                mm_oc[oc] = mm

                if oc < 32:
                    @blk.scalar
                    def _(s, oc=oc, p=p):
                        s.wait_ge(MM, mm_oc[oc])
                        if oc >= 2:
                            s.wait_ge(SO, so_z[oc - 2])
                        s.activation(zsb[:, p * TPC:(p + 1) * TPC],
                                     psum[p][:, 128:THALO], AF.Silu,
                                     bias=0.0, scale=1.0).then_inc(SE, 1)
                    se += 1
                    se_psum[oc] = se

                    @blk.sync
                    def _(s, oc=oc, p=p, se_w=se):
                        s.wait_ge(SE, se_w)
                        s.dma_start(out=o_siluz[oc * 128:(oc + 1) * 128, :],
                                    in_=zsb[:, p * TPC:(p + 1) * TPC]).then_inc(SO, 16)
                    so += 16
                    so_z[oc] = so

                elif oc < 66:
                    ch = oc - 32

                    @blk.scalar
                    def _(s, oc=oc, p=p, ch=ch):
                        s.wait_ge(MM, mm_oc[oc])
                        if ch >= 2:
                            s.wait_ge(MM, cm_ch[ch - 2])
                        s.activation(xbc[:, p * THALO:(p + 1) * THALO],
                                     psum[p][:, 0:THALO], AF.Copy,
                                     bias=0.0, scale=1.0).then_inc(SE, 1)
                    se += 1
                    se_psum[oc] = se
                    se_xbc[oc] = se

                    @blk.vector
                    def _(v, ch=ch, p=p):
                        v.wait_ge(VE, ve_p0)
                        if ch >= 2:
                            v.wait_ge(MM, cm_ch[ch - 2])
                        for k in range(4):
                            v.tensor_scalar(dg[:, (p * 4 + k) * 128:(p * 4 + k + 1) * 128],
                                            idt[:], scalar1=cw[:, ch * 4 + k:ch * 4 + k + 1],
                                            scalar2=None, op0=ALU.mult).then_inc(VE, 1)
                    ve += 4
                    ve_dg[ch] = ve

                    @blk.tensor
                    def _(t, ch=ch, p=p):
                        t.wait_ge(SE, se_xbc[ch + 32])
                        t.wait_ge(VE, ve_dg[ch])
                        if ch >= 1:
                            t.wait_ge(SE, se_cv[ch - 1])
                        for k in range(4):
                            for (cs, cwd) in ((0, 512), (512, 512)):
                                ins = t.matmul(psC[:, cs:cs + cwd],
                                               dg[:, (p * 4 + k) * 128:(p * 4 + k + 1) * 128],
                                               xbc[:, p * THALO + 125 + k + cs:
                                                   p * THALO + 125 + k + cs + cwd],
                                               start=(k == 0), stop=(k == 3))
                                if k == 3 and cs == 512:
                                    ins.then_inc(MM, 1)
                    mm += 1
                    cm_ch[ch] = mm

                    @blk.scalar
                    def _(s, ch=ch, p=p):
                        s.wait_ge(MM, cm_ch[ch])
                        if ch >= 2:
                            s.wait_ge(SO, so_cv[ch - 2])
                        s.activation(xcv[:, p * TPC:(p + 1) * TPC], psC[:, 0:TPC],
                                     AF.Silu, bias=cb[:, ch:ch + 1],
                                     scale=1.0).then_inc(SE, 1)
                    se += 1
                    se_cv[ch] = se

                    @blk.sync
                    def _(s, ch=ch, p=p, se_w=se):
                        s.wait_ge(SE, se_w)
                        if ch < 32:
                            dst = o_xconv[ch * 128:(ch + 1) * 128, :]
                        elif ch == 32:
                            dst = o_B[:, :]
                        else:
                            dst = o_C[:, :]
                        s.dma_start(out=dst, in_=xcv[:, p * TPC:(p + 1) * TPC]).then_inc(SO, 16)
                    so += 16
                    so_cv[ch] = so

                else:
                    @blk.scalar
                    def _(s, oc=oc, p=p, se0=se):
                        preload_set6(s)
                        s.wait_ge(MM, mm_oc[oc])
                        s.activation(tdt[:], psum[p][0:64, 0:THALO], AF.Exp,
                                     bias=db[:, 0:1], scale=1.0).then_inc(SE, 1)
                        s.wait_ge(SE, se0 + 1)
                        s.activation(dtf[:], tdt[:], AF.Ln, bias=cONE[0:64, 0:1],
                                     scale=1.0).then_inc(SE, 1)
                    se += 2
                    se_psum[oc] = se - 1
                    se_dt = se

                    v0 = ve

                    @blk.vector
                    def _(v, v0=v0):
                        v.wait_ge(SE, se_dt)
                        v.tensor_scalar(adt[:], dtf[:, 128:THALO],
                                        scalar1=negA[:, 0:1], scalar2=None,
                                        op0=ALU.mult).then_inc(VE, 1)
                        v.wait_ge(VE, v0 + 1)
                        for c in range(16):
                            v.tensor_tensor_scan(acs[:, c * 64:(c + 1) * 64],
                                                 ones64[:], adt[:, c * 64:(c + 1) * 64],
                                                 initial=0.0, op0=ALU.mult,
                                                 op1=ALU.add).then_inc(VE, 1)
                    ve += 17
                    ve_dt = ve

                    @blk.sync
                    def _(s):
                        s.wait_ge(SE, se_dt)
                        s.dma_start(out=o_dt[:, :], in_=dtf[:, 128:THALO]).then_inc(SO, 16)
                        s.wait_ge(VE, ve_dt)
                        s.dma_start(out=o_Adt[:, :], in_=adt[:, :]).then_inc(SO, 16)
                        s.dma_start(out=o_Acs[:, :], in_=acs[:, :]).then_inc(SO, 16)
                    so += 48

            @blk.sync
            def _(s):
                s.wait_ge(SO, so)
    return nc


def host_A(inputs):
    u = np.asarray(inputs["L0_attn_u"])[0]
    u_pad = np.pad(f32(u), ((128, 0), (0, 0)))
    w_inT = f32(np.asarray(inputs["L0_attn_W_in"]).T)
    cwi = np.asarray(inputs["L0_attn_conv_w"])
    convw = f32(cwi.reshape(34, 128, 4).transpose(1, 0, 2).reshape(128, 136))
    cbi = np.asarray(inputs["L0_attn_conv_b"])
    convb = f32(cbi.reshape(34, 128).T)
    dtb = f32(np.asarray(inputs["L0_attn_dt_bias"]).reshape(64, 1))
    alog = f32(np.asarray(inputs["L0_attn_A_log"]).reshape(64, 1))
    ident = bf16(np.eye(128, dtype=np.float32))
    maps = []
    for c in range(NCORES):
        maps.append({
            "u": f32(u_pad[c * TPC:c * TPC + THALO]),
            "w_inT": w_inT,
            "convw": convw,
            "convb": convb,
            "dtb": dtb,
            "alog": alog,
            "ident": ident,
        })
    return maps



def build_B1():
    import os
    NGRP = int(os.environ.get("B1_NGRP", "8"))
    SUB = int(os.environ.get("B1_SUB", "9"))
    FOPS = int(os.environ.get("B1_FOPS", "3"))
    DO_STATES = os.environ.get("B1_STATES", "1") == "1"
    DO_SCAN = os.environ.get("B1_SCAN", "1") == "1"
    nc = bass.Bass("TRN2", target_bir_lowering=False)
    dram = nc.dram_tensor
    i_xconv = dram("xconv", [INNER, TPC], BF16, kind="ExternalInput").ap()
    i_B = dram("Bo", [128, TPC], BF16, kind="ExternalInput").ap()
    i_dt = dram("dto", [64, TPC], F32, kind="ExternalInput").ap()
    i_Adt = dram("Adto", [64, TPC], F32, kind="ExternalInput").ap()
    i_Acs = dram("Acso", [64, TPC], F32, kind="ExternalInput").ap()
    i_sel = dram("selg", [16, TPC], F32, kind="ExternalInput").ap()
    i_eye64 = dram("eye64", [64, 64], F32, kind="ExternalInput").ap()
    i_eye128b = dram("eye128b", [128, 128], BF16, kind="ExternalInput").ap()

    o_cs = dram("cso", [128, 16 * 4096], F32, kind="ExternalOutput").ap()
    o_seg = dram("sego", [128, 4096], F32, kind="ExternalOutput").ap()
    o_lnl = dram("lnlo", [64, 1], F32, kind="ExternalOutput").ap()

    with ExitStack() as ex:
        e = ex.enter_context
        selg = e(nc.sbuf_tensor("selg_s", [16, TPC], F32))
        ey64 = e(nc.sbuf_tensor("ey64", [64, 64], F32))
        ey128 = e(nc.sbuf_tensor("ey128", [128, 128], BF16))
        dtfm = e(nc.sbuf_tensor("dtfm", [64, TPC], F32))
        adtfm = e(nc.sbuf_tensor("adtfm", [64, TPC], F32))
        acsfm = e(nc.sbuf_tensor("acsfm", [64, TPC], F32))
        aend = e(nc.sbuf_tensor("aend", [64, 16], F32))
        aendT = e(nc.sbuf_tensor("aendT", [16, 64], F32))
        lam = e(nc.sbuf_tensor("lam", [1, 16 * 64], F32))
        lnl = e(nc.sbuf_tensor("lnl", [64, 1], F32))
        ones1 = e(nc.sbuf_tensor("ones1", [1, 128], F32))
        Bfull = e(nc.sbuf_tensor("Bfull", [128, TPC], BF16))
        Btm = e(nc.sbuf_tensor("Btm", [128, 128], BF16))
        xcg = e(nc.sbuf_tensor("xcg", [128, 2 * 4096], BF16))
        xtm = e(nc.sbuf_tensor("xtm", [128, 4096], BF16))
        xdd = e(nc.sbuf_tensor("xdd", [128, 4096], BF16))
        acstm = e(nc.sbuf_tensor("acstm", [128, 64], F32))
        dttm = e(nc.sbuf_tensor("dttm", [128, 64], F32))
        Ftm = e(nc.sbuf_tensor("Ftm", [128, 64], F32))
        state = e(nc.sbuf_tensor("state", [128, 4096], F32))
        csb = e(nc.sbuf_tensor("csb", [128, 2 * 2048], F32))

        psS = e(nc.psum_tensor("psS", [128, 2048], F32))
        psTb = e(nc.psum_tensor("psTb", [128, 128], BF16))
        psTf = e(nc.psum_tensor("psTf", [128, 128], F32))
        psM = e(nc.psum_tensor("psM", [128, 512], F32))

        LD = e(nc.semaphore("LD"))
        MM = e(nc.semaphore("MM"))
        SE = e(nc.semaphore("SE"))
        VE = e(nc.semaphore("VE"))
        SO = e(nc.semaphore("SO"))
        ld = mm = se = ve = so = 0

        with nc.Block() as blk:
            @blk.sync
            def _(s):
                s.dma_start(out=selg[:], in_=i_sel[:, :]).then_inc(LD, 16)
                s.dma_start(out=ey64[:], in_=i_eye64[:, :]).then_inc(LD, 16)
                s.dma_start(out=ey128[:], in_=i_eye128b[:, :]).then_inc(LD, 16)
                s.dma_start(out=dtfm[:], in_=i_dt[:, :]).then_inc(LD, 16)
                s.dma_start(out=adtfm[:], in_=i_Adt[:, :]).then_inc(LD, 16)
                s.dma_start(out=acsfm[:], in_=i_Acs[:, :]).then_inc(LD, 16)
                s.dma_start(out=Bfull[:], in_=i_B[:, :]).then_inc(LD, 16)
            ld += 112
            ld_p0 = ld

            v0 = ve

            @blk.vector
            def _(v, v0=v0):
                v.wait_ge(LD, ld_p0)
                for c in range(16):
                    v.reduce_sum(aend[:, c:c + 1], adtfm[:, c * 64:(c + 1) * 64],
                                 axis=AX.X).then_inc(VE, 1)
                v.wait_ge(VE, v0 + 16)
                v.reduce_sum(lnl[:], aend[:], axis=AX.X).then_inc(VE, 1)
                v.memset(ones1[:], 1.0).then_inc(VE, 1)
                v.memset(state[:], 0.0).then_inc(VE, 1)
            ve += 19
            ve_pre = ve

            @blk.sync
            def _(s):
                s.wait_ge(VE, ve_pre)
                s.dma_start(out=o_lnl[:, :], in_=lnl[:]).then_inc(SO, 16)
            so += 16

            @blk.scalar
            def _(s):
                s.preload_act_table(AF.Exp)

            @blk.tensor
            def _(t):
                t.wait_ge(VE, ve_pre)
                t.transpose(psTf[0:16, 0:64], aend[:], ey64[:]).then_inc(MM, 1)
            mm += 1
            mm_aT = mm

            @blk.vector
            def _(v):
                v.wait_ge(MM, mm_aT)
                v.tensor_copy(aendT[:], psTf[0:16, 0:64]).then_inc(VE, 1)
            ve += 1
            ve_aT = ve

            for c in range(16):
                @blk.tensor
                def _(t, c=c):
                    t.wait_ge(VE, ve_aT)
                    if c > 0:
                        t.wait_ge(SE, se + c)
                    t.matmul(psM[0:1, 64:128], aend[:, c:c + 1], ey64[:],
                             start=True, stop=True).then_inc(MM, 1)
                mm += 1

                @blk.scalar
                def _(s, c=c, mm_w=mm):
                    s.wait_ge(MM, mm_w)
                    s.activation(lam[0:1, c * 64:(c + 1) * 64], psM[0:1, 64:128],
                                 AF.Exp, bias=0.0, scale=1.0).then_inc(SE, 1)
            se += 16
            se_lam = se

            ve_grp = [0] * 8
            so_cs = {}
            ve_scan = {}
            for g in range(NGRP):
                pg = g % 2

                @blk.sync
                def _(s, g=g, pg=pg):
                    if g >= 2:
                        s.wait_ge(VE, ve_grp[g - 2])
                    src = i_xconv[:, g * 128:(g + 1) * 128]
                    src = src.rearrange("(pc p) t -> p pc t", p=128)
                    s.dma_start(out=xcg[:, pg * 4096:(pg + 1) * 4096],
                                in_=src).then_inc(LD, 16)
                ld += 16
                ld_g = ld

                v0 = ve

                if SUB >= 2:
                    @blk.tensor
                    def _(t, g=g, pg=pg, ld_g=ld_g, v0=v0):
                        t.wait_ge(LD, ld_g)
                        t.wait_ge(SE, se_lam)
                        if g >= 1:
                            t.wait_ge(VE, ve_grp[g - 1])
                        for pc in range(32):
                            if pc > 0:
                                t.wait_ge(VE, v0 + pc)
                            t.transpose(psTb[:],
                                        xcg[:, pg * 4096 + pc * 128:pg * 4096 + (pc + 1) * 128],
                                        ey128[:]).then_inc(MM, 1)
                        if SUB >= 3:
                            t.wait_ge(VE, v0 + 32)
                            t.transpose(psTb[:], Bfull[:, g * 128:(g + 1) * 128],
                                        ey128[:]).then_inc(MM, 1)
                            t.wait_ge(VE, v0 + 33)
                            t.transpose(psTf[:, 0:64], acsfm[0:64, g * 128:(g + 1) * 128],
                                        ey64[:]).then_inc(MM, 1)
                            t.wait_ge(VE, v0 + 34)
                            t.transpose(psTf[:, 64:128], dtfm[0:64, g * 128:(g + 1) * 128],
                                        ey64[:]).then_inc(MM, 1)
                    mm0 = mm
                    mm += 32 if SUB == 2 else 35

                    @blk.vector
                    def _(v, g=g, mm0=mm0):
                        for pc in range(32):
                            v.wait_ge(MM, mm0 + pc + 1)
                            v.tensor_copy(xtm[:, pc * 128:(pc + 1) * 128], psTb[:]).then_inc(VE, 1)
                        if SUB >= 3:
                            v.wait_ge(MM, mm0 + 33)
                            v.tensor_copy(Btm[:], psTb[:]).then_inc(VE, 1)
                            v.wait_ge(MM, mm0 + 34)
                            v.tensor_copy(acstm[:, 0:64], psTf[:, 0:64]).then_inc(VE, 1)
                            v.wait_ge(MM, mm0 + 35)
                            v.tensor_copy(dttm[:, 0:64], psTf[:, 64:128]).then_inc(VE, 1)
                    ve += 32 if SUB == 2 else 35
                else:
                    mm0 = mm

                if SUB >= 4:
                    @blk.tensor
                    def _(t, g=g, ve_w=ve):
                        t.wait_ge(VE, ve_w)
                        t.matmul(psM[:, 0:64], selg[:, g * 128:(g + 1) * 128], aendT[:],
                                 start=True, stop=True).then_inc(MM, 1)
                    mm += 1
                mm_E = mm

                if SUB >= 5:
                    @blk.vector
                    def _(v, g=g, mm_E=mm_E):
                        v.wait_ge(MM, mm_E)
                        if FOPS >= 1:
                            v.tensor_tensor(Ftm[:], psM[:, 0:64], acstm[:, 0:64],
                                            op=ALU.subtract).then_inc(VE, 1)
                        else:
                            v.engine_nop().then_inc(VE, 1)
                    ve += 1
                    ve_F1 = ve

                    @blk.scalar
                    def _(s, g=g, ve_F1=ve_F1):
                        s.wait_ge(VE, ve_F1)
                        if FOPS >= 2:
                            s.activation(Ftm[:], Ftm[:], AF.Exp, bias=0.0,
                                         scale=1.0).then_inc(SE, 1)
                        else:
                            s.nop().then_inc(SE, 1)
                    se += 1
                    se_F = se

                    v0 = ve

                    @blk.vector
                    def _(v, g=g, pg=pg, v0=v0, se_F=se_F):
                        v.wait_ge(SE, se_F)
                        if FOPS >= 3:
                            v.tensor_tensor(Ftm[:], Ftm[:], dttm[:, 0:64],
                                            op=ALU.mult).then_inc(VE, 1)
                        else:
                            v.engine_nop().then_inc(VE, 1)
                        if SUB >= 6:
                            v.wait_ge(VE, v0 + 1)
                            xv = xtm[:, :].rearrange("p (h q) -> p h q", h=64)
                            fv = Ftm[:, 0:64].unsqueeze(2).broadcast_to((128, 64, 64))
                            v.tensor_tensor(xdd[:, :].rearrange("p (h q) -> p h q", h=64),
                                            xv, fv, op=ALU.mult).then_inc(VE, 1)
                        else:
                            v.engine_nop().then_inc(VE, 1)
                    ve += 2
                ve_xdd = ve
                ve_grp[g] = ve

                for ci in range(2 if DO_STATES else 0):
                    c = 2 * g + ci
                    for hh in range(2):
                        sl_idx = (2 * c + hh) % 2

                        @blk.tensor
                        def _(t, c=c, ci=ci, hh=hh, ve_xdd=ve_xdd):
                            t.wait_ge(VE, ve_xdd)
                            if not (c == 0 and hh == 0):
                                pc_, ph_ = (c, 0) if hh == 1 else (c - 1, 1)
                                t.wait_ge(VE, ve_scan[(pc_, ph_)])
                            ins = None
                            for h in range(32 * hh, 32 * hh + 32):
                                ins = t.matmul(psS[:, (h - 32 * hh) * 64:(h - 32 * hh + 1) * 64],
                                               Btm[ci * 64:(ci + 1) * 64, :],
                                               xdd[ci * 64:(ci + 1) * 64, h * 64:(h + 1) * 64],
                                               start=True, stop=True)
                            ins.then_inc(MM, 1)
                        mm += 1
                        mm_S = mm

                        @blk.tensor
                        def _(t, c=c, hh=hh, ve_xdd=ve_xdd):
                            t.wait_ge(VE, ve_xdd)
                            t.matmul(psM[:, 128:160], ones1[:, :],
                                     lam[0:1, c * 64 + hh * 32:c * 64 + hh * 32 + 32],
                                     start=True, stop=True).then_inc(MM, 1)
                        mm += 1
                        mm_L = mm

                        v0 = ve

                        @blk.vector
                        def _(v, c=c, hh=hh, v0=v0, mm_S=mm_S, mm_L=mm_L,
                              sl_idx=sl_idx):
                            v.wait_ge(MM, mm_S)
                            prev2 = (c * 2 + hh) - 2
                            if prev2 >= 0:
                                v.wait_ge(SO, so_cs[(prev2 // 2, prev2 % 2)])
                            v.tensor_copy(csb[:, sl_idx * 2048:(sl_idx + 1) * 2048],
                                          psS[:]).then_inc(VE, 1)
                            v.wait_ge(MM, mm_L)
                            v.wait_ge(VE, v0 + 1)
                            sl = state[:, hh * 2048:(hh + 1) * 2048]
                            sv = sl.rearrange("p (h q) -> p h q", h=32)
                            lv = psM[:, 128:160].unsqueeze(2).broadcast_to((128, 32, 64))
                            if DO_SCAN:
                                v.tensor_tensor(sv, sv, lv, op=ALU.mult).then_inc(VE, 1)
                                v.wait_ge(VE, v0 + 2)
                                v.tensor_tensor(sl, sl, psS[:], op=ALU.add).then_inc(VE, 1)
                            else:
                                v.engine_nop().then_inc(VE, 1)
                                v.engine_nop().then_inc(VE, 1)
                        ve += 3
                        ve_scan[(c, hh)] = ve

                        @blk.sync
                        def _(s, c=c, hh=hh, sl_idx=sl_idx, ve_w=ve - 2):
                            s.wait_ge(VE, ve_w)
                            s.dma_start(out=o_cs[:, c * 4096 + hh * 2048:
                                                 c * 4096 + (hh + 1) * 2048],
                                        in_=csb[:, sl_idx * 2048:(sl_idx + 1) * 2048]
                                        ).then_inc(SO, 16)
                        so += 16
                        so_cs[(c, hh)] = so

            @blk.sync
            def _(s):
                if (2 * NGRP - 1, 1) in ve_scan:
                    s.wait_ge(VE, ve_scan[(2 * NGRP - 1, 1)])
                else:
                    s.wait_ge(VE, ve)
                s.dma_start(out=o_seg[:, :], in_=state[:]).then_inc(SO, 16)
                s.wait_ge(SO, so + 16)
            so += 16
    return nc


def host_B1(a_outs, consts):
    maps = []
    for c in range(NCORES):
        r = a_outs[c]
        maps.append({
            "xconv": r["xconv"], "Bo": r["Bo"], "dto": r["dto"],
            "Adto": r["Adto"], "Acso": r["Acso"],
            "selg": consts["selg"], "eye64": consts["eye64"],
            "eye128b": consts["eye128b"],
        })
    return maps


def mk_consts():
    sel = np.zeros((16, TPC), np.float32)
    for g in range(8):
        for s_ in range(128):
            sel[2 * g + (1 if s_ >= 64 else 0), g * 128 + s_] = 1.0
    return {
        "selg": f32(sel),
        "eye64": f32(np.eye(64)),
        "eye128b": bf16(np.eye(128)),
    }



def build_B2():
    nc = bass.Bass("TRN2", target_bir_lowering=False)
    dram = nc.dram_tensor
    i_xconv = dram("xconv", [INNER, TPC], BF16, kind="ExternalInput").ap()
    i_siluz = dram("siluz", [INNER, TPC], BF16, kind="ExternalInput").ap()
    i_B = dram("Bo", [128, TPC], BF16, kind="ExternalInput").ap()
    i_C = dram("Co", [128, TPC], BF16, kind="ExternalInput").ap()
    i_dt = dram("dto", [64, TPC], F32, kind="ExternalInput").ap()
    i_Adt = dram("Adto", [64, TPC], F32, kind="ExternalInput").ap()
    i_Acs = dram("Acso", [64, TPC], F32, kind="ExternalInput").ap()
    i_cs = dram("cso", [128, 16 * 4096], F32, kind="ExternalInput").ap()
    i_segs = dram("segs", [8 * 128, 4096], F32, kind="ExternalInput").ap()
    i_lnl = dram("lnl_all", [8, 64], F32, kind="ExternalInput").ap()
    i_mmT = dram("mmT", [8, 8], F32, kind="ExternalInput").ap()
    i_wb = dram("wbias", [8, 1], F32, kind="ExternalInput").ap()
    i_rep = dram("rep64", [64, 4096], F32, kind="ExternalInput").ap()
    i_madd = dram("madd", [128, 128], F32, kind="ExternalInput").ap()
    i_drep = dram("drep", [128, 32], F32, kind="ExternalInput").ap()
    i_nw = dram("normw", [128, 32], F32, kind="ExternalInput").ap()
    i_eye64 = dram("eye64", [64, 64], F32, kind="ExternalInput").ap()
    i_eye128b = dram("eye128b", [128, 128], BF16, kind="ExternalInput").ap()
    i_woutT = dram("woutT", [INNER, D], F32, kind="ExternalInput").ap()

    o_s0 = dram("s0o", [D, TPC], BF16, kind="ExternalOutput").ap()

    with ExitStack() as ex:
        e = ex.enter_context
        ey64 = e(nc.sbuf_tensor("ey64", [64, 64], F32))
        ey128 = e(nc.sbuf_tensor("ey128", [128, 128], BF16))
        dtfm = e(nc.sbuf_tensor("dtfm", [64, TPC], F32))
        adtfm = e(nc.sbuf_tensor("adtfm", [64, TPC], F32))
        acsfm = e(nc.sbuf_tensor("acsfm", [64, TPC], F32))
        aend = e(nc.sbuf_tensor("aend", [64, 16], F32))
        lam = e(nc.sbuf_tensor("lam", [1, 16 * 64], F32))
        lnla = e(nc.sbuf_tensor("lnla", [8, 64], F32))
        mmTs = e(nc.sbuf_tensor("mmTs", [8, 8], F32))
        wbs = e(nc.sbuf_tensor("wbs", [8, 1], F32))
        madd = e(nc.sbuf_tensor("madd_s", [128, 128], F32))
        drep = e(nc.sbuf_tensor("drep_s", [128, 32], F32))
        nw = e(nc.sbuf_tensor("nw_s", [128, 32], F32))
        ones1 = e(nc.sbuf_tensor("ones1", [1, 128], F32))
        onesb = e(nc.sbuf_tensor("onesb", [128, 1], BF16))
        ww = e(nc.sbuf_tensor("ww", [8, 64], F32))
        wwT = e(nc.sbuf_tensor("wwT", [64, 8], F32))
        Bfull = e(nc.sbuf_tensor("Bfull", [128, TPC], BF16))
        Cfull = e(nc.sbuf_tensor("Cfull", [128, TPC], BF16))
        state = e(nc.sbuf_tensor("state", [128, 4096], F32))
        yg = e(nc.sbuf_tensor("yg", [128, 32 * TPC], BF16))
        ysq = e(nc.sbuf_tensor("ysq", [128, TPC], BF16))
        rr = e(nc.sbuf_tensor("rr", [1, TPC], F32))

        cEPS_B2 = e(nc.sbuf_tensor("cEPS_B2", [1, 1], F32))
        psM2 = e(nc.psum_tensor("psM2", [128, 512], F32))

        LD = e(nc.semaphore("LD"))
        WL = e(nc.semaphore("WL"))
        MM = e(nc.semaphore("MM"))
        SE = e(nc.semaphore("SE"))
        VE = e(nc.semaphore("VE"))
        SO = e(nc.semaphore("SO"))
        ld = wl = mm = se = ve = so = 0

        with nc.Block() as blk:
            @blk.sync
            def _(s):
                for (dst, src) in ((ey64, i_eye64), (ey128, i_eye128b),
                                   (dtfm, i_dt), (adtfm, i_Adt), (acsfm, i_Acs),
                                   (lnla, i_lnl), (mmTs, i_mmT), (wbs, i_wb),
                                   (madd, i_madd), (drep, i_drep),
                                   (nw, i_nw), (Bfull, i_B), (Cfull, i_C)):
                    s.dma_start(out=dst[:], in_=src[:, :]).then_inc(LD, 16)
            ld += 13 * 16
            ld_p0 = ld

            v0 = ve

            @blk.scalar
            def _(s):
                s.preload_act_table(AF.Exp)

            @blk.vector
            def _(v, v0=v0):
                v.wait_ge(LD, ld_p0)
                for c in range(16):
                    v.reduce_sum(aend[:, c:c + 1], adtfm[:, c * 64:(c + 1) * 64],
                                 axis=AX.X).then_inc(VE, 1)
                v.memset(ones1[:], 1.0).then_inc(VE, 1)
                v.memset(onesb[:], 1.0).then_inc(VE, 1)
                v.memset(cEPS_B2[:], 1e-5).then_inc(VE, 1)
            ve += 19
            ve_pre = ve

            for c in range(16):
                @blk.tensor
                def _(t, c=c):
                    t.wait_ge(VE, ve_pre)
                    if c > 0:
                        t.wait_ge(SE, se + c)
                    t.matmul(psM2[0:1, 256:320], aend[:, c:c + 1], ey64[:],
                             start=True, stop=True).then_inc(MM, 1)
                mm += 1

                @blk.scalar
                def _(s, c=c, mm_w=mm):
                    s.wait_ge(MM, mm_w)
                    s.activation(lam[0:1, c * 64:(c + 1) * 64], psM2[0:1, 256:320],
                                 AF.Exp, bias=0.0, scale=1.0).then_inc(SE, 1)
            se += 16
            se_lam = se

            psW_g = nc.psum_tensor("psW", [128, 2048], F32)
            psW = psW_g.__enter__()
            rep_g = nc.sbuf_tensor("rep_s", [64, 4096], F32)
            rep = rep_g.__enter__()
            segsl_g = nc.sbuf_tensor("segsl", [128, 2 * 2048], F32)
            segsl = segsl_g.__enter__()

            @blk.sync
            def _(s):
                s.dma_start(out=rep[:], in_=i_rep[:, :]).then_inc(LD, 16)
            ld += 16
            ld_rep = ld

            @blk.tensor
            def _(t):
                t.wait_ge(VE, ve_pre)
                t.wait_ge(SE, se_lam)
                t.matmul(psM2[0:8, 0:64], mmTs[:], lnla[:],
                         start=True, stop=True).then_inc(MM, 1)
            mm += 1
            mm_wl = mm

            @blk.scalar
            def _(s):
                s.wait_ge(MM, mm_wl)
                s.wait_ge(SE, se_lam)
                s.activation(ww[:], psM2[0:8, 0:64], AF.Exp, bias=wbs[:, 0:1],
                             scale=1.0).then_inc(SE, 1)
            se += 1
            se_ww = se

            @blk.tensor
            def _(t):
                t.wait_ge(SE, se_ww)
                t.transpose(psM2[0:64, 64:72], ww[:], ey64[0:8, 0:8]).then_inc(MM, 1)
            mm += 1
            mm_wT = mm

            @blk.vector
            def _(v):
                v.wait_ge(MM, mm_wT)
                v.tensor_copy(wwT[:], psM2[0:64, 64:72]).then_inc(VE, 1)
                v.memset(state[:], 0.0).then_inc(VE, 1)
            ve += 2
            ve_wT = ve

            ve_inc = ve

            for j in range(8):
                for hf in range(2):
                    idx = j * 2 + hf
                    pj = idx % 2

                    @blk.sync
                    def _(s, j=j, hf=hf, pj=pj, idx=idx):
                        if idx >= 2:
                            s.wait_ge(VE, ve_inc + 2 * (idx - 1))
                        s.dma_start(out=segsl[:, pj * 2048:(pj + 1) * 2048],
                                    in_=i_segs[j * 128:(j + 1) * 128,
                                               hf * 2048:(hf + 1) * 2048]).then_inc(LD, 16)
                    ld += 16
                    ld_ij = ld

                    @blk.tensor
                    def _(t, j=j, hf=hf, ld_ij=ld_ij, idx=idx):
                        t.wait_ge(VE, ve_wT)
                        t.wait_ge(LD, ld_rep)
                        if idx >= 1:
                            t.wait_ge(VE, ve_inc + 2 * idx)
                        lh = wwT[:, j:j + 1].broadcast_to((64, 128))
                        for k in range(4):
                            t.matmul(psW[:, k * 512:(k + 1) * 512], lh,
                                     rep[:, hf * 2048 + k * 512:hf * 2048 + (k + 1) * 512],
                                     start=True, stop=True).then_inc(MM, 1)
                    mm += 4
                    mm_wb = mm

                    v0 = ve

                    @blk.vector
                    def _(v, j=j, hf=hf, pj=pj, v0=v0, mm_wb=mm_wb, ld_ij=ld_ij):
                        v.wait_ge(MM, mm_wb)
                        v.wait_ge(LD, ld_ij)
                        v.tensor_tensor(segsl[:, pj * 2048:(pj + 1) * 2048],
                                        segsl[:, pj * 2048:(pj + 1) * 2048],
                                        psW[:], op=ALU.mult).then_inc(VE, 1)
                        v.wait_ge(VE, v0 + 1)
                        v.tensor_tensor(state[:, hf * 2048:(hf + 1) * 2048],
                                        state[:, hf * 2048:(hf + 1) * 2048],
                                        segsl[:, pj * 2048:(pj + 1) * 2048],
                                        op=ALU.add).then_inc(VE, 1)
                    ve += 2

            psW_g.__exit__(None, None, None)
            segsl_g.__exit__(None, None, None)
            rep_g.__exit__(None, None, None)
            _p2_guards = []

            def _p2alloc(name, shape, dt_):
                g_ = nc.sbuf_tensor(name, shape, dt_)
                _p2_guards.append(g_)
                return g_.__enter__()

            snap0 = _p2alloc("snap0", [128, 4096], BF16)
            snap1 = _p2alloc("snap1", [128, 4096], BF16)
            csl = _p2alloc("csl", [128, 4096], F32)
            xcg = _p2alloc("xcg", [128, 4096], BF16)
            szg = _p2alloc("szg", [128, 4096], BF16)
            xtm = _p2alloc("xtm", [128, 4096], BF16)
            xdt = _p2alloc("xdt", [128, 4096], BF16)
            acstm = _p2alloc("acstm", [128, 64], F32)
            dttm = _p2alloc("dttm", [128, 64], F32)
            acsflat = _p2alloc("acsflat", [1, 2048], F32)
            eacs = _p2alloc("eacs", [64, 128], F32)
            eflat = _p2alloc("eflat", [1, 2048], F32)
            dq = _p2alloc("dq", [128, 2048], F32)
            sq = _p2alloc("sq", [128, 2048], BF16)
            cdq = _p2alloc("cdq", [128, 2048], BF16)
            ve_incdone = ve

            psCB_g = nc.psum_tensor("psCB", [128, 128], F32)
            psCB = psCB_g.__enter__()
            psQ_g = nc.psum_tensor("psQ", [128, 2048], F32)
            psQ = psQ_g.__enter__()
            psTb_g = nc.psum_tensor("psTb", [128, 128], BF16)
            psTb = psTb_g.__enter__()
            psY = [psQ[:, 0:128], psQ[:, 1024:1152]]

            ve_grp = [0] * 8
            ve_gate = [0] * 8
            last_evac = {0: None, 1: None}
            for g in range(8):
                @blk.sync
                def _(s, g=g):
                    if g >= 1:
                        s.wait_ge(VE, ve_gate[g - 1])
                    src = i_xconv[:, g * 128:(g + 1) * 128]
                    s.dma_start(out=xcg[:],
                                in_=src.rearrange("(pc p) t -> p pc t", p=128)).then_inc(LD, 16)
                    src2 = i_siluz[:, g * 128:(g + 1) * 128]
                    s.dma_start(out=szg[:],
                                in_=src2.rearrange("(pc p) t -> p pc t", p=128)).then_inc(LD, 16)
                ld += 32
                ld_g = ld

                v0 = ve
                mm0 = mm

                @blk.tensor
                def _(t, g=g, ld_g=ld_g, v0=v0):
                    t.wait_ge(LD, ld_g)
                    t.wait_ge(SE, se_lam)
                    t.wait_ge(VE, ve_incdone)
                    if g >= 1:
                        t.wait_ge(VE, ve_grp[g - 1])
                    for pc in range(32):
                        if pc > 0:
                            t.wait_ge(VE, v0 + pc)
                        t.transpose(psTb[:], xcg[:, pc * 128:(pc + 1) * 128],
                                    ey128[:]).then_inc(MM, 1)
                    t.wait_ge(VE, v0 + 32)
                    t.transpose(psM2[:, 320:384], acsfm[0:64, g * 128:(g + 1) * 128],
                                ey64[:]).then_inc(MM, 1)
                    t.wait_ge(VE, v0 + 33)
                    t.transpose(psM2[:, 384:448], dtfm[0:64, g * 128:(g + 1) * 128],
                                ey64[:]).then_inc(MM, 1)
                mm += 34

                @blk.vector
                def _(v, g=g, mm0=mm0):
                    for pc in range(32):
                        v.wait_ge(MM, mm0 + pc + 1)
                        v.tensor_copy(xtm[:, pc * 128:(pc + 1) * 128], psTb[:]).then_inc(VE, 1)
                    v.wait_ge(MM, mm0 + 33)
                    v.tensor_copy(acstm[:, 0:64], psM2[:, 320:384]).then_inc(VE, 1)
                    v.wait_ge(MM, mm0 + 34)
                    v.tensor_copy(dttm[:, 0:64], psM2[:, 384:448]).then_inc(VE, 1)
                ve += 34
                ve_tr = ve

                @blk.vector
                def _(v, g=g, ve_tr=ve_tr):
                    v.wait_ge(VE, ve_tr)
                    xv = xtm[:, :].rearrange("p (h q) -> p h q", h=64)
                    fv = dttm[:, 0:64].unsqueeze(2).broadcast_to((128, 64, 64))
                    v.tensor_tensor(xdt[:, :].rearrange("p (h q) -> p h q", h=64),
                                    xv, fv, op=ALU.mult).then_inc(VE, 1)
                ve += 1
                ve_xdt = ve

                @blk.scalar
                def _(s, g=g, ve_tr=ve_tr):
                    s.wait_ge(VE, ve_tr)
                    s.activation(eacs[:], acsfm[0:64, g * 128:(g + 1) * 128],
                                 AF.Exp, bias=0.0, scale=1.0).then_inc(SE, 1)
                se += 1
                se_ea = se


                @blk.tensor
                def _(t, g=g, ve_xdt=ve_xdt):
                    t.wait_ge(VE, ve_xdt)
                    t.matmul(psCB[:], Bfull[:, g * 128:(g + 1) * 128],
                             Cfull[:, g * 128:(g + 1) * 128],
                             start=True, stop=True).then_inc(MM, 1)
                mm += 1
                mm_cb = mm

                v0 = ve

                @blk.vector
                def _(v, g=g, v0=v0, ld_g=ld_g):
                    v.wait_ge(LD, ld_g)
                    v.tensor_copy(snap0[:], state[:]).then_inc(VE, 1)
                ve += 1
                ve_sn0 = ve

                for ci in range(2):
                    c = 2 * g + ci

                    @blk.sync
                    def _(s, c=c, ci=ci, ve_w=ve):
                        if not (c == 0):
                            s.wait_ge(VE, ve_w)
                        s.dma_start(out=csl[:],
                                    in_=i_cs[:, c * 4096:(c + 1) * 4096]).then_inc(LD, 16)
                    ld += 16
                    ld_c = ld

                    @blk.tensor
                    def _(t, c=c, ve_w=ve):
                        t.wait_ge(VE, ve_w)
                        t.matmul(psM2[:, 64:128], ones1[:, :],
                                 lam[0:1, c * 64:(c + 1) * 64],
                                 start=True, stop=True).then_inc(MM, 1)
                    mm += 1
                    mm_lam = mm

                    v0 = ve

                    @blk.vector
                    def _(v, c=c, ci=ci, v0=v0, mm_lam=mm_lam, ld_c=ld_c):
                        v.wait_ge(MM, mm_lam)
                        v.wait_ge(LD, ld_c)
                        sv = state[:, :].rearrange("p (h q) -> p h q", h=64)
                        lv = psM2[:, 64:128].unsqueeze(2).broadcast_to((128, 64, 64))
                        v.tensor_tensor(sv, sv, lv, op=ALU.mult).then_inc(VE, 1)
                        v.wait_ge(VE, v0 + 1)
                        v.tensor_tensor(state[:], state[:], csl[:],
                                        op=ALU.add).then_inc(VE, 1)
                        if ci == 0:
                            v.wait_ge(VE, v0 + 2)
                            v.tensor_copy(snap1[:], state[:]).then_inc(VE, 1)
                    ve += 3 if ci == 0 else 2
                ve_scan = ve

                ve_q = [0] * 4
                for q in range(4):
                    @blk.gpsimd
                    def _(gp, g=g, q=q, se_ea=se_ea, ve_w=ve):
                        gp.wait_ge(SE, se_ea)
                        if q > 0:
                            gp.wait_ge(VE, ve_q[q - 1])
                        gp.dma_start(out=acsflat[0:1, :],
                                     in_=acsfm[16 * q:16 * (q + 1),
                                               g * 128:(g + 1) * 128]).then_inc(WL, 16)
                        gp.dma_start(out=eflat[0:1, :],
                                     in_=eacs[16 * q:16 * (q + 1), :]).then_inc(WL, 16)
                    wl += 32
                    wl_fl = wl

                    @blk.tensor
                    def _(t, g=g, q=q, wl_fl=wl_fl, mm_cb=mm_cb):
                        t.wait_ge(WL, wl_fl)
                        t.wait_ge(VE, ve_scan)
                        if q > 0:
                            t.wait_ge(VE, ve_q[q - 1])
                        for k in range(4):
                            t.matmul(psQ[:, k * 512:(k + 1) * 512], ones1[:, :],
                                     acsflat[0:1, k * 512:(k + 1) * 512],
                                     start=True, stop=True).then_inc(MM, 1)
                    mm += 4
                    mm_t1 = mm

                    v0 = ve

                    @blk.vector
                    def _(v, g=g, q=q, v0=v0, mm_t1=mm_t1):
                        v.wait_ge(MM, mm_t1)
                        t2 = acstm[:, q * 16:(q + 1) * 16].unsqueeze(2).broadcast_to((128, 16, 128))
                        v.tensor_tensor(dq[:, :].rearrange("p (h l) -> p h l", h=16),
                                        psQ[:, :].rearrange("p (h l) -> p h l", h=16),
                                        t2, op=ALU.subtract).then_inc(VE, 1)
                        v.wait_ge(VE, v0 + 1)
                        mv = madd[:, :].unsqueeze(1).broadcast_to((128, 16, 128))
                        v.tensor_tensor(dq[:, :].rearrange("p (h l) -> p h l", h=16),
                                        dq[:, :].rearrange("p (h l) -> p h l", h=16),
                                        mv, op=ALU.add).then_inc(VE, 1)
                    ve += 2
                    ve_d = ve

                    @blk.scalar
                    def _(s, g=g, q=q, ve_d=ve_d):
                        s.wait_ge(VE, ve_d)
                        s.activation(dq[:], dq[:], AF.Exp, bias=0.0,
                                     scale=1.0).then_inc(SE, 1)
                    se += 1
                    se_l = se

                    v0 = ve

                    @blk.vector
                    def _(v, g=g, q=q, v0=v0, se_l=se_l, mm_cb=mm_cb):
                        v.wait_ge(SE, se_l)
                        v.wait_ge(MM, mm_cb)
                        cb = psCB[:, :].unsqueeze(1).broadcast_to((128, 16, 128))
                        v.tensor_tensor(sq[:, :].rearrange("p (h l) -> p h l", h=16),
                                        dq[:, :].rearrange("p (h l) -> p h l", h=16),
                                        cb, op=ALU.mult).then_inc(VE, 1)
                    ve += 1
                    ve_s = ve

                    @blk.tensor
                    def _(t, g=g, q=q, ve_s=ve_s):
                        t.wait_ge(VE, ve_s)
                        for k in range(4):
                            t.matmul(psQ[:, k * 512:(k + 1) * 512], ones1[:, :],
                                     eflat[0:1, k * 512:(k + 1) * 512],
                                     start=True, stop=True).then_inc(MM, 1)
                    mm += 4
                    mm_ea = mm

                    v0 = ve

                    @blk.vector
                    def _(v, g=g, q=q, v0=v0, mm_ea=mm_ea):
                        v.wait_ge(MM, mm_ea)
                        cf = Cfull[:, g * 128:(g + 1) * 128].unsqueeze(1).broadcast_to((128, 16, 128))
                        v.tensor_tensor(cdq[:, :].rearrange("p (h l) -> p h l", h=16),
                                        psQ[:, :].rearrange("p (h l) -> p h l", h=16),
                                        cf, op=ALU.mult).then_inc(VE, 1)
                    ve += 1
                    ve_cd = ve

                    for i in range(8):
                        pc = q * 8 + i
                        h0 = 16 * q + 2 * i
                        pp = i % 2

                        le = last_evac[pp]
                        ycol = 0 if pp == 0 else 1024

                        @blk.tensor
                        def _(t, g=g, q=q, i=i, pc=pc, h0=h0, pp=pp,
                              ve_cd=ve_cd, le=le, ycol=ycol):
                            t.wait_ge(VE, ve_cd)
                            if le is not None:
                                t.wait_ge(VE, le)
                            for hh in range(2):
                                h = h0 + hh
                                base = hh * 64
                                t.matmul(psQ[base:base + 64, ycol:ycol + 128],
                                         xdt[:, h * 64:(h + 1) * 64],
                                         sq[:, (h - 16 * q) * 128:(h - 16 * q + 1) * 128],
                                         start=True, stop=False)
                                t.matmul(psQ[base:base + 64, ycol:ycol + 64],
                                         snap0[:, h * 64:(h + 1) * 64],
                                         cdq[:, (h - 16 * q) * 128:(h - 16 * q) * 128 + 64],
                                         start=False, stop=False)
                                ins = t.matmul(psQ[base:base + 64, ycol + 64:ycol + 128],
                                               snap1[:, h * 64:(h + 1) * 64],
                                               cdq[:, (h - 16 * q) * 128 + 64:(h - 16 * q + 1) * 128],
                                               start=False, stop=True)
                            ins.then_inc(MM, 1)
                        mm += 1
                        mm_y = mm

                        @blk.vector
                        def _(v, g=g, pc=pc, pp=pp, mm_y=mm_y, ycol=ycol):
                            v.wait_ge(MM, mm_y)
                            v.scalar_tensor_tensor(yg[:, pc * TPC + g * 128:pc * TPC + (g + 1) * 128],
                                                   xcg[:, pc * 128:(pc + 1) * 128],
                                                   drep[:, pc:pc + 1],
                                                   psQ[:, ycol:ycol + 128],
                                                   op0=ALU.mult, op1=ALU.add).then_inc(VE, 1)
                        ve += 1
                        last_evac[pp] = ve
                    ve_q[q] = ve

                ve_grp[g] = ve

                @blk.vector
                def _(v, g=g, ve_w=ve):
                    v.wait_ge(VE, ve_w)
                    ygv = yg[:, :].rearrange("p (pc t) -> p pc t", pc=32)[:, :, g * 128:(g + 1) * 128]
                    szv = szg[:, :].rearrange("p (pc t) -> p pc t", pc=32)
                    v.tensor_tensor(ygv, ygv, szv, op=ALU.mult).then_inc(VE, 1)
                ve += 1
                ve_gate[g] = ve

            for gg in (psTb_g, psQ_g, psCB_g):
                gg.__exit__(None, None, None)
            for gg in reversed(_p2_guards):
                gg.__exit__(None, None, None)

            wsl_g = nc.sbuf_tensor("wsl", [128, 2 * 32 * 128], BF16)
            wsl = wsl_g.__enter__()
            s0b_g = nc.sbuf_tensor("s0b", [128, 2 * TPC], BF16)
            s0b = s0b_g.__enter__()
            psR_g = nc.psum_tensor("psR", [1, TPC], F32)
            psR = psR_g.__enter__()
            psBc_g = nc.psum_tensor("psBc", [128, TPC], F32)
            psBc = psBc_g.__enter__()
            psO_g = nc.psum_tensor("psO", [128, TPC], F32)
            psO = psO_g.__enter__()

            @blk.scalar
            def _(s):
                preload_set6(s)

            ve_all = ve_gate[7]
            mm_sq = [0] * 32
            for pc in range(32):
                @blk.scalar
                def _(s, pc=pc):
                    s.wait_ge(VE, ve_all)
                    if pc >= 1:
                        s.wait_ge(MM, mm_sq[pc - 1])
                    s.activation(ysq[:], yg[:, pc * TPC:(pc + 1) * TPC], AF.Square,
                                 bias=0.0, scale=1.0).then_inc(SE, 1)
                se += 1

                @blk.tensor
                def _(t, pc=pc, se_w=se):
                    t.wait_ge(SE, se_w)
                    for k in range(2):
                        t.matmul(psR[0:1, k * 512:(k + 1) * 512], onesb[:],
                                 ysq[:, k * 512:(k + 1) * 512],
                                 start=(pc == 0), stop=(pc == 31)).then_inc(MM, 1)
                mm += 2
                mm_sq[pc] = mm
            mm_r = mm

            v0 = ve

            @blk.vector
            def _(v, v0=v0, mm_r=mm_r):
                v.wait_ge(MM, mm_r)
                v.tensor_scalar(rr[0:1, 0:TPC], psR[0:1, :], scalar1=1.0 / INNER,
                                scalar2=None, op0=ALU.mult).then_inc(VE, 1)
            ve += 1
            ve_rr = ve

            @blk.scalar
            def _(s, ve_rr=ve_rr, se0=se):
                s.wait_ge(VE, ve_rr)
                s.activation(rr[0:1, 0:TPC], rr[0:1, 0:TPC], AF.Ln,
                             bias=cEPS_B2[0:1, 0:1], scale=1.0).then_inc(SE, 1)
                s.wait_ge(SE, se0 + 1)
                s.activation(rr[0:1, 0:TPC], rr[0:1, 0:TPC], AF.Exp,
                             bias=0.0, scale=-0.5).then_inc(SE, 1)
            se += 2
            se_rstd = se

            @blk.tensor
            def _(t, se_rstd=se_rstd):
                t.wait_ge(SE, se_rstd)
                for k in range(2):
                    t.matmul(psBc[:, k * 512:(k + 1) * 512], ones1[:, :],
                             rr[0:1, k * 512:(k + 1) * 512],
                             start=True, stop=True).then_inc(MM, 1)
            mm += 2
            mm_bc = mm

            for pc in range(32):
                @blk.vector
                def _(v, pc=pc, mm_bc=mm_bc):
                    v.wait_ge(MM, mm_bc)
                    v.scalar_tensor_tensor(yg[:, pc * TPC:(pc + 1) * TPC],
                                           yg[:, pc * TPC:(pc + 1) * TPC],
                                           nw[:, pc:pc + 1], psBc[:],
                                           op0=ALU.mult, op1=ALU.mult).then_inc(VE, 1)
            ve += 32
            ve_yn = ve

            so_s0 = [0] * 16
            mm_oc = [0] * 16
            se_oc = [0] * 16
            for oc in range(16):
                p = oc % 2

                @blk.gpsimd
                def _(gp, oc=oc, p=p):
                    if oc >= 2:
                        gp.wait_ge(MM, mm_oc[oc - 2])
                    src = i_woutT[:, oc * 128:(oc + 1) * 128]
                    gp.dma_start(out=wsl[:, p * 4096:(p + 1) * 4096],
                                 in_=src.rearrange("(kc q) m -> q kc m", q=128)).then_inc(WL, 16)
                wl += 16
                wl_oc = wl

                @blk.tensor
                def _(t, oc=oc, p=p, wl_oc=wl_oc):
                    t.wait_ge(WL, wl_oc)
                    t.wait_ge(VE, ve_yn)
                    if oc >= 1:
                        t.wait_ge(SE, se_oc[oc - 1])
                    for kc in range(32):
                        for ks in range(2):
                            ins = t.matmul(psO[:, ks * 512:(ks + 1) * 512],
                                           wsl[:, p * 4096 + kc * 128:p * 4096 + (kc + 1) * 128],
                                           yg[:, kc * TPC + ks * 512:kc * TPC + (ks + 1) * 512],
                                           start=(kc == 0), stop=(kc == 31))
                    ins.then_inc(MM, 1)
                mm += 1
                mm_oc[oc] = mm

                @blk.scalar
                def _(s, oc=oc, p=p, mm_w=mm):
                    s.wait_ge(MM, mm_w)
                    if oc >= 2:
                        s.wait_ge(SO, so_s0[oc - 2])
                    s.activation(s0b[:, p * TPC:(p + 1) * TPC], psO[:], AF.Copy,
                                 bias=0.0, scale=1.0).then_inc(SE, 1)
                se += 1
                se_oc[oc] = se

                @blk.sync
                def _(s, oc=oc, p=p, se_w=se):
                    s.wait_ge(SE, se_w)
                    s.dma_start(out=o_s0[oc * 128:(oc + 1) * 128, :],
                                in_=s0b[:, p * TPC:(p + 1) * TPC]).then_inc(SO, 16)
                so += 16
                so_s0[oc] = so

            @blk.sync
            def _(s):
                s.wait_ge(SO, so)

            psO_g.__exit__(None, None, None)
            psBc_g.__exit__(None, None, None)
            psR_g.__exit__(None, None, None)
            s0b_g.__exit__(None, None, None)
            wsl_g.__exit__(None, None, None)
    return nc


def mk_consts_b2(inputs):
    rep64 = np.zeros((64, 4096), np.float32)
    for h in range(64):
        rep64[h, h * 64:(h + 1) * 64] = 1.0
    ss = np.arange(128)[:, None]
    ll = np.arange(128)[None, :]
    madd = np.where((ss // 64 == ll // 64) & (ll >= ss), 0.0, -1e30).astype(np.float32)
    Dp = np.asarray(inputs["L0_attn_D_param"], np.float32)
    drep = np.zeros((128, 32), np.float32)
    for pc in range(32):
        drep[0:64, pc] = Dp[2 * pc]
        drep[64:128, pc] = Dp[2 * pc + 1]
    nwv = np.asarray(inputs["L0_attn_norm_w"], np.float32)
    normw = nwv.reshape(32, 128).T.copy()
    woutT = f32(np.asarray(inputs["L0_attn_W_out"]).T)
    return {"rep64": f32(rep64), "madd": f32(madd), "drep": drep,
            "normw": f32(normw), "woutT": woutT}


def host_B2(inputs, a_outs, b1_outs, consts, consts2):
    segs = np.concatenate([f32(b1_outs[c]["sego"]) for c in range(NCORES)], axis=0)
    lnl_all = np.concatenate([f32(b1_outs[c]["lnlo"]).reshape(1, 64)
                              for c in range(NCORES)], axis=0)
    maps = []
    for c in range(NCORES):
        r = a_outs[c]
        mm_ = np.zeros((8, 8), np.float32)
        wb = np.full((8, 1), -1e30, np.float32)
        for j in range(c):
            wb[j, 0] = 0.0
            for m in range(j + 1, c):
                mm_[j, m] = 1.0
        maps.append({
            "xconv": r["xconv"], "siluz": r["siluz"], "Bo": r["Bo"], "Co": r["Co"],
            "dto": r["dto"], "Adto": r["Adto"], "Acso": r["Acso"],
            "cso": f32(b1_outs[c]["cso"]), "segs": segs, "lnl_all": lnl_all,
            "mmT": f32(mm_.T), "wbias": wb,
            "rep64": consts2["rep64"], "madd": consts2["madd"],
            "drep": consts2["drep"], "normw": consts2["normw"],
            "eye64": consts["eye64"], "eye128b": consts["eye128b"],
            "woutT": consts2["woutT"],
        })
    return maps



CAP = 384


def build_C():
    import os
    PH = int(os.environ.get("C_PH", "9"))
    nc = bass.Bass("TRN2", target_bir_lowering=False)
    dram = nc.dram_tensor
    i_xnfm = dram("xn_fm", [D, TPC], BF16, kind="ExternalInput").ap()
    i_xntm = dram("xn_tm", [TPC, D], BF16, kind="ExternalInput").ap()
    i_rwT = dram("rwT", [D, 8], F32, kind="ExternalInput").ap()
    i_w1 = dram("w1", [8, D, 512], F32, kind="ExternalInput").ap()
    i_w2 = dram("w2", [8, 512, D], F32, kind="ExternalInput").ap()
    i_w3 = dram("w3", [8, D, 512], F32, kind="ExternalInput").ap()
    i_ufm = dram("u_fm", [D, TPC], F32, kind="ExternalInput").ap()
    i_s0 = dram("s0o", [D, TPC], BF16, kind="ExternalInput").ap()
    i_triu = dram("triu", [128, 128], F32, kind="ExternalInput").ap()
    i_eidx = dram("eidx", [128, 8], F32, kind="ExternalInput").ap()
    i_eye128b = dram("eye128b", [128, 128], BF16, kind="ExternalInput").ap()

    o_x1 = dram("x1o", [D, TPC], F32, kind="ExternalOutput").ap()
    o_dbg = dram("dbgslot", [128, 16], U32, kind="ExternalOutput").ap()

    gin = dram("gin", [8 * CAP, D], BF16)
    gout = dram("gout", [8 * CAP, D], BF16)

    with ExitStack() as ex:
        e = ex.enter_context
        xn = e(nc.sbuf_tensor("xn", [128, 16 * TPC], BF16))
        rwT = e(nc.sbuf_tensor("rwT_s", [128, 16 * 8], BF16))
        onesf = e(nc.sbuf_tensor("onesf", [128, 1], F32))
        triu = e(nc.sbuf_tensor("triu_s", [128, 128], F32))
        eidx = e(nc.sbuf_tensor("eidx_s", [128, 8], F32))
        ey128 = e(nc.sbuf_tensor("ey128", [128, 128], BF16))
        G = e(nc.sbuf_tensor("G", [128, 8 * 8], F32))
        E1 = e(nc.sbuf_tensor("E1", [128, 8], F32))
        E2 = e(nc.sbuf_tensor("E2", [128, 8], F32))
        P1 = e(nc.sbuf_tensor("P1", [128, 8], F32))
        P2 = e(nc.sbuf_tensor("P2", [128, 8], F32))
        scr = e(nc.sbuf_tensor("scr", [128, 32], F32))
        eall = e(nc.sbuf_tensor("eall", [128, 16], F32))
        inde = e(nc.sbuf_tensor("inde", [128, 16], F32))
        slote = e(nc.sbuf_tensor("slote", [128, 16], F32))
        slacc = e(nc.sbuf_tensor("slacc", [128, 16], F32))
        slotu = e(nc.sbuf_tensor("slotu", [128, 16], U32))
        colsum = e(nc.sbuf_tensor("colsum", [1, 48], F32))
        onesb = e(nc.sbuf_tensor("onesb", [128, 1], BF16))
        ones1 = e(nc.sbuf_tensor("ones1", [1, 128], F32))
        xtt = e(nc.sbuf_tensor("xtt", [128, 2 * D], BF16))
        xe = e(nc.sbuf_tensor("xe", [128, 16 * CAP], BF16))
        wup = e(nc.sbuf_tensor("wup", [128, 2 * 16 * 512], BF16))
        w2s = e(nc.sbuf_tensor("w2s", [128, 4 * D], BF16))
        h1s = e(nc.sbuf_tensor("h1s", [128, 4 * CAP], BF16))
        hs = e(nc.sbuf_tensor("hs", [128, 4 * CAP], BF16))
        dsb = e(nc.sbuf_tensor("dsb", [128, 512], BF16))
        g1b = e(nc.sbuf_tensor("g1b", [128, D], BF16))
        g2b = e(nc.sbuf_tensor("g2b", [128, D], BF16))
        cmb = e(nc.sbuf_tensor("cmb", [128, D], BF16))
        gfm = e(nc.sbuf_tensor("gfm", [128, 16 * TPC], BF16))
        usl = e(nc.sbuf_tensor("usl", [128, 2 * TPC], F32))
        s0s = e(nc.sbuf_tensor("s0s", [128, 2 * TPC], BF16))
        x1s = e(nc.sbuf_tensor("x1s", [128, 2 * TPC], F32))

        psG = e(nc.psum_tensor("psG", [128, 1024], F32))
        psU0 = e(nc.psum_tensor("psU0", [128, 384], F32))
        psU1 = e(nc.psum_tensor("psU1", [128, 384], F32))
        psD = e(nc.psum_tensor("psD", [128, 512], F32))
        psT = e(nc.psum_tensor("psT", [128, 128], BF16))

        LD = e(nc.semaphore("LD"))
        WL = e(nc.semaphore("WL"))
        MM = e(nc.semaphore("MM"))
        SE = e(nc.semaphore("SE"))
        VE = e(nc.semaphore("VE"))
        SO = e(nc.semaphore("SO"))
        GP = e(nc.semaphore("GP"))
        ld = wl = mm = se = ve = so = gp = 0

        with nc.Block() as blk:
            @blk.sync
            def _(s):
                s.dma_start(out=xn[:], in_=i_xnfm.rearrange("(kc p) t -> p kc t", p=128)).then_inc(LD, 16)
                s.dma_start(out=triu[:], in_=i_triu[:, :]).then_inc(LD, 16)
                s.dma_start(out=eidx[:], in_=i_eidx[:, :]).then_inc(LD, 16)
                s.dma_start(out=ey128[:], in_=i_eye128b[:, :]).then_inc(LD, 16)
            ld += 64

            @blk.gpsimd
            def _(g):
                g.dma_start(out=rwT[:], in_=i_rwT.rearrange("(kc p) e -> p kc e", p=128)).then_inc(LD, 16)
            ld += 16
            ld_p0 = ld

            @blk.vector
            def _(v):
                v.wait_ge(LD, ld_p0)
                v.memset(onesb[:], 1.0).then_inc(VE, 1)
                v.memset(ones1[:], 1.0).then_inc(VE, 1)
                v.memset(onesf[:], 1.0).then_inc(VE, 1)
            ve += 3
            ve_p0 = ve

            @blk.scalar
            def _(s):
                s.preload_act_table(AF.Sigmoid)

            ve_rt_tile = [0] * 8
            for tt in range(8 if PH >= 1 else 0):
                @blk.tensor
                def _(t, tt=tt):
                    t.wait_ge(LD, ld_p0)
                    if tt >= 2:
                        t.wait_ge(VE, ve_rt_tile[tt - 2])
                    for kc in range(16):
                        ins = t.matmul(psG[:, (tt % 2) * 512:(tt % 2) * 512 + 8],
                                       xn[:, kc * TPC + tt * 128:kc * TPC + (tt + 1) * 128],
                                       rwT[:, kc * 8:(kc + 1) * 8],
                                       start=(kc == 0), stop=(kc == 15))
                    ins.then_inc(MM, 1)
                mm += 1
                mm_g = mm

                v0 = ve

                @blk.vector
                def _(v, tt=tt, v0=v0, mm_g=mm_g):
                    p8 = (tt % 2) * 512
                    gg = G[:, tt * 8:(tt + 1) * 8]
                    v.wait_ge(MM, mm_g)
                    v.tensor_copy(gg, psG[:, p8:p8 + 8]).then_inc(VE, 1)
                    v.wait_ge(VE, v0 + 1)
                    m1 = scr[:, 0:1]
                    v.reduce_max(m1, gg, axis=AX.X).then_inc(VE, 1)
                    v.wait_ge(VE, v0 + 2)
                    i1 = scr[:, 2:10]
                    v.tensor_scalar(i1, gg, scalar1=m1, scalar2=None,
                                    op0=ALU.is_ge).then_inc(VE, 1)
                    v.wait_ge(VE, v0 + 3)
                    t1 = scr[:, 10:18]
                    v.tensor_tensor(t1, i1, eidx[:], op=ALU.mult).then_inc(VE, 1)
                    v.wait_ge(VE, v0 + 4)
                    v.reduce_max(E1[:, tt:tt + 1], t1, axis=AX.X).then_inc(VE, 1)
                    g2 = scr[:, 18:26]
                    v.wait_ge(VE, v0 + 5)
                    v.scalar_tensor_tensor(g2, i1, -1e30, gg,
                                           op0=ALU.mult, op1=ALU.add).then_inc(VE, 1)
                    v.wait_ge(VE, v0 + 6)
                    m2 = scr[:, 1:2]
                    v.reduce_max(m2, g2, axis=AX.X).then_inc(VE, 1)
                    v.wait_ge(VE, v0 + 7)
                    i2 = scr[:, 2:10]
                    v.tensor_scalar(i2, g2, scalar1=m2, scalar2=None,
                                    op0=ALU.is_ge).then_inc(VE, 1)
                    v.wait_ge(VE, v0 + 8)
                    v.tensor_tensor(t1, i2, eidx[:], op=ALU.mult).then_inc(VE, 1)
                    v.wait_ge(VE, v0 + 9)
                    v.reduce_max(E2[:, tt:tt + 1], t1, axis=AX.X).then_inc(VE, 1)
                    v.wait_ge(VE, v0 + 10)
                    v.tensor_tensor(scr[:, 0:1], m1, m2, op=ALU.subtract).then_inc(VE, 1)
                ve += 11

                @blk.scalar
                def _(s, tt=tt, ve_w=ve):
                    s.wait_ge(VE, ve_w)
                    s.activation(P1[:, tt:tt + 1], scr[:, 0:1], AF.Sigmoid,
                                 bias=0.0, scale=1.0).then_inc(SE, 1)
                se += 1

                @blk.vector
                def _(v, tt=tt, se_w=se):
                    v.wait_ge(SE, se_w)
                    v.tensor_scalar(P2[:, tt:tt + 1], P1[:, tt:tt + 1],
                                    scalar1=-1.0, scalar2=1.0, op0=ALU.mult,
                                    op1=ALU.add).then_inc(VE, 1)
                ve += 1
                ve_rt_tile[tt] = ve
            ve_rt = ve

            SUB = int(os.environ.get("C_SUB", "9"))

            @blk.vector
            def _(v):
                v.wait_ge(VE, ve_rt)
                v.tensor_copy(eall[:, 0:8], E1[:]).then_inc(VE, 1)
                v.tensor_copy(eall[:, 8:16], E2[:]).then_inc(VE, 1)
            ve += 2
            ve_ea = ve

            ve_sl = [ve] * 8
            for ee in range(8 if (PH >= 1 and SUB >= 2) else 0):
                @blk.vector
                def _(v, ee=ee):
                    v.wait_ge(VE, ve_ea)
                    if ee >= 1:
                        v.wait_ge(VE, ve_sl[ee - 1])
                    v.tensor_scalar(inde[:], eall[:], scalar1=float(ee), scalar2=None,
                                    op0=ALU.is_equal).then_inc(VE, 1)
                ve += 1
                ve_ind = ve

                @blk.tensor
                def _(t, ee=ee, ve_ind=ve_ind):
                    t.wait_ge(VE, ve_ind)
                    t.matmul(psG[:, 16:32], triu[:], inde[:],
                             start=True, stop=True)
                    t.matmul(psG[0:1, 32:48], onesf[:], inde[:],
                             start=True, stop=True).then_inc(MM, 1)
                mm += 1
                mm_tri = mm

                v0 = ve

                @blk.vector
                def _(v, ee=ee, v0=v0, mm_tri=mm_tri):
                    v.wait_ge(MM, mm_tri)
                    v.tensor_copy(colsum[0:1, 0:16], psG[0:1, 32:48]).then_inc(VE, 1)
                    v.memset(colsum[0:1, 32:48], 1.0).then_inc(VE, 1)
                    v.wait_ge(VE, v0 + 2)
                    v.tensor_tensor_scan(colsum[0:1, 16:32], colsum[0:1, 32:48],
                                         colsum[0:1, 0:16], initial=0.0,
                                         op0=ALU.mult, op1=ALU.add).then_inc(VE, 1)
                    v.wait_ge(VE, v0 + 3)
                    v.tensor_tensor(colsum[0:1, 16:32], colsum[0:1, 16:32],
                                    colsum[0:1, 0:16], op=ALU.subtract).then_inc(VE, 1)
                ve += 4
                ve_cs = ve

                @blk.tensor
                def _(t, ee=ee, ve_cs=ve_cs):
                    t.wait_ge(VE, ve_cs)
                    t.matmul(psG[:, 48:64], ones1[:, :], colsum[0:1, 16:32],
                             start=True, stop=True).then_inc(MM, 1)
                mm += 1
                mm_cb = mm

                v0 = ve

                @blk.vector
                def _(v, ee=ee, v0=v0, mm_cb=mm_cb):
                    v.wait_ge(MM, mm_cb)
                    v.tensor_copy(scr[:, 0:16], psG[:, 16:32]).then_inc(VE, 1)
                    v.wait_ge(VE, v0 + 1)
                    v.tensor_tensor(slote[:], scr[:, 0:16], psG[:, 48:64],
                                    op=ALU.add).then_inc(VE, 1)
                    v.wait_ge(VE, v0 + 2)
                    v.tensor_scalar(scr[:, 0:16], slote[:], scalar1=float(CAP) - 0.5,
                                    scalar2=None, op0=ALU.is_ge).then_inc(VE, 1)
                    v.wait_ge(VE, v0 + 3)
                    v.scalar_tensor_tensor(slote[:], scr[:, 0:16], 1e9, slote[:],
                                           op0=ALU.mult, op1=ALU.add).then_inc(VE, 1)
                    v.wait_ge(VE, v0 + 4)
                    v.tensor_scalar(slote[:], slote[:], scalar1=float(CAP * ee),
                                    scalar2=None, op0=ALU.add).then_inc(VE, 1)
                    v.wait_ge(VE, v0 + 5)
                    v.tensor_tensor(inde[:], inde[:], slote[:], op=ALU.mult).then_inc(VE, 1)
                    v.wait_ge(VE, v0 + 6)
                    if ee == 0:
                        v.tensor_copy(slacc[:], inde[:]).then_inc(VE, 1)
                    else:
                        v.tensor_tensor(slacc[:], slacc[:], inde[:],
                                        op=ALU.add).then_inc(VE, 1)
                ve += 7
                ve_sl[ee] = ve
            ve_slots = ve

            @blk.vector
            def _(v):
                v.wait_ge(VE, ve_slots)
                if SUB >= 3:
                    v.tensor_copy(slotu[:], slacc[:]).then_inc(VE, 1)
                else:
                    v.engine_nop().then_inc(VE, 1)
            ve += 1
            ve_slotu = ve

            @blk.sync
            def _(s):
                s.wait_ge(VE, ve_slotu)
                s.dma_start(out=o_dbg[:, :], in_=slotu[:]).then_inc(SO, 16)
            so += 16
            so_slotflush = so

            ld_tts = [0] * 8
            for tt in range(8 if PH >= 2 else 0):
                @blk.sync
                def _(s, tt=tt):
                    if tt >= 2:
                        s.wait_ge(GP, 32 * (tt - 1))
                    s.dma_start(out=xtt[:, (tt % 2) * D:(tt % 2 + 1) * D],
                                in_=i_xntm[tt * 128:(tt + 1) * 128, :]).then_inc(LD, 16)
                ld += 16
                ld_tts[tt] = ld

                for ch in range(2):
                    @blk.gpsimd
                    def _(g, tt=tt, ch=ch, gp0=gp):
                        g.wait_ge(LD, ld_tts[tt])
                        g.wait_ge(SO, so_slotflush)
                        g.wait_ge(GP, gp0)
                        col = ch * 8 + tt
                        g.indirect_dma_start(
                            out=gin.ap()[:, :],
                            in_=xtt[:, (tt % 2) * D:(tt % 2 + 1) * D],
                            out_offset=bass.IndirectOffsetOnAxis(ap=slotu[:, col:col + 1], axis=0),
                            in_offset=None,
                            bounds_check=8 * CAP - 1, oob_is_err=False,
                        ).then_inc(GP, 16)
                    gp += 16
            gp_scatter = gp

            @blk.scalar
            def _(s):
                s.preload_act_table(AF.Silu)

            mm_dn_last = 0
            se_dn_last = 0
            for ee in range(8 if PH >= 3 else 0):
                @blk.sync
                def _(s, ee=ee, mm_w=mm_dn_last):
                    s.wait_ge(GP, gp_scatter)
                    if ee >= 1:
                        s.wait_ge(MM, mm_w)
                    for kc in range(16):
                        s.dma_start(out=xe[:, kc * CAP:(kc + 1) * CAP],
                                    in_=gin.ap()[ee * CAP:(ee + 1) * CAP,
                                                 kc * 128:(kc + 1) * 128],
                                    transpose=True).then_inc(LD, 16)
                ld += 256
                ld_xe = ld

                @blk.gpsimd
                def _(g, ee=ee, mm_w=mm_dn_last):
                    if ee >= 1:
                        g.wait_ge(MM, mm_w)
                    g.dma_start(out=wup[:, 0:8192],
                                in_=i_w1[ee].rearrange("(kc p) f -> p kc f", p=128)).then_inc(WL, 16)
                    g.dma_start(out=wup[:, 8192:16384],
                                in_=i_w3[ee].rearrange("(kc p) f -> p kc f", p=128)).then_inc(WL, 16)
                    g.dma_start(out=w2s[:, :],
                                in_=i_w2[ee].rearrange("(fc p) d -> p fc d", p=128)).then_inc(WL, 16)
                wl += 48
                wl_e = wl

                mm_h1 = [0] * 4
                mm_h3 = [0] * 4
                se_h1 = [0] * 4
                ve_hh = [0] * 4
                for fc in range(4):
                    @blk.tensor
                    def _(t, ee=ee, fc=fc, ld_xe=ld_xe, wl_e=wl_e):
                        t.wait_ge(LD, ld_xe)
                        t.wait_ge(WL, wl_e)
                        if fc >= 1:
                            t.wait_ge(SE, se_h1[fc - 1])
                            t.wait_ge(VE, ve_hh[fc - 1])
                        for kc in range(16):
                            ins = t.matmul(psU0[:, 0:CAP],
                                           wup[:, kc * 512 + fc * 128:kc * 512 + (fc + 1) * 128],
                                           xe[:, kc * CAP:(kc + 1) * CAP],
                                           start=(kc == 0), stop=(kc == 15))
                        ins.then_inc(MM, 1)
                        for kc in range(16):
                            ins = t.matmul(psU1[:, 0:CAP],
                                           wup[:, 8192 + kc * 512 + fc * 128:8192 + kc * 512 + (fc + 1) * 128],
                                           xe[:, kc * CAP:(kc + 1) * CAP],
                                           start=(kc == 0), stop=(kc == 15))
                        ins.then_inc(MM, 1)
                    mm += 2
                    mm_h1[fc] = mm - 1
                    mm_h3[fc] = mm

                    @blk.scalar
                    def _(s, ee=ee, fc=fc, mm_w=mm_h1[fc]):
                        s.wait_ge(MM, mm_w)
                        s.activation(h1s[:, fc * CAP:(fc + 1) * CAP], psU0[:, 0:CAP],
                                     AF.Silu, bias=0.0, scale=1.0).then_inc(SE, 1)
                    se += 1
                    se_h1[fc] = se

                    @blk.vector
                    def _(v, ee=ee, fc=fc, se_w=se, mm_w=mm_h3[fc]):
                        v.wait_ge(SE, se_w)
                        v.wait_ge(MM, mm_w)
                        v.tensor_tensor(hs[:, fc * CAP:(fc + 1) * CAP],
                                        h1s[:, fc * CAP:(fc + 1) * CAP],
                                        psU1[:, 0:CAP], op=ALU.mult).then_inc(VE, 1)
                    ve += 1
                    ve_hh[fc] = ve
                ve_h = ve

                for sc in range(3):
                    for dg in range(4):
                        @blk.tensor
                        def _(t, ee=ee, sc=sc, dg=dg, ve_h=ve_h, se_w=se_dn_last):
                            t.wait_ge(VE, ve_h)
                            if se_w:
                                t.wait_ge(SE, se_w)
                            for fc in range(4):
                                ins = t.matmul(psD[:, 0:512],
                                               hs[:, fc * CAP + sc * 128:fc * CAP + sc * 128 + 128],
                                               w2s[:, fc * D + dg * 512:fc * D + (dg + 1) * 512],
                                               start=(fc == 0), stop=(fc == 3))
                            ins.then_inc(MM, 1)
                        mm += 1
                        mm_dn = mm

                        @blk.scalar
                        def _(s, ee=ee, sc=sc, dg=dg, mm_w=mm_dn, so_w=so - 16):
                            s.wait_ge(MM, mm_w)
                            if so_w >= 0:
                                s.wait_ge(SO, so_w + 16)
                            s.activation(dsb[:, 0:512], psD[:, 0:512], AF.Copy,
                                         bias=0.0, scale=1.0).then_inc(SE, 1)
                        se += 1
                        se_dn_last = se

                        @blk.sync
                        def _(s, ee=ee, sc=sc, dg=dg, se_w=se):
                            s.wait_ge(SE, se_w)
                            s.dma_start(out=gout.ap()[ee * CAP + sc * 128:ee * CAP + (sc + 1) * 128,
                                                      dg * 512:(dg + 1) * 512],
                                        in_=dsb[:, 0:512]).then_inc(SO, 16)
                        so += 16
                mm_dn_last = mm
            so_experts = so

            mm_tr_last = 0
            for tt in range(8 if PH >= 4 else 0):
                v0 = ve

                @blk.vector
                def _(v, tt=tt, v0=v0, mm_w=mm_tr_last):
                    v.wait_ge(VE, v0)
                    if mm_w:
                        v.wait_ge(MM, mm_w)
                    v.memset(g1b[:], 0.0).then_inc(VE, 1)
                    v.memset(g2b[:], 0.0).then_inc(VE, 1)
                ve += 2
                ve_z = ve

                @blk.gpsimd
                def _(g, tt=tt, ve_z=ve_z, gp0=gp):
                    g.wait_ge(SO, so_experts)
                    g.wait_ge(SO, so_slotflush)
                    g.wait_ge(VE, ve_z)
                    g.wait_ge(GP, gp0)
                    g.indirect_dma_start(
                        out=g1b[:], in_=gout.ap()[:, :],
                        in_offset=bass.IndirectOffsetOnAxis(ap=slotu[:, tt:tt + 1], axis=0),
                        out_offset=None,
                        bounds_check=8 * CAP - 1, oob_is_err=False,
                    ).then_inc(GP, 16)
                    g.wait_ge(GP, gp0 + 16)
                    g.indirect_dma_start(
                        out=g2b[:], in_=gout.ap()[:, :],
                        in_offset=bass.IndirectOffsetOnAxis(ap=slotu[:, 8 + tt:8 + tt + 1], axis=0),
                        out_offset=None,
                        bounds_check=8 * CAP - 1, oob_is_err=False,
                    ).then_inc(GP, 16)
                gp += 32
                gp_gb = gp

                v0 = ve

                @blk.vector
                def _(v, tt=tt, v0=v0, gp_gb=gp_gb):
                    v.wait_ge(GP, gp_gb)
                    v.tensor_scalar(cmb[:], g1b[:], scalar1=P1[:, tt:tt + 1],
                                    scalar2=None, op0=ALU.mult).then_inc(VE, 1)
                    v.wait_ge(VE, v0 + 1)
                    v.scalar_tensor_tensor(cmb[:], g2b[:], P2[:, tt:tt + 1], cmb[:],
                                           op0=ALU.mult, op1=ALU.add).then_inc(VE, 1)
                ve += 2
                ve_cmb = ve

                mm0 = mm

                @blk.tensor
                def _(t, tt=tt, ve_cmb=ve_cmb, mm0=mm0, se_w=se):
                    t.wait_ge(VE, ve_cmb)
                    for kc in range(16):
                        if kc > 0:
                            t.wait_ge(SE, se_w + kc)
                        elif tt > 0:
                            t.wait_ge(SE, se_w)
                        t.transpose(psT[:], cmb[:, kc * 128:(kc + 1) * 128],
                                    ey128[:]).then_inc(MM, 1)
                mm += 16
                mm_tr_last = mm

                for kc in range(16):
                    @blk.scalar
                    def _(s, tt=tt, kc=kc, mm_w=mm0 + kc + 1):
                        s.wait_ge(MM, mm_w)
                        s.activation(gfm[:, kc * TPC + tt * 128:kc * TPC + (tt + 1) * 128],
                                     psT[:], AF.Silu, bias=0.0, scale=1.0).then_inc(SE, 1)
                    se += 1
            se_gfm = se

            so_x1 = [0] * 16
            ve_x1 = [0] * 16
            for kc in range(16 if PH >= 5 else 0):
                p = kc % 2

                @blk.sync
                def _(s, kc=kc, p=p):
                    if kc >= 2:
                        s.wait_ge(VE, ve_x1[kc - 2])
                    s.dma_start(out=usl[:, p * TPC:(p + 1) * TPC],
                                in_=i_ufm[kc * 128:(kc + 1) * 128, :]).then_inc(LD, 16)
                    s.dma_start(out=s0s[:, p * TPC:(p + 1) * TPC],
                                in_=i_s0[kc * 128:(kc + 1) * 128, :]).then_inc(LD, 16)
                ld += 32
                ld_kc = ld

                v0 = ve

                @blk.vector
                def _(v, kc=kc, p=p, ld_kc=ld_kc, v0=v0):
                    v.wait_ge(LD, ld_kc)
                    v.wait_ge(SE, se_gfm)
                    if kc >= 2:
                        v.wait_ge(SO, so_x1[kc - 2])
                    v.tensor_tensor(x1s[:, p * TPC:(p + 1) * TPC],
                                    s0s[:, p * TPC:(p + 1) * TPC],
                                    gfm[:, kc * TPC:(kc + 1) * TPC],
                                    op=ALU.mult).then_inc(VE, 1)
                    v.wait_ge(VE, v0 + 1)
                    v.tensor_tensor(x1s[:, p * TPC:(p + 1) * TPC],
                                    x1s[:, p * TPC:(p + 1) * TPC],
                                    usl[:, p * TPC:(p + 1) * TPC],
                                    op=ALU.add).then_inc(VE, 1)
                ve += 2
                ve_x1[kc] = ve

                @blk.sync
                def _(s, kc=kc, p=p, ve_w=ve):
                    s.wait_ge(VE, ve_w)
                    s.dma_start(out=o_x1[kc * 128:(kc + 1) * 128, :],
                                in_=x1s[:, p * TPC:(p + 1) * TPC]).then_inc(SO, 16)
                so += 16
                so_x1[kc] = so

            @blk.sync
            def _(s):
                s.wait_ge(SO, so)
    return nc


def host_C(inputs, a_outs, b2_outs, consts, u_fm_full):
    rwT = f32(np.asarray(inputs["L0_ffn_router_w"]).T)
    w1 = f32(inputs["L0_ffn_w1"])
    w2 = f32(inputs["L0_ffn_w2"])
    w3 = f32(inputs["L0_ffn_w3"])
    triu = f32(np.triu(np.ones((128, 128), np.float32), 1))
    eidx = f32(np.tile(np.arange(8, dtype=np.float32), (128, 1)))
    maps = []
    for c in range(NCORES):
        maps.append({
            "xn_fm": a_outs[c]["xn_fm"], "xn_tm": a_outs[c]["xn_tm"],
            "rwT": rwT, "w1": w1, "w2": w2, "w3": w3,
            "u_fm": f32(u_fm_full[:, c * TPC:(c + 1) * TPC]),
            "s0o": b2_outs[c]["s0o"],
            "triu": triu, "eidx": eidx, "eye128b": consts["eye128b"],
        })
    return maps



def build_D():
    import os
    PH = int(os.environ.get("D_PH", "9"))
    nc = bass.Bass("TRN2", target_bir_lowering=False)
    dram = nc.dram_tensor
    i_x1 = dram("x1o", [D, TPC], F32, kind="ExternalInput").ap()
    i_wqkvT = dram("wqkvT", [D, 3 * D], F32, kind="ExternalInput").ap()
    i_wgateT = dram("wgateT", [D, D], F32, kind="ExternalInput").ap()
    i_wqkv2T = dram("wqkv2T", [D, 3 * D], F32, kind="ExternalInput").ap()
    i_wg2T = dram("wg2T", [D, 16], F32, kind="ExternalInput").ap()
    i_slope = dram("slope", [16, 1], F32, kind="ExternalInput").ap()
    i_dmask = dram("dmask", [128, 512], F32, kind="ExternalInput").ap()
    i_armask = dram("armask", [1, TPC], F32, kind="ExternalInput").ap()
    i_kmask = dram("kmask", [128, 2], F32, kind="ExternalInput").ap()
    i_eye128b = dram("eye128b", [128, 128], BF16, kind="ExternalInput").ap()
    i_eye16 = dram("eye16", [16, 16], F32, kind="ExternalInput").ap()

    o_attn = dram("attno", [D, TPC], BF16, kind="ExternalOutput").ap()
    o_qd = dram("qdo", [D, TPC], BF16, kind="ExternalOutput").ap()
    o_kvseg = dram("kvsego", [128, 16 * 128], F32, kind="ExternalOutput").ap()
    o_gate = dram("gateo", [D, TPC], BF16, kind="ExternalOutput").ap()
    o_q2 = dram("q2o", [D, TPC], BF16, kind="ExternalOutput").ap()
    o_ksh = dram("ksho", [128, TPC], BF16, kind="ExternalOutput").ap()
    o_vsh = dram("vsho", [TPC, 128], BF16, kind="ExternalOutput").ap()

    with ExitStack() as ex:
        e = ex.enter_context
        ey128 = e(nc.sbuf_tensor("ey128", [128, 128], BF16))
        ey16 = e(nc.sbuf_tensor("ey16", [16, 16], F32))
        slp = e(nc.sbuf_tensor("slp", [16, 1], F32))
        dmask = e(nc.sbuf_tensor("dmask_s", [128, 512], F32))
        armask = e(nc.sbuf_tensor("armask_s", [1, TPC], F32))
        kmask = e(nc.sbuf_tensor("kmask_s", [128, 2], F32))
        snegs = e(nc.sbuf_tensor("snegs", [128, 16], F32))
        bdf = e(nc.sbuf_tensor("bdf", [128, 16], F32))
        x1f = e(nc.sbuf_tensor("x1f", [128, 16 * TPC], F32))
        xsq = e(nc.sbuf_tensor("xsq", [128, TPC], F32))
        rowA = e(nc.sbuf_tensor("rowA", [1, TPC], F32))
        rowB = e(nc.sbuf_tensor("rowB", [1, TPC], F32))
        abc = e(nc.sbuf_tensor("abc", [128, 2 * TPC], F32))
        xn1 = e(nc.sbuf_tensor("xn1", [128, 16 * TPC], BF16))
        ones1 = e(nc.sbuf_tensor("ones1", [1, 128], F32))
        onesf = e(nc.sbuf_tensor("onesf", [128, 1], F32))
        wsl = e(nc.sbuf_tensor("wsl", [128, 2 * 16 * 128], BF16))
        qh = e(nc.sbuf_tensor("qh", [128, TPC], BF16))
        kh = e(nc.sbuf_tensor("kh", [128, TPC], BF16))
        vh = e(nc.sbuf_tensor("vh", [128, TPC], BF16))
        ktm = e(nc.sbuf_tensor("ktm", [128, TPC], BF16))
        vtm = e(nc.sbuf_tensor("vtm", [128, TPC], BF16))
        ksc = e(nc.sbuf_tensor("ksc", [128, TPC], BF16))
        qd = e(nc.sbuf_tensor("qd", [128, TPC], BF16))
        dth = e(nc.sbuf_tensor("dth", [128, 512], BF16))
        kdc = e(nc.sbuf_tensor("kdc", [128, 2], F32))
        qdrow = e(nc.sbuf_tensor("qdrow", [1, TPC], F32))
        pb = e(nc.sbuf_tensor("pb", [128, 512], BF16))
        kvst = e(nc.sbuf_tensor("kvst", [128, 128], F32))
        kvbf = e(nc.sbuf_tensor("kvbf", [128, 128], BF16))
        aout = e(nc.sbuf_tensor("aout", [128, TPC], BF16))
        G2 = e(nc.sbuf_tensor("G2", [128, 8 * 16], F32))
        IND = e(nc.sbuf_tensor("IND", [128, 8 * 16], BF16))
        T1V = e(nc.sbuf_tensor("T1V", [128, 8], F32))
        kacc = e(nc.sbuf_tensor("kacc", [128, 8 * 128], F32))
        vacc = e(nc.sbuf_tensor("vacc", [128, 8 * 128], F32))
        shtm = e(nc.sbuf_tensor("shtm", [128, 8 * 128], BF16))
        gsl = e(nc.sbuf_tensor("gsl", [128, 2 * TPC], BF16))

        psM = e(nc.psum_tensor("psM", [128, 512], F32))
        psP0 = e(nc.psum_tensor("psP0", [128, TPC], F32))
        psP1 = e(nc.psum_tensor("psP1", [128, TPC], F32))
        psQK = e(nc.psum_tensor("psQK", [128, 1024], F32))
        psT = e(nc.psum_tensor("psT", [128, 128], BF16))
        psIN = psM[:, 128:384]

        LD = e(nc.semaphore("LD"))
        WL = e(nc.semaphore("WL"))
        MM = e(nc.semaphore("MM"))
        SE = e(nc.semaphore("SE"))
        VE = e(nc.semaphore("VE"))
        SO = e(nc.semaphore("SO"))
        ld = wl = mm = se = ve = so = 0
        psP = [psP0, psP1]

        with nc.Block() as blk:
            @blk.sync
            def _(s):
                for (dst, src) in ((ey128, i_eye128b), (ey16, i_eye16),
                                   (slp, i_slope), (dmask, i_dmask),
                                   (armask, i_armask), (kmask, i_kmask)):
                    s.dma_start(out=dst[:], in_=src[:, :]).then_inc(LD, 16)
                s.dma_start(out=x1f[:], in_=i_x1.rearrange("(kc p) t -> p kc t", p=128)).then_inc(LD, 16)
            ld += 7 * 16
            ld_p0 = ld

            @blk.vector
            def _(v):
                v.wait_ge(LD, ld_p0)
                v.memset(ones1[:], 1.0).then_inc(VE, 1)
                v.memset(onesf[:], 1.0).then_inc(VE, 1)
            ve += 2
            ve_p0 = ve

            SLP = int(os.environ.get("D_SLP", "9"))

            @blk.tensor
            def _(t):
                t.wait_ge(VE, ve_p0)
                if SLP >= 1:
                    t.transpose(psM[0:1, 0:16], slp[:], ey16[:]).then_inc(MM, 1)
                else:
                    t.nop().then_inc(MM, 1)
            mm += 1
            mm_st = mm

            @blk.vector
            def _(v):
                v.wait_ge(MM, mm_st)
                if SLP >= 2:
                    v.tensor_copy(rowA[0:1, 0:16], psM[0:1, 0:16]).then_inc(VE, 1)
                else:
                    v.engine_nop().then_inc(VE, 1)
            ve += 1
            ve_srow = ve

            @blk.tensor
            def _(t):
                t.wait_ge(VE, ve_srow)
                if SLP >= 3:
                    t.matmul(psM[:, 16:32], ones1[:, :], rowA[0:1, 0:16],
                             start=True, stop=True).then_inc(MM, 1)
                else:
                    t.nop().then_inc(MM, 1)
            mm += 1
            mm_sb = mm

            @blk.vector
            def _(v):
                v.wait_ge(MM, mm_sb)
                v.tensor_scalar(snegs[:], psM[:, 16:32], scalar1=-1.0, scalar2=None,
                                op0=ALU.mult).then_inc(VE, 1)
            ve += 1
            ve_sn = ve

            @blk.scalar
            def _(s):
                preload_set6(s)
                s.wait_ge(VE, ve_sn)
                s.activation(bdf[:], snegs[:], AF.Exp, bias=0.0, scale=256.0).then_inc(SE, 1)
            se += 1
            se_bdf = se

            LNG = int(os.environ.get("D_LN", "9"))
            mm_ln = mm
            se_sq = [0] * 16
            for kc in range(16 if LNG >= 1 else 0):
                @blk.scalar
                def _(s, kc=kc, mm_w=mm_ln):
                    s.wait_ge(LD, ld_p0)
                    if kc >= 1:
                        s.wait_ge(MM, mm_ln + kc)
                    s.activation(xsq[:], x1f[:, kc * TPC:(kc + 1) * TPC], AF.Square,
                                 bias=0.0, scale=1.0).then_inc(SE, 1)
                se += 1
                se_sq[kc] = se

                @blk.tensor
                def _(t, kc=kc, se_w=se):
                    t.wait_ge(SE, se_w)
                    t.wait_ge(SE, se_bdf)
                    ins = None
                    for ks in range(2):
                        t.matmul(psP0[0:1, ks * 512:(ks + 1) * 512],
                                 onesf[:], xsq[:, ks * 512:(ks + 1) * 512],
                                 start=(kc == 0), stop=(kc == 15))
                        ins = t.matmul(psP1[0:1, ks * 512:(ks + 1) * 512],
                                       onesf[:], x1f[:, kc * TPC + ks * 512:kc * TPC + ks * 512 + 512],
                                       start=(kc == 0), stop=(kc == 15))
                    ins.then_inc(MM, 1)
                mm += 1
            mm_stats = mm

            v0 = ve

            @blk.vector
            def _(v, v0=v0):
                v.wait_ge(MM, mm_stats)
                if LNG < 2:
                    v.engine_nop().then_inc(VE, 1)
                    v.engine_nop().then_inc(VE, 1)
                    v.engine_nop().then_inc(VE, 1)
                    v.engine_nop().then_inc(VE, 1)
                    v.engine_nop().then_inc(VE, 1)
                    return
                v.tensor_scalar(rowA[0:1, :], psP1[0:1, 0:TPC], scalar1=1.0 / D,
                                scalar2=None, op0=ALU.mult).then_inc(VE, 1)
                v.wait_ge(VE, v0 + 1)
                v.tensor_scalar(rowB[0:1, :], psP0[0:1, 0:TPC], scalar1=1.0 / D,
                                scalar2=None, op0=ALU.mult).then_inc(VE, 1)
                v.wait_ge(VE, v0 + 2)
                v.tensor_tensor(xsq[0:1, 0:TPC], rowA[0:1, :], rowA[0:1, :],
                                op=ALU.mult).then_inc(VE, 1)
                v.wait_ge(VE, v0 + 3)
                v.tensor_tensor(rowB[0:1, :], rowB[0:1, :], xsq[0:1, 0:TPC],
                                op=ALU.subtract).then_inc(VE, 1)
                v.memset(xsq[0:1, 0:1], 1e-5).then_inc(VE, 1)
            ve += 5
            ve_var = ve

            @blk.scalar
            def _(s, se0=se):
                s.wait_ge(VE, ve_var)
                if LNG < 3:
                    s.nop().then_inc(SE, 1)
                    s.nop().then_inc(SE, 1)
                    return
                s.activation(rowB[0:1, :], rowB[0:1, :], AF.Ln,
                             bias=xsq[0:1, 0:1], scale=1.0).then_inc(SE, 1)
                s.wait_ge(SE, se0 + 1)
                s.activation(rowB[0:1, :], rowB[0:1, :], AF.Exp,
                             bias=0.0, scale=-0.5).then_inc(SE, 1)
            se += 2
            se_rstd = se

            v0 = ve

            @blk.vector
            def _(v, v0=v0):
                v.wait_ge(SE, se_rstd)
                v.tensor_tensor(rowA[0:1, :], rowA[0:1, :], rowB[0:1, :],
                                op=ALU.mult).then_inc(VE, 1)
                v.wait_ge(VE, v0 + 1)
                v.tensor_scalar(rowA[0:1, :], rowA[0:1, :], scalar1=-1.0,
                                scalar2=None, op0=ALU.mult).then_inc(VE, 1)
            ve += 2
            ve_ab = ve

            @blk.tensor
            def _(t):
                t.wait_ge(VE, ve_ab)
                ins = None
                for ks in range(2):
                    t.matmul(psP0[:, ks * 512:(ks + 1) * 512], ones1[:, :],
                             rowB[0:1, ks * 512:(ks + 1) * 512],
                             start=True, stop=True)
                    ins = t.matmul(psP1[:, ks * 512:(ks + 1) * 512], ones1[:, :],
                                   rowA[0:1, ks * 512:(ks + 1) * 512],
                                   start=True, stop=True)
                ins.then_inc(MM, 1)
            mm += 1
            mm_ab = mm

            v0 = ve

            @blk.vector
            def _(v, v0=v0):
                v.wait_ge(MM, mm_ab)
                v.tensor_copy(abc[:, 0:TPC], psP0[:, 0:TPC]).then_inc(VE, 1)
                v.tensor_copy(abc[:, TPC:2 * TPC], psP1[:, 0:TPC]).then_inc(VE, 1)
            ve += 2
            ve_abc = ve

            ve_xn = [0] * 16
            for kc in range(16):
                v0 = ve

                @blk.vector
                def _(v, kc=kc, v0=v0):
                    v.wait_ge(VE, ve_abc)
                    v.tensor_tensor(xn1[:, kc * TPC:(kc + 1) * TPC],
                                    x1f[:, kc * TPC:(kc + 1) * TPC],
                                    abc[:, 0:TPC], op=ALU.mult).then_inc(VE, 1)
                    v.wait_ge(VE, v0 + 1)
                    v.tensor_tensor(xn1[:, kc * TPC:(kc + 1) * TPC],
                                    xn1[:, kc * TPC:(kc + 1) * TPC],
                                    abc[:, TPC:2 * TPC], op=ALU.add).then_inc(VE, 1)
                ve += 2
                ve_xn[kc] = ve
            ve_xn1 = ve

            pstate = {"n": 0, "se0": 0, "se1": 0, "mmw": {0: 0, 1: 0}}

            def proj_oc(wT, oc):
                nonlocal ld, wl, mm
                pp = pstate["n"] % 2
                pstate["n"] += 1
                se_free = pstate["se0"] if pp == 0 else pstate["se1"]
                mm_prev = pstate["mmw"][pp]

                @blk.gpsimd
                def _(g, oc=oc, pp=pp, wT=wT, mm_prev=mm_prev):
                    if mm_prev:
                        g.wait_ge(MM, mm_prev)
                    src = wT[:, oc * 128:(oc + 1) * 128]
                    g.dma_start(out=wsl[:, pp * 2048:(pp + 1) * 2048],
                                in_=src.rearrange("(kc p) m -> p kc m", p=128)).then_inc(WL, 16)
                wl += 16
                wl_w = wl

                @blk.tensor
                def _(t, oc=oc, pp=pp, wl_w=wl_w, se_free=se_free):
                    t.wait_ge(WL, wl_w)
                    t.wait_ge(VE, ve_xn1)
                    if se_free:
                        t.wait_ge(SE, se_free)
                    for kc in range(16):
                        for ks in range(2):
                            ins = t.matmul(psP[pp][:, ks * 512:(ks + 1) * 512],
                                           wsl[:, pp * 2048 + kc * 128:pp * 2048 + (kc + 1) * 128],
                                           xn1[:, kc * TPC + ks * 512:kc * TPC + ks * 512 + 512],
                                           start=(kc == 0), stop=(kc == 15))
                    ins.then_inc(MM, 1)
                mm += 1
                pstate["mmw"][pp] = mm
                return pp, mm

            def proj_done(pp):
                if pp == 0:
                    pstate["se0"] = se
                else:
                    pstate["se1"] = se

            @blk.scalar
            def _(s):
                s.preload_act_table(AF.Silu)

            se_dth = 0
            ve_blk_last = 0
            se_ev = {}
            mm_kv = {}
            so_ah = [0] * 16
            prev_head_done = 0
            for h in range(16 if PH >= 2 else 0):
                for (oc, dst, act) in ((h, qh, AF.Silu), (16 + h, kh, AF.Silu),
                                       (32 + h, vh, AF.Silu)):
                    pp, mm_p = proj_oc(i_wqkvT, oc)

                    @blk.scalar
                    def _(s, oc=oc, dst=dst, act=act, pp=pp, mm_p=mm_p, h=h):
                        if h >= 1 and oc == h:
                            s.preload_act_table(AF.Silu)
                        s.wait_ge(MM, mm_p)
                        if h >= 1:
                            s.wait_ge(SO, so_ah[h - 1])
                        s.activation(dst[:], psP[pp][:, 0:TPC], act,
                                     bias=0.0, scale=1.0).then_inc(SE, 1)
                    se += 1
                    proj_done(pp)
                se_qkv = se

                v0 = ve
                mm0 = mm

                @blk.tensor
                def _(t, h=h, v0=v0, se_qkv=se_qkv):
                    t.wait_ge(SE, se_qkv)
                    t.wait_ge(VE, v0)
                    for i8 in range(8):
                        if i8 > 0:
                            t.wait_ge(VE, v0 + i8)
                        t.transpose(psT[:], kh[:, i8 * 128:(i8 + 1) * 128],
                                    ey128[:]).then_inc(MM, 1)
                    for i8 in range(8):
                        t.wait_ge(VE, v0 + 8 + i8)
                        t.transpose(psT[:], vh[:, i8 * 128:(i8 + 1) * 128],
                                    ey128[:]).then_inc(MM, 1)
                mm += 16

                @blk.vector
                def _(v, h=h, mm0=mm0):
                    for i8 in range(8):
                        v.wait_ge(MM, mm0 + i8 + 1)
                        v.tensor_copy(ktm[:, i8 * 128:(i8 + 1) * 128], psT[:]).then_inc(VE, 1)
                    for i8 in range(8):
                        v.wait_ge(MM, mm0 + 8 + i8 + 1)
                        v.tensor_copy(vtm[:, i8 * 128:(i8 + 1) * 128], psT[:]).then_inc(VE, 1)
                ve += 16
                ve_tm = ve

                @blk.scalar
                def _(s, h=h, ve_tm=ve_tm):
                    s.preload_act_table(AF.Exp)
                    s.wait_ge(VE, ve_tm)
                    s.activation(dth[:], dmask[:], AF.Exp, bias=0.0,
                                 scale=snegs[:, h:h + 1]).then_inc(SE, 1)
                    s.activation(kdc[:], kmask[:], AF.Exp, bias=0.0,
                                 scale=snegs[:, h:h + 1]).then_inc(SE, 1)
                    s.activation(qdrow[0:1, :], armask[0:1, :], AF.Exp, bias=0.0,
                                 scale=snegs[0:1, h:h + 1]).then_inc(SE, 1)
                se += 3
                se_dk = se

                @blk.tensor
                def _(t, h=h, se_dk=se_dk):
                    t.wait_ge(SE, se_dk)
                    for ks in range(2):
                        t.matmul(psQK[:, 0:512], ones1[:, :],
                                 qdrow[0:1, ks * 512:(ks + 1) * 512],
                                 start=True, stop=True).then_inc(MM, 1)
                        t.wait_ge(VE, ve + 1 + ks)
                mm += 2
                mm_qd0 = mm

                @blk.vector
                def _(v, h=h, mm_qd0=mm_qd0):
                    for ks in range(2):
                        v.wait_ge(MM, mm_qd0 - 1 + ks)
                        v.tensor_tensor(qd[:, ks * 512:(ks + 1) * 512],
                                        qh[:, ks * 512:(ks + 1) * 512],
                                        psQK[:, 0:512], op=ALU.mult).then_inc(VE, 1)
                ve += 2
                ve_qd = ve

                @blk.sync
                def _(s, h=h, ve_qd=ve_qd):
                    s.wait_ge(VE, ve_qd)
                    s.dma_start(out=o_qd[h * 128:(h + 1) * 128, :], in_=qd[:]).then_inc(SO, 16)
                so += 16

                v0 = ve

                @blk.vector
                def _(v, h=h, v0=v0, se_dk=se_dk):
                    v.wait_ge(SE, se_dk)
                    for i8 in range(8):
                        v.tensor_scalar(ksc[:, i8 * 128:(i8 + 1) * 128],
                                        ktm[:, i8 * 128:(i8 + 1) * 128],
                                        scalar1=kdc[:, i8 % 2:i8 % 2 + 1],
                                        scalar2=None, op0=ALU.mult).then_inc(VE, 1)
                    v.memset(kvst[:], 0.0).then_inc(VE, 1)
                    v.memset(kvbf[:], 0.0).then_inc(VE, 1)
                ve += 10
                ve_ks = ve

                ve_bl = ve
                for bi in range(4):
                    mm0 = mm

                    @blk.tensor
                    def _(t, h=h, bi=bi, ve_ks=ve_ks, mm0=mm0, ve_bl=ve_bl):
                        t.wait_ge(VE, ve_bl)
                        for hf in range(2):
                            t.matmul(psQK[:, hf * 512:hf * 512 + 256],
                                     kh[:, bi * 256 + hf * 128:bi * 256 + (hf + 1) * 128],
                                     qh[:, bi * 256:(bi + 1) * 256],
                                     start=True, stop=True).then_inc(MM, 1)
                    mm += 2

                    v0 = ve

                    @blk.vector
                    def _(v, h=h, bi=bi, v0=v0, mm0=mm0):
                        for hf in range(2):
                            v.wait_ge(MM, mm0 + hf + 1)
                            v.tensor_tensor(pb[:, hf * 256:(hf + 1) * 256],
                                            psQK[:, hf * 512:hf * 512 + 256],
                                            dth[:, hf * 256:(hf + 1) * 256],
                                            op=ALU.mult).then_inc(VE, 1)
                    ve += 2
                    ve_pb = ve
                    mm0b = mm

                    @blk.tensor
                    def _(t, h=h, bi=bi, ve_pb=ve_pb, mm0b=mm0b):
                        t.wait_ge(VE, ve_pb)
                        t.matmul(psIN,
                                 vtm[:, bi * 256:bi * 256 + 128].rearrange("a b -> a b") if False
                                 else vtm[:, bi * 256 * 0 + 0:128] if False
                                 else vtm[:, (bi * 2) * 128:(bi * 2 + 1) * 128],
                                 pb[:, 0:256], start=True, stop=False)
                        t.matmul(psIN,
                                 vtm[:, (bi * 2 + 1) * 128:(bi * 2 + 2) * 128],
                                 pb[:, 256:512], start=False, stop=False)
                        t.matmul(psIN, kvbf[:],
                                 qd[:, bi * 256:(bi + 1) * 256],
                                 start=False, stop=True).then_inc(MM, 1)
                        t.matmul(psQK[:, 0:128],
                                 ksc[:, (bi * 2) * 128:(bi * 2 + 1) * 128],
                                 vtm[:, (bi * 2) * 128:(bi * 2 + 1) * 128],
                                 start=True, stop=False)
                        t.matmul(psQK[:, 0:128],
                                 ksc[:, (bi * 2 + 1) * 128:(bi * 2 + 2) * 128],
                                 vtm[:, (bi * 2 + 1) * 128:(bi * 2 + 2) * 128],
                                 start=False, stop=True).then_inc(MM, 1)
                    mm += 2
                    mm_in = mm

                    v0 = ve

                    @blk.vector
                    def _(v, h=h, bi=bi, v0=v0, mm_in=mm_in):
                        v.wait_ge(MM, mm_in - 1)
                        v.tensor_copy(aout[:, bi * 256:(bi + 1) * 256],
                                      psIN).then_inc(VE, 1)
                        v.wait_ge(MM, mm_in)
                        v.wait_ge(VE, v0 + 1)
                        v.scalar_tensor_tensor(kvst[:], kvst[:], bdf[:, h:h + 1],
                                               psQK[:, 0:128], op0=ALU.mult,
                                               op1=ALU.add).then_inc(VE, 1)
                        v.wait_ge(VE, v0 + 2)
                        v.tensor_copy(kvbf[:], kvst[:]).then_inc(VE, 1)
                    ve += 3
                    ve_bl = ve
                ve_blocks = ve

                @blk.sync
                def _(s, h=h, ve_blocks=ve_blocks):
                    s.wait_ge(VE, ve_blocks)
                    s.dma_start(out=o_attn[h * 128:(h + 1) * 128, :], in_=aout[:]).then_inc(SO, 16)
                    s.dma_start(out=o_kvseg[:, h * 128:(h + 1) * 128], in_=kvst[:]).then_inc(SO, 16)
                so += 32
                so_ah[h] = so
                prev_head_done = mm
            so_light = so

            @blk.gpsimd
            def _(g):
                g.dma_start(out=shtm[:, 0:256],
                            in_=i_wg2T.rearrange("(kc p) m -> p kc m", p=128)).then_inc(WL, 16)
            wl += 16
            wl_g2 = wl

            ve_light_end = ve
            ve_g2t = [0] * 8
            for tt in range(8 if PH >= 3 else 0):
                pg2 = [psM, psQK][tt % 2]

                @blk.tensor
                def _(t, tt=tt, wl_g2=wl_g2, pg2=pg2):
                    t.wait_ge(WL, wl_g2)
                    t.wait_ge(VE, ve_light_end)
                    if tt >= 2:
                        t.wait_ge(VE, ve_g2t[tt - 2])
                    for kc in range(16):
                        ins = t.matmul(pg2[:, 0:16],
                                       xn1[:, kc * TPC + tt * 128:kc * TPC + (tt + 1) * 128],
                                       shtm[:, kc * 16:(kc + 1) * 16],
                                       start=(kc == 0), stop=(kc == 15))
                    ins.then_inc(MM, 1)
                mm += 1
                mm_g2 = mm

                v0 = ve

                @blk.vector
                def _(v, tt=tt, v0=v0, mm_g2=mm_g2, pg2=pg2):
                    v.wait_ge(MM, mm_g2)
                    gg = G2[:, tt * 16:(tt + 1) * 16]
                    v.tensor_copy(gg, pg2[:, 0:16]).then_inc(VE, 1)
                    v.wait_ge(VE, v0 + 1)
                    v.reduce_max(T1V[:, tt:tt + 1], gg, axis=AX.X).then_inc(VE, 1)
                    v.wait_ge(VE, v0 + 2)
                    v.tensor_scalar(IND[:, tt * 16:(tt + 1) * 16], gg,
                                    scalar1=T1V[:, tt:tt + 1], scalar2=None,
                                    op0=ALU.is_ge).then_inc(VE, 1)
                ve += 3
                ve_g2t[tt] = ve
            ve_g2 = ve

            v0 = ve

            @blk.vector
            def _(v, v0=v0):
                v.wait_ge(VE, ve_g2)
                v.tensor_scalar(T1V[:, 0:8], T1V[:, 0:8], scalar1=-1.0,
                                scalar2=None, op0=ALU.mult).then_inc(VE, 1)
            ve += 1
            ve_negm = ve

            se_ex = [0] * 8
            for tt in range(8):
                @blk.scalar
                def _(s, tt=tt):
                    s.wait_ge(VE, ve_negm)
                    s.activation(G2[:, tt * 16:(tt + 1) * 16],
                                 G2[:, tt * 16:(tt + 1) * 16], AF.Exp,
                                 bias=T1V[:, tt:tt + 1], scale=1.0).then_inc(SE, 1)
                se += 1
                se_ex[tt] = se

            v0 = ve

            @blk.vector
            def _(v, v0=v0):
                v.wait_ge(SE, se_ex[7])
                for tt in range(8):
                    v.reduce_sum(kacc[:, tt:tt + 1], G2[:, tt * 16:(tt + 1) * 16],
                                 axis=AX.X).then_inc(VE, 1)
                v.wait_ge(VE, v0 + 8)
                v.reciprocal(T1V[:, 0:8], kacc[:, 0:8]).then_inc(VE, 1)
            ve += 9
            ve_top1 = ve

            v0 = ve

            @blk.vector
            def _(v, v0=v0):
                v.wait_ge(VE, ve_top1)
                v.memset(kacc[:], 0.0).then_inc(VE, 1)
                v.memset(vacc[:], 0.0).then_inc(VE, 1)
            ve += 2
            ve_acc0 = ve

            ve_sel = ve_acc0
            so_q2_prev = 0
            for h in range(16 if PH >= 4 else 0):
                for (oc, isq) in ((h, True), (16 + h, False), (32 + h, False)):
                    pp, mm_p = proj_oc(i_wqkv2T, oc)
                    if isq:
                        @blk.scalar
                        def _(s, oc=oc, pp=pp, mm_p=mm_p, h=h, so_w=so_q2_prev):
                            s.wait_ge(MM, mm_p)
                            if so_w:
                                s.wait_ge(SO, so_w)
                            s.activation(qh[:], psP[pp][:, 0:TPC], AF.Copy,
                                         bias=0.0, scale=1.0).then_inc(SE, 1)
                        se += 1
                        se_q2 = se
                        proj_done(pp)

                        @blk.sync
                        def _(s, h=h, se_q2=se_q2):
                            s.wait_ge(SE, se_q2)
                            s.dma_start(out=o_q2[h * 128:(h + 1) * 128, :],
                                        in_=qh[:]).then_inc(SO, 16)
                        so += 16
                        so_q2_prev = so
                    else:
                        isk = (oc < 32)
                        dst = kh if isk else vh

                        @blk.scalar
                        def _(s, oc=oc, pp=pp, mm_p=mm_p, dst=dst):
                            s.wait_ge(MM, mm_p)
                            s.activation(dst[:], psP[pp][:, 0:TPC], AF.Copy,
                                         bias=0.0, scale=1.0).then_inc(SE, 1)
                        se += 1
                        se_kv = se
                        proj_done(pp)

                        v0 = ve
                        mm0 = mm

                        @blk.tensor
                        def _(t, dst=dst, v0=v0, se_kv=se_kv):
                            t.wait_ge(SE, se_kv)
                            t.wait_ge(VE, v0)
                            for i8 in range(8):
                                if i8 > 0:
                                    t.wait_ge(VE, v0 + i8)
                                t.transpose(psT[:], dst[:, i8 * 128:(i8 + 1) * 128],
                                            ey128[:]).then_inc(MM, 1)
                        mm += 8

                        @blk.vector
                        def _(v, h=h, isk=isk, v0=v0, mm0=mm0):
                            acc = kacc if isk else vacc
                            for i8 in range(8):
                                v.wait_ge(MM, mm0 + i8 + 1)
                                v.tensor_copy(ktm[:, i8 * 128:(i8 + 1) * 128],
                                              psT[:]).then_inc(VE, 1)
                            v.wait_ge(VE, v0 + 8)
                            for i8 in range(8):
                                v.wait_ge(VE, v0 + 8 + i8)
                                v.scalar_tensor_tensor(acc[:, i8 * 128:(i8 + 1) * 128],
                                                       ktm[:, i8 * 128:(i8 + 1) * 128],
                                                       IND[:, i8 * 16 + h:i8 * 16 + h + 1],
                                                       acc[:, i8 * 128:(i8 + 1) * 128],
                                                       op0=ALU.mult, op1=ALU.add).then_inc(VE, 1)
                        ve += 16
                        ve_sel = ve
            ve_selend = ve

            v0 = ve

            @blk.vector
            def _(v, v0=v0):
                v.wait_ge(VE, ve_selend)
                for i8 in range(8):
                    v.tensor_scalar(shtm[:, i8 * 128:(i8 + 1) * 128],
                                    vacc[:, i8 * 128:(i8 + 1) * 128],
                                    scalar1=T1V[:, i8:i8 + 1], scalar2=None,
                                    op0=ALU.mult).then_inc(VE, 1)
            ve += 8
            ve_vsh = ve

            @blk.sync
            def _(s):
                s.wait_ge(VE, ve_vsh)
                for i8 in range(8):
                    s.dma_start(out=o_vsh[i8 * 128:(i8 + 1) * 128, :],
                                in_=shtm[:, i8 * 128:(i8 + 1) * 128]).then_inc(SO, 16)
            so += 128

            v0 = ve

            @blk.vector
            def _(v, v0=v0):
                v.wait_ge(VE, ve_vsh)
                for i8 in range(8):
                    v.tensor_copy(ktm[:, i8 * 128:(i8 + 1) * 128],
                                  kacc[:, i8 * 128:(i8 + 1) * 128]).then_inc(VE, 1)
            ve += 8
            ve_kbf = ve
            mm0 = mm

            @blk.tensor
            def _(t, v0=ve, mm0=mm0):
                t.wait_ge(VE, ve_kbf)
                for i8 in range(8):
                    if i8 > 0:
                        t.wait_ge(VE, v0 + i8)
                    t.transpose(psT[:], ktm[:, i8 * 128:(i8 + 1) * 128],
                                ey128[:]).then_inc(MM, 1)
            mm += 8

            @blk.vector
            def _(v, mm0=mm0):
                for i8 in range(8):
                    v.wait_ge(MM, mm0 + i8 + 1)
                    v.tensor_copy(vtm[:, i8 * 128:(i8 + 1) * 128], psT[:]).then_inc(VE, 1)
            ve += 8
            ve_ksh = ve

            @blk.sync
            def _(s):
                s.wait_ge(VE, ve_ksh)
                s.dma_start(out=o_ksh[:, :], in_=vtm[:, 0:TPC]).then_inc(SO, 16)
            so += 16

            so_g = [0] * 16
            for oc in range(16 if PH >= 5 else 0):
                pp, mm_p = proj_oc(i_wgateT, oc)

                @blk.scalar
                def _(s, oc=oc, pp=pp, mm_p=mm_p, so_w=so_g[oc - 2] if oc >= 2 else 0):
                    if oc == 0:
                        s.preload_act_table(AF.Sigmoid)
                    s.wait_ge(MM, mm_p)
                    if so_w:
                        s.wait_ge(SO, so_w)
                    s.activation(gsl[:, (oc % 2) * TPC:(oc % 2 + 1) * TPC],
                                 psP[pp][:, 0:TPC], AF.Sigmoid,
                                 bias=0.0, scale=1.0).then_inc(SE, 1)
                se += 1
                proj_done(pp)

                @blk.sync
                def _(s, oc=oc, se_w=se):
                    s.wait_ge(SE, se_w)
                    s.dma_start(out=o_gate[oc * 128:(oc + 1) * 128, :],
                                in_=gsl[:, (oc % 2) * TPC:(oc % 2 + 1) * TPC]).then_inc(SO, 16)
                so += 16
                so_g[oc] = so

            @blk.sync
            def _(s):
                s.wait_ge(SO, so)
    return nc



def build_E():
    nc = bass.Bass("TRN2", target_bir_lowering=False)
    dram = nc.dram_tensor
    i_attn = dram("attno", [D, TPC], BF16, kind="ExternalInput").ap()
    i_qd = dram("qdo", [D, TPC], BF16, kind="ExternalInput").ap()
    i_kvs = dram("kvstack", [8 * 128, 16 * 128], F32, kind="ExternalInput").ap()
    i_slope = dram("slope", [16, 1], F32, kind="ExternalInput").ap()
    i_delta = dram("delta", [1, 8], F32, kind="ExternalInput").ap()
    i_nw = dram("normw1", [128, 16], F32, kind="ExternalInput").ap()
    i_gate = dram("gateo", [D, TPC], BF16, kind="ExternalInput").ap()
    i_woT = dram("wout1T", [D, D], F32, kind="ExternalInput").ap()
    i_eye16 = dram("eye16", [16, 16], F32, kind="ExternalInput").ap()

    o_s1 = dram("s1o", [D, TPC], BF16, kind="ExternalOutput").ap()

    with ExitStack() as ex:
        e = ex.enter_context
        ey16 = e(nc.sbuf_tensor("ey16_s", [16, 16], F32))
        slp = e(nc.sbuf_tensor("slp", [16, 1], F32))
        dlt = e(nc.sbuf_tensor("dlt", [1, 8], F32))
        nw = e(nc.sbuf_tensor("nw_s", [128, 16], F32))
        ones1 = e(nc.sbuf_tensor("ones1", [1, 128], F32))
        onesb = e(nc.sbuf_tensor("onesb", [128, 1], BF16))
        cEPS = e(nc.sbuf_tensor("cEPS", [1, 1], F32))
        Wjh = e(nc.sbuf_tensor("Wjh", [8, 16], F32))
        wflat = e(nc.sbuf_tensor("wflat", [1, 128], F32))
        wfill = e(nc.sbuf_tensor("wfill", [128, 128], F32))
        bdfill = e(nc.sbuf_tensor("bdfill", [128, 64], F32))
        segl = e(nc.sbuf_tensor("segl", [128, 2 * 2048], F32))
        kvin = e(nc.sbuf_tensor("kvin", [128, 16 * 128], F32))
        kvsc = e(nc.sbuf_tensor("kvsc", [128, 128], BF16))
        qdh = e(nc.sbuf_tensor("qdh", [128, TPC], BF16))
        aout = e(nc.sbuf_tensor("aout", [128, 16 * TPC], BF16))
        ysq = e(nc.sbuf_tensor("ysq", [128, TPC], BF16))
        rr = e(nc.sbuf_tensor("rr", [1, TPC], F32))
        gsl = e(nc.sbuf_tensor("gsl", [128, 2 * TPC], BF16))
        wsl = e(nc.sbuf_tensor("wsl", [128, 2 * 16 * 128], BF16))
        s1b = e(nc.sbuf_tensor("s1b", [128, 2 * TPC], BF16))

        psA = e(nc.psum_tensor("psA", [128, 512], F32))
        psC1 = e(nc.psum_tensor("psC1", [128, 512], F32))
        psR = e(nc.psum_tensor("psR", [1, TPC], F32))
        psB = e(nc.psum_tensor("psB", [128, TPC], F32))
        psO = e(nc.psum_tensor("psO", [128, TPC], F32))

        LD = e(nc.semaphore("LD"))
        WL = e(nc.semaphore("WL"))
        MM = e(nc.semaphore("MM"))
        SE = e(nc.semaphore("SE"))
        VE = e(nc.semaphore("VE"))
        SO = e(nc.semaphore("SO"))
        ld = wl = mm = se = ve = so = 0

        with nc.Block() as blk:
            @blk.sync
            def _(s):
                for (dst, src) in ((ey16, i_eye16), (slp, i_slope), (dlt, i_delta),
                                   (nw, i_nw)):
                    s.dma_start(out=dst[:], in_=src[:, :]).then_inc(LD, 16)
                s.dma_start(out=aout[:], in_=i_attn.rearrange("(h p) t -> p h t", p=128)).then_inc(LD, 16)
            ld += 80
            ld_p0 = ld

            @blk.vector
            def _(v):
                v.wait_ge(LD, ld_p0)
                v.memset(ones1[:], 1.0).then_inc(VE, 1)
                v.memset(onesb[:], 1.0).then_inc(VE, 1)
                v.memset(cEPS[:], 1e-6).then_inc(VE, 1)
            ve += 3
            ve_p0 = ve

            @blk.scalar
            def _(s):
                preload_set6(s)

            @blk.tensor
            def _(t):
                t.wait_ge(VE, ve_p0)
                t.transpose(psA[0:1, 0:16], slp[:], ey16[:]).then_inc(MM, 1)
            mm += 1
            mm_sr = mm

            @blk.vector
            def _(v):
                v.wait_ge(MM, mm_sr)
                v.tensor_copy(rr[0:1, 0:16], psA[0:1, 0:16]).then_inc(VE, 1)
            ve += 1
            ve_srow = ve

            @blk.tensor
            def _(t):
                t.wait_ge(VE, ve_srow)
                t.matmul(psA[0:8, 16:32], dlt[:], rr[0:1, 0:16],
                         start=True, stop=True).then_inc(MM, 1)
            mm += 1
            mm_w = mm

            @blk.scalar
            def _(s):
                s.wait_ge(MM, mm_w)
                s.activation(Wjh[:], psA[0:8, 16:32], AF.Exp, bias=0.0,
                             scale=-1024.0).then_inc(SE, 1)
            se += 1
            se_w = se

            @blk.gpsimd
            def _(g):
                g.wait_ge(SE, se_w)
                g.dma_start(out=wflat[0:1, :], in_=Wjh[:, :]).then_inc(WL, 16)
            wl += 16
            wl_wf = wl

            @blk.tensor
            def _(t):
                t.wait_ge(WL, wl_wf)
                t.matmul(psA[:, 32:160], ones1[:, :], wflat[0:1, :],
                         start=True, stop=True).then_inc(MM, 1)
            mm += 1
            mm_wf = mm

            @blk.vector
            def _(v):
                v.wait_ge(MM, mm_wf)
                v.tensor_copy(wfill[:], psA[:, 32:160]).then_inc(VE, 1)
                v.memset(kvin[:], 0.0).then_inc(VE, 1)
            ve += 2
            ve_wf = ve

            ve_kv = [0] * 8
            for j in range(8):
                pj = j % 2

                @blk.sync
                def _(s, j=j, pj=pj):
                    if j >= 2:
                        s.wait_ge(VE, ve_kv[j - 2])
                    s.dma_start(out=segl[:, pj * 2048:(pj + 1) * 2048],
                                in_=i_kvs[j * 128:(j + 1) * 128, :]).then_inc(LD, 16)
                ld += 16
                ld_j = ld

                v0 = ve

                @blk.vector
                def _(v, j=j, pj=pj, v0=v0, ld_j=ld_j):
                    v.wait_ge(LD, ld_j)
                    v.wait_ge(VE, ve_wf)
                    sv = segl[:, pj * 2048:(pj + 1) * 2048].rearrange(
                        "p (h q) -> p h q", h=16)
                    wv = wfill[:, j * 16:(j + 1) * 16].unsqueeze(2).broadcast_to((128, 16, 128))
                    v.tensor_tensor(sv, sv, wv, op=ALU.mult).then_inc(VE, 1)
                    v.wait_ge(VE, v0 + 1)
                    v.tensor_tensor(kvin[:], kvin[:],
                                    segl[:, pj * 2048:(pj + 1) * 2048],
                                    op=ALU.add).then_inc(VE, 1)
                ve += 2
                ve_kv[j] = ve
            ve_kvin = ve

            @blk.gpsimd
            def _(g, ve_w=ve_kvin):
                g.wait_ge(VE, ve_w)
                g.iota(rr[0:1, 16:20], pattern=[[1, 4]], base=0,
                       channel_multiplier=0,
                       allow_small_or_imprecise_dtypes=True).then_inc(WL, 16)
            wl += 16
            wl_io = wl

            @blk.tensor
            def _(t):
                t.wait_ge(WL, wl_io)
                t.matmul(psA[0:4, 256:272], rr[0:1, 16:20], rr[0:1, 0:16],
                         start=True, stop=True).then_inc(MM, 1)
            mm += 1
            mm_bd1 = mm

            @blk.scalar
            def _(s):
                s.wait_ge(MM, mm_bd1)
                s.activation(ysq[0:4, 0:16], psA[0:4, 256:272], AF.Exp,
                             bias=0.0, scale=-256.0).then_inc(SE, 1)
            se += 1
            se_bd = se

            @blk.gpsimd
            def _(g):
                g.wait_ge(SE, se_bd)
                g.dma_start(out=rr[0:1, 128:192], in_=ysq[0:4, 0:16]).then_inc(WL, 16)
            wl += 16
            wl_bd = wl

            @blk.tensor
            def _(t):
                t.wait_ge(WL, wl_bd)
                t.matmul(psA[:, 256:320], ones1[:, :], rr[0:1, 128:192],
                         start=True, stop=True).then_inc(MM, 1)
            mm += 1
            mm_bd2 = mm

            @blk.vector
            def _(v):
                v.wait_ge(MM, mm_bd2)
                v.tensor_copy(bdfill[:], psA[:, 256:320]).then_inc(VE, 1)
            ve += 1
            ve_bd = ve

            ld_qd = [0] * 16
            ve_corr = [0] * 16
            for h in range(16):
                @blk.sync
                def _(s, h=h):
                    if h >= 1:
                        s.wait_ge(VE, ve_corr[h - 1])
                    s.dma_start(out=qdh[:],
                                in_=i_qd[h * 128:(h + 1) * 128, :]).then_inc(LD, 16)
                ld += 16
                ld_qd[h] = ld

                for bi in range(4):
                    v0 = ve

                    @blk.vector
                    def _(v, h=h, bi=bi, v0=v0):
                        v.wait_ge(VE, ve_bd)
                        v.wait_ge(VE, ve_kvin)
                        if not (h == 0 and bi == 0):
                            v.wait_ge(MM, mm)
                        v.tensor_scalar(kvsc[:], kvin[:, h * 128:(h + 1) * 128],
                                        scalar1=bdfill[:, bi * 16 + h:bi * 16 + h + 1],
                                        scalar2=None, op0=ALU.mult).then_inc(VE, 1)
                    ve += 1
                    ve_sc = ve

                    @blk.tensor
                    def _(t, h=h, bi=bi, ve_sc=ve_sc, ld_h=ld_qd[h]):
                        t.wait_ge(VE, ve_sc)
                        t.wait_ge(LD, ld_h)
                        t.matmul(psC1[:, 0:256], kvsc[:],
                                 qdh[:, bi * 256:(bi + 1) * 256],
                                 start=True, stop=True).then_inc(MM, 1)
                    mm += 1
                    mm_c = mm

                    @blk.vector
                    def _(v, h=h, bi=bi, mm_c=mm_c):
                        v.wait_ge(MM, mm_c)
                        sl_ = aout[:, h * TPC + bi * 256:h * TPC + (bi + 1) * 256]
                        v.tensor_tensor(sl_, sl_, psC1[:, 0:256],
                                        op=ALU.add).then_inc(VE, 1)
                    ve += 1
                ve_corr[h] = ve
            ve_corrall = ve

            mm_sq = [0] * 16
            for pc in range(16):
                @blk.scalar
                def _(s, pc=pc):
                    s.wait_ge(VE, ve_corrall)
                    if pc >= 1:
                        s.wait_ge(MM, mm_sq[pc - 1])
                    s.activation(ysq[:], aout[:, pc * TPC:(pc + 1) * TPC], AF.Square,
                                 bias=0.0, scale=1.0).then_inc(SE, 1)
                se += 1

                @blk.tensor
                def _(t, pc=pc, se_w=se):
                    t.wait_ge(SE, se_w)
                    for k in range(2):
                        ins = t.matmul(psR[0:1, k * 512:(k + 1) * 512], onesb[:],
                                       ysq[:, k * 512:(k + 1) * 512],
                                       start=(pc == 0), stop=(pc == 15))
                    ins.then_inc(MM, 1)
                mm += 1
                mm_sq[pc] = mm
            mm_r = mm

            v0 = ve

            @blk.vector
            def _(v, v0=v0):
                v.wait_ge(MM, mm_r)
                v.tensor_scalar(rr[0:1, 0:TPC], psR[0:1, :], scalar1=1.0 / D,
                                scalar2=None, op0=ALU.mult).then_inc(VE, 1)
            ve += 1
            ve_rr = ve

            @blk.scalar
            def _(s, se0=se):
                s.wait_ge(VE, ve_rr)
                s.activation(rr[0:1, 0:TPC], rr[0:1, 0:TPC], AF.Ln,
                             bias=cEPS[0:1, 0:1], scale=1.0).then_inc(SE, 1)
                s.wait_ge(SE, se0 + 1)
                s.activation(rr[0:1, 0:TPC], rr[0:1, 0:TPC], AF.Exp,
                             bias=0.0, scale=-0.5).then_inc(SE, 1)
            se += 2
            se_rstd = se

            @blk.tensor
            def _(t):
                t.wait_ge(SE, se_rstd)
                ins = None
                for k in range(2):
                    ins = t.matmul(psB[:, k * 512:(k + 1) * 512], ones1[:, :],
                                   rr[0:1, k * 512:(k + 1) * 512],
                                   start=True, stop=True)
                ins.then_inc(MM, 1)
            mm += 1
            mm_bc = mm

            ld_g = [0] * 16
            ve_yn = [0] * 16
            for pc in range(16):
                @blk.sync
                def _(s, pc=pc):
                    if pc >= 2:
                        s.wait_ge(VE, ve_yn[pc - 2])
                    s.dma_start(out=gsl[:, (pc % 2) * TPC:(pc % 2 + 1) * TPC],
                                in_=i_gate[pc * 128:(pc + 1) * 128, :]).then_inc(LD, 16)
                ld += 16
                ld_g[pc] = ld

                v0 = ve

                @blk.vector
                def _(v, pc=pc, v0=v0, ld_w=ld_g[pc]):
                    v.wait_ge(MM, mm_bc)
                    v.wait_ge(LD, ld_w)
                    sl_ = aout[:, pc * TPC:(pc + 1) * TPC]
                    v.scalar_tensor_tensor(sl_, sl_, nw[:, pc:pc + 1], psB[:],
                                           op0=ALU.mult, op1=ALU.mult).then_inc(VE, 1)
                    v.wait_ge(VE, v0 + 1)
                    v.tensor_tensor(sl_, sl_, gsl[:, (pc % 2) * TPC:(pc % 2 + 1) * TPC],
                                    op=ALU.mult).then_inc(VE, 1)
                ve += 2
                ve_yn[pc] = ve
            ve_ynall = ve

            so_s1 = [0] * 16
            se_oc = [0] * 16
            for oc in range(16):
                p = oc % 2

                @blk.gpsimd
                def _(g, oc=oc, p=p):
                    if oc >= 2:
                        g.wait_ge(MM, mm)
                    src = i_woT[:, oc * 128:(oc + 1) * 128]
                    g.dma_start(out=wsl[:, p * 2048:(p + 1) * 2048],
                                in_=src.rearrange("(kc q) m -> q kc m", q=128)).then_inc(WL, 16)
                wl += 16
                wl_oc = wl

                @blk.tensor
                def _(t, oc=oc, p=p, wl_oc=wl_oc):
                    t.wait_ge(WL, wl_oc)
                    t.wait_ge(VE, ve_ynall)
                    if oc >= 1:
                        t.wait_ge(SE, se_oc[oc - 1])
                    ins = None
                    for kc in range(16):
                        for ks in range(2):
                            ins = t.matmul(psO[:, ks * 512:(ks + 1) * 512],
                                           wsl[:, p * 2048 + kc * 128:p * 2048 + (kc + 1) * 128],
                                           aout[:, kc * TPC + ks * 512:kc * TPC + ks * 512 + 512],
                                           start=(kc == 0), stop=(kc == 15))
                    ins.then_inc(MM, 1)
                mm += 1
                mm_oc = mm

                @blk.scalar
                def _(s, oc=oc, p=p, mm_w=mm_oc, so_w=so_s1[oc - 2] if oc >= 2 else 0):
                    s.wait_ge(MM, mm_w)
                    if so_w:
                        s.wait_ge(SO, so_w)
                    s.activation(s1b[:, p * TPC:(p + 1) * TPC], psO[:], AF.Copy,
                                 bias=0.0, scale=1.0).then_inc(SE, 1)
                se += 1
                se_oc[oc] = se

                @blk.sync
                def _(s, oc=oc, p=p, se_w=se):
                    s.wait_ge(SE, se_w)
                    s.dma_start(out=o_s1[oc * 128:(oc + 1) * 128, :],
                                in_=s1b[:, p * TPC:(p + 1) * TPC]).then_inc(SO, 16)
                so += 16
                so_s1[oc] = so

            @blk.sync
            def _(s):
                s.wait_ge(SO, so)
    return nc



SEQ = 8192


def build_F():
    nc = bass.Bass("TRN2", target_bir_lowering=False)
    dram = nc.dram_tensor
    i_q = dram("q2h", [256, SEQ], BF16, kind="ExternalInput").ap()
    i_k = dram("kshf", [128, SEQ], BF16, kind="ExternalInput").ap()
    i_v = dram("vshf", [SEQ, 128], BF16, kind="ExternalInput").ap()
    i_msk = dram("cmask", [128, 4 * 512], BF16, kind="ExternalInput").ap()
    o_y = dram("y2o", [256, SEQ], BF16, kind="ExternalOutput").ap()

    with ExitStack() as ex:
        e = ex.enter_context
        qf = e(nc.sbuf_tensor("qf", [128, 2 * SEQ], BF16))
        kf = e(nc.sbuf_tensor("kf", [128, SEQ], BF16))
        vt = e(nc.sbuf_tensor("vt", [128, SEQ], BF16))
        msk = e(nc.sbuf_tensor("msk", [128, 2048], BF16))
        qsq = e(nc.sbuf_tensor("qsq", [128, SEQ], BF16))
        row = e(nc.sbuf_tensor("row", [1, SEQ], F32))
        srow = e(nc.sbuf_tensor("srow", [1, 512], F32))
        sc1 = e(nc.sbuf_tensor("sc1", [1, 8], F32))
        bias = e(nc.sbuf_tensor("biasc", [128, 1], F32))
        onesb = e(nc.sbuf_tensor("onesb", [128, 1], BF16))
        ones1 = e(nc.sbuf_tensor("ones1", [1, 128], F32))
        pb = e(nc.sbuf_tensor("pb", [128, 4 * 512], BF16))
        ysb = e(nc.sbuf_tensor("ysb", [128, 512], F32))
        y2 = e(nc.sbuf_tensor("y2", [128, 2 * SEQ], BF16))

        psS4a = e(nc.psum_tensor("psS4a", [128, 1024], F32))
        psS4b = e(nc.psum_tensor("psS4b", [128, 1024], F32))
        psO0 = e(nc.psum_tensor("psO0", [128, 512], F32))
        psO1 = e(nc.psum_tensor("psO1", [128, 512], F32))
        psSm0 = e(nc.psum_tensor("psSm0", [1, 512], F32))
        psSm1 = e(nc.psum_tensor("psSm1", [1, 512], F32))

        LD = e(nc.semaphore("LD"))
        MM = e(nc.semaphore("MM"))
        SE = e(nc.semaphore("SE"))
        VE = e(nc.semaphore("VE"))
        SO = e(nc.semaphore("SO"))
        ld = mm = se = ve = so = 0
        psS4 = [psS4a, psS4b]
        psO = [psO0, psO1]
        psSm = [psSm0, psSm1]
        psB = psS4a[:, 0:512]

        with nc.Block() as blk:
            @blk.sync
            def _(s):
                s.dma_start(out=qf[:], in_=i_q.rearrange("(h p) t -> p h t", p=128)).then_inc(LD, 16)
                s.dma_start(out=kf[:], in_=i_k[:, :]).then_inc(LD, 16)
                s.dma_start(out=vt[:], in_=i_v.rearrange("(j p) e -> p j e", p=128)).then_inc(LD, 16)
                s.dma_start(out=msk[:], in_=i_msk[:, :]).then_inc(LD, 16)
            ld += 64
            ld_p0 = ld

            @blk.vector
            def _(v):
                v.wait_ge(LD, ld_p0)
                v.memset(onesb[:], 1.0).then_inc(VE, 1)
                v.memset(ones1[:], 1.0).then_inc(VE, 1)
            ve += 2
            ve_p0 = ve

            se_b = se
            for (src, roff) in ((0, 0), (1, 1), (2, 2)):
                @blk.vector
                def _(v, src=src, v0=ve):
                    v.wait_ge(VE, v0)
                    if src < 2:
                        v.tensor_tensor(qsq[:], qf[:, src * SEQ:(src + 1) * SEQ],
                                        qf[:, src * SEQ:(src + 1) * SEQ],
                                        op=ALU.mult).then_inc(VE, 1)
                    else:
                        v.tensor_tensor(qsq[:], kf[:], kf[:],
                                        op=ALU.mult).then_inc(VE, 1)
                ve += 1
                se_sq = se

                mm0 = mm

                @blk.tensor
                def _(t, src=src, se_sq=se_sq, mm0=mm0, v0=ve):
                    t.wait_ge(VE, v0)
                    for j in range(16):
                        if j > 0:
                            t.wait_ge(VE, v0 + j)
                        t.matmul(psSm0[0:1, :], onesb[:],
                                 qsq[:, j * 512:(j + 1) * 512],
                                 start=True, stop=True).then_inc(MM, 1)
                mm += 16

                @blk.vector
                def _(v, src=src, mm0=mm0):
                    for j in range(16):
                        v.wait_ge(MM, mm0 + j + 1)
                        v.tensor_copy(row[0:1, j * 512:(j + 1) * 512],
                                      psSm0[0:1, :]).then_inc(VE, 1)
                    v.wait_ge(VE, ve + 16)
                    v.reduce_max(sc1[0:1, src:src + 1], row[0:1, :],
                                 axis=AX.X).then_inc(VE, 1)
                ve += 17
            ve_bnd = ve

            v0 = ve

            @blk.vector
            def _(v, v0=v0):
                v.wait_ge(VE, ve_bnd)
                v.tensor_tensor(sc1[0:1, 0:1], sc1[0:1, 0:1], sc1[0:1, 1:2],
                                op=ALU.max).then_inc(VE, 1)
                v.wait_ge(VE, v0 + 1)
                v.tensor_tensor(sc1[0:1, 3:4], sc1[0:1, 0:1], sc1[0:1, 2:3],
                                op=ALU.mult).then_inc(VE, 1)
            ve += 2
            ve_b2 = ve

            v0 = ve

            @blk.vector
            def _(v, v0=v0):
                v.wait_ge(VE, ve_b2)
                v.tensor_tensor(sc1[0:1, 5:6], sc1[0:1, 0:1], sc1[0:1, 2:3],
                                op=ALU.add).then_inc(VE, 1)
                v.wait_ge(VE, v0 + 1)
                v.tensor_scalar(sc1[0:1, 6:7], sc1[0:1, 5:6],
                                scalar1=-0.5 / (128.0 ** 0.5), scalar2=None,
                                op0=ALU.mult).then_inc(VE, 1)
            ve += 2
            ve_bv = ve

            @blk.tensor
            def _(t):
                t.wait_ge(VE, ve_bv)
                t.matmul(psB[:, 0:1], ones1[:, :], sc1[0:1, 6:7],
                         start=True, stop=True).then_inc(MM, 1)
            mm += 1
            mm_bb = mm

            @blk.vector
            def _(v):
                v.wait_ge(MM, mm_bb)
                v.tensor_copy(bias[:], psB[:, 0:1]).then_inc(VE, 1)
            ve += 1
            ve_bias = ve

            @blk.scalar
            def _(s):
                s.preload_act_table(AF.Exp)

            SCALE = 1.0 / (128.0 ** 0.5)
            se_exp = {}
            ve_ep = 0
            se_prev_pair = 0
            for g in range(16):
                nj = 4 * g + 4
                nb = nj // 2
                mm_pv_last = [0, 0]
                for b in range(nb):
                    j0 = 2 * b
                    for h in range(2):
                        @blk.tensor
                        def _(t, g=g, b=b, j0=j0, h=h):
                            t.wait_ge(VE, ve_bias)
                            key = (h, g, b - 1)
                            if key in se_exp:
                                t.wait_ge(SE, se_exp[key])
                            elif b == 0 and g > 0:
                                t.wait_ge(SE, se_prev_pair)
                                if h == 0:
                                    t.wait_ge(VE, ve_ep)
                            for dj in range(2):
                                t.matmul(psS4[h][:, dj * 512:(dj + 1) * 512],
                                         kf[:, (j0 + dj) * 128:(j0 + dj + 1) * 128],
                                         qf[:, h * SEQ + g * 512:h * SEQ + (g + 1) * 512],
                                         start=True, stop=True).then_inc(MM, 1)
                        mm += 2

                        @blk.scalar
                        def _(s, g=g, b=b, h=h, mm_w=mm):
                            s.wait_ge(MM, mm_w)
                            s.activation(pb[:, h * 1024:(h + 1) * 1024],
                                         psS4[h][:, :], AF.Exp, bias=bias[:, 0:1],
                                         scale=SCALE).then_inc(SE, 1)
                        se += 1
                        se_exp[(h, g, b)] = se

                        ismask = (j0 >= 4 * g)
                        if ismask:
                            r = j0 - 4 * g

                            @blk.vector
                            def _(v, r=r, h=h, se_w=se):
                                v.wait_ge(SE, se_w)
                                v.tensor_tensor(pb[:, h * 1024:(h + 1) * 1024],
                                                pb[:, h * 1024:(h + 1) * 1024],
                                                msk[:, r * 512:(r + 2) * 512],
                                                op=ALU.mult).then_inc(VE, 1)
                            ve += 1
                        ve_m = ve

                        @blk.tensor
                        def _(t, g=g, b=b, j0=j0, h=h, nb=nb, se_w=se,
                              ismask=ismask, ve_m=ve_m):
                            if ismask:
                                t.wait_ge(VE, ve_m)
                            else:
                                t.wait_ge(SE, se_w)
                            ins = None
                            for dj in range(2):
                                t.matmul(psO[h][:],
                                         vt[:, (j0 + dj) * 128:(j0 + dj + 1) * 128],
                                         pb[:, h * 1024 + dj * 512:h * 1024 + (dj + 1) * 512],
                                         start=(b == 0 and dj == 0),
                                         stop=(b == nb - 1 and dj == 1))
                                ins = t.matmul(psSm[h][0:1, :], onesb[:],
                                               pb[:, h * 1024 + dj * 512:h * 1024 + (dj + 1) * 512],
                                               start=(b == 0 and dj == 0),
                                               stop=(b == nb - 1 and dj == 1))
                            ins.then_inc(MM, 1)
                        mm += 1
                        mm_pv_last[h] = mm

                for h in range(2):
                    v0 = ve

                    @blk.vector
                    def _(v, g=g, h=h, v0=v0, mm_w=mm_pv_last[h]):
                        v.wait_ge(MM, mm_w)
                        v.tensor_copy(srow[0:1, :], psSm[h][0:1, :]).then_inc(VE, 1)
                        v.wait_ge(VE, v0 + 1)
                        v.reciprocal(srow[0:1, :], srow[0:1, :]).then_inc(VE, 1)
                        v.wait_ge(VE, v0 + 2)
                        v.tensor_copy(ysb[:], psO[h][:]).then_inc(VE, 1)
                    ve += 3
                    ve_ep1 = ve

                    @blk.tensor
                    def _(t, ve_ep1=ve_ep1, se_w=se):
                        t.wait_ge(VE, ve_ep1)
                        t.wait_ge(SE, se_w)
                        t.matmul(psB[:, :], ones1[:, :], srow[0:1, :],
                                 start=True, stop=True).then_inc(MM, 1)
                    mm += 1
                    mm_b2 = mm

                    @blk.vector
                    def _(v, g=g, h=h, mm_b2=mm_b2):
                        v.wait_ge(MM, mm_b2)
                        v.tensor_tensor(y2[:, h * SEQ + g * 512:h * SEQ + (g + 1) * 512],
                                        ysb[:], psB[:, :], op=ALU.mult).then_inc(VE, 1)
                    ve += 1
                    ve_ep = ve
                se_prev_pair = se

            @blk.sync
            def _(s, ve_w=ve):
                s.wait_ge(VE, ve_w)
                s.dma_start(out=o_y[0:128, :], in_=y2[:, 0:SEQ]).then_inc(SO, 16)
                s.dma_start(out=o_y[128:256, :], in_=y2[:, SEQ:2 * SEQ]).then_inc(SO, 16)
                s.wait_ge(SO, 32)
            so += 32
    return nc



def build_G():
    nc = bass.Bass("TRN2", target_bir_lowering=False)
    dram = nc.dram_tensor
    i_x1 = dram("x1o", [D, TPC], F32, kind="ExternalInput").ap()
    i_s1 = dram("s1o", [D, TPC], BF16, kind="ExternalInput").ap()
    i_y2 = dram("y2sl", [D, TPC], BF16, kind="ExternalInput").ap()
    i_eye128f = dram("eye128f", [128, 128], F32, kind="ExternalInput").ap()
    o_out = dram("outo", [TPC, D], F32, kind="ExternalOutput").ap()

    with ExitStack() as ex:
        e = ex.enter_context
        ey = e(nc.sbuf_tensor("ey_g", [128, 128], F32))
        x1s = e(nc.sbuf_tensor("x1s", [128, 2 * TPC], F32))
        s1s = e(nc.sbuf_tensor("s1s", [128, 2 * TPC], BF16))
        y2s = e(nc.sbuf_tensor("y2s", [128, 2 * TPC], BF16))
        g1s = e(nc.sbuf_tensor("g1s", [128, TPC], BF16))
        of = e(nc.sbuf_tensor("of", [128, 16 * TPC], F32))
        ot = e(nc.sbuf_tensor("ot", [128, 2 * D], F32))

        psT = e(nc.psum_tensor("psT_g", [128, 128], F32))

        LD = e(nc.semaphore("LD"))
        MM = e(nc.semaphore("MM"))
        SE = e(nc.semaphore("SE"))
        VE = e(nc.semaphore("VE"))
        SO = e(nc.semaphore("SO"))
        ld = mm = se = ve = so = 0

        with nc.Block() as blk:
            @blk.sync
            def _(s):
                s.dma_start(out=ey[:], in_=i_eye128f[:, :]).then_inc(LD, 16)
            ld += 16
            ld_ey = ld

            @blk.scalar
            def _(s):
                s.preload_act_table(AF.Silu)

            ve_kc = [0] * 16
            for kc in range(16):
                p = kc % 2

                @blk.sync
                def _(s, kc=kc, p=p):
                    if kc >= 2:
                        s.wait_ge(VE, ve_kc[kc - 2])
                    s.dma_start(out=x1s[:, p * TPC:(p + 1) * TPC],
                                in_=i_x1[kc * 128:(kc + 1) * 128, :]).then_inc(LD, 16)
                    s.dma_start(out=s1s[:, p * TPC:(p + 1) * TPC],
                                in_=i_s1[kc * 128:(kc + 1) * 128, :]).then_inc(LD, 16)
                    s.dma_start(out=y2s[:, p * TPC:(p + 1) * TPC],
                                in_=i_y2[kc * 128:(kc + 1) * 128, :]).then_inc(LD, 16)
                ld += 48
                ld_kc = ld

                @blk.scalar
                def _(s, kc=kc, p=p, ld_kc=ld_kc, ve_w=ve):
                    s.wait_ge(LD, ld_kc)
                    if kc >= 1:
                        s.wait_ge(VE, ve_w)
                    s.activation(g1s[:], y2s[:, p * TPC:(p + 1) * TPC], AF.Silu,
                                 bias=0.0, scale=1.0).then_inc(SE, 1)
                se += 1
                se_g = se

                v0 = ve

                @blk.vector
                def _(v, kc=kc, p=p, v0=v0, se_g=se_g):
                    v.wait_ge(SE, se_g)
                    v.tensor_tensor(of[:, kc * TPC:(kc + 1) * TPC],
                                    s1s[:, p * TPC:(p + 1) * TPC], g1s[:],
                                    op=ALU.mult).then_inc(VE, 1)
                    v.wait_ge(VE, v0 + 1)
                    v.tensor_tensor(of[:, kc * TPC:(kc + 1) * TPC],
                                    of[:, kc * TPC:(kc + 1) * TPC],
                                    x1s[:, p * TPC:(p + 1) * TPC],
                                    op=ALU.add).then_inc(VE, 1)
                ve += 2
                ve_kc[kc] = ve
            ve_of = ve

            so_t = {}
            for tt in range(8):
                mm0 = mm
                v0 = ve

                @blk.tensor
                def _(t, tt=tt, mm0=mm0, v0=v0):
                    t.wait_ge(VE, ve_of)
                    if tt > 0:
                        t.wait_ge(VE, v0)
                    for kc in range(16):
                        if kc > 0:
                            t.wait_ge(VE, v0 + kc)
                        t.transpose(psT[:], of[:, kc * TPC + tt * 128:kc * TPC + (tt + 1) * 128],
                                    ey[:]).then_inc(MM, 1)
                mm += 16

                @blk.vector
                def _(v, tt=tt, mm0=mm0):
                    if tt >= 2:
                        v.wait_ge(SO, so_t[tt - 2])
                    for kc in range(16):
                        v.wait_ge(MM, mm0 + kc + 1)
                        v.tensor_copy(ot[:, (tt % 2) * D + kc * 128:(tt % 2) * D + (kc + 1) * 128],
                                      psT[:]).then_inc(VE, 1)
                ve += 16

                @blk.sync
                def _(s, tt=tt, ve_w=ve):
                    s.wait_ge(VE, ve_w)
                    s.dma_start(out=o_out[tt * 128:(tt + 1) * 128, :],
                                in_=ot[:, (tt % 2) * D:(tt % 2 + 1) * D]).then_inc(SO, 16)
                so += 16
                so_t[tt] = so

            @blk.sync
            def _(s):
                s.wait_ge(SO, so)
    return nc



def host_D(inputs, c_outs):
    slope = np.asarray(inputs["L1_attn_slope_rate"], np.float32)
    dmask = np.full((128, 512), 1e30, np.float32)
    pp = np.arange(128)
    mm_ = np.arange(256)
    for half in range(2):
        kk = half * 128 + pp
        val = mm_[None, :] - kk[:, None]
        dmask[:, half * 256:(half + 1) * 256] = np.where(val >= 0, val, 1e30)
    armask = ((np.arange(TPC) % 256) + 1).astype(np.float32).reshape(1, TPC)
    kmask = np.zeros((128, 2), np.float32)
    for par in range(2):
        kmask[:, par] = 255 - (par * 128 + np.arange(128))
    wqkvT = f32(np.asarray(inputs["L1_attn_w_qkv"]).T)
    wgateT = f32(np.asarray(inputs["L1_attn_w_gate"]).T)
    wqkv2T = f32(np.asarray(inputs["L1_ffn_W_qkv"]).T)
    wg2T = f32(np.asarray(inputs["L1_ffn_W_gate"]).T)
    maps = []
    for c in range(NCORES):
        maps.append({
            "x1o": c_outs[c]["x1o"],
            "wqkvT": wqkvT, "wgateT": wgateT, "wqkv2T": wqkv2T, "wg2T": wg2T,
            "slope": f32(slope.reshape(16, 1)),
            "dmask": f32(dmask), "armask": f32(armask), "kmask": f32(kmask),
            "eye128b": bf16(np.eye(128)), "eye16": f32(np.eye(16)),
        })
    return maps


def host_E(inputs, d_outs):
    slope = np.asarray(inputs["L1_attn_slope_rate"], np.float32)
    kvstack = np.concatenate([f32(d_outs[c]["kvsego"]) for c in range(NCORES)], axis=0)
    nw1 = f32(np.asarray(inputs["L1_attn_norm_w"]).reshape(16, 128).T)
    woT = f32(np.asarray(inputs["L1_attn_w_out"]).T)
    maps = []
    for c in range(NCORES):
        delta = np.full((1, 8), 1e20, np.float32)
        for j in range(c):
            delta[0, j] = (c - 1 - j)
        maps.append({
            "attno": d_outs[c]["attno"], "qdo": d_outs[c]["qdo"],
            "kvstack": kvstack, "slope": f32(slope.reshape(16, 1)),
            "delta": f32(delta), "normw1": nw1,
            "gateo": d_outs[c]["gateo"], "wout1T": woT,
            "eye16": f32(np.eye(16)),
        })
    return maps


def host_F(d_outs):
    qfull = np.concatenate([np.asarray(d_outs[c]["q2o"]) for c in range(NCORES)],
                           axis=1)
    kshf = np.concatenate([np.asarray(d_outs[c]["ksho"]) for c in range(NCORES)],
                          axis=1)
    vshf = np.concatenate([np.asarray(d_outs[c]["vsho"]) for c in range(NCORES)],
                          axis=0)
    cmask = np.zeros((128, 4 * 512), np.float32)
    kk = np.arange(128)[:, None]
    mm_ = np.arange(512)[None, :]
    for r in range(4):
        cmask[:, r * 512:(r + 1) * 512] = (mm_ >= r * 128 + kk)
    maps = []
    for c in range(NCORES):
        maps.append({
            "q2h": bf16(np.ascontiguousarray(qfull[c * 256:(c + 1) * 256])),
            "kshf": bf16(kshf), "vshf": bf16(vshf), "cmask": bf16(cmask),
        })
    return maps


def host_G(c_outs, e_outs, f_res):
    y2full = np.concatenate([np.asarray(f_res[c]["y2o"]) for c in range(NCORES)],
                            axis=0)
    maps = []
    for c in range(NCORES):
        maps.append({
            "x1o": c_outs[c]["x1o"], "s1o": e_outs[c]["s1o"],
            "y2sl": bf16(np.ascontiguousarray(y2full[:, c * TPC:(c + 1) * TPC])),
            "eye128f": f32(np.eye(128)),
        })
    return maps



def kernel(**inputs):
    consts = mk_consts()
    consts2 = mk_consts_b2(inputs)
    u_fm_full = f32(np.asarray(inputs["L0_attn_u"])[0].T)

    cores = list(range(NCORES))
    a_res = run_bass_kernel_spmd(build_A(), host_A(inputs), core_ids=cores).results
    b1_res = run_bass_kernel_spmd(build_B1(), host_B1(a_res, consts),
                                  core_ids=cores).results
    b2_res = run_bass_kernel_spmd(build_B2(),
                                  host_B2(inputs, a_res, b1_res, consts, consts2),
                                  core_ids=cores).results
    c_res = run_bass_kernel_spmd(build_C(),
                                 host_C(inputs, a_res, b2_res, consts, u_fm_full),
                                 core_ids=cores).results
    d_res = run_bass_kernel_spmd(build_D(), host_D(inputs, c_res),
                                 core_ids=cores).results
    e_res = run_bass_kernel_spmd(build_E(), host_E(inputs, d_res),
                                 core_ids=cores).results
    f_res = run_bass_kernel_spmd(build_F(), host_F(d_res), core_ids=cores).results
    g_res = run_bass_kernel_spmd(build_G(), host_G(c_res, e_res, f_res),
                                 core_ids=cores).results
    out = np.concatenate([np.asarray(g_res[c]["outo"], np.float32)
                          for c in range(NCORES)], axis=0)
    return out.reshape(1, 8192, 2048)
